# version 21
# baseline (speedup 1.0000x reference)
import contextlib
import numpy as np

import concourse.bass as bass
import concourse.mybir as mybir
from concourse.bass_utils import run_bass_kernel_spmd

F32 = mybir.dt.float32
BF16 = mybir.dt.bfloat16
ALU = mybir.AluOpType
ACT = mybir.ActivationFunctionType
AX = mybir.AxisListType.X

NCORES = 8
D = 2048
T = 8192
TC = 1024
HALO = 4
TF = TC + HALO
KT = D // 128
CH = 128
NCH = TC // CH

L0_HV, L0_HK, DK, DV = 32, 16, 128, 128
KW = 4
L1_H = 16
DE = 8192
NSH0 = 2048 // NCORES
NSHE = DE // NCORES
NSHD = D // NCORES

DEBUG_TAPS = ()
SIM_DRAINS = True
VEC_DRAINS = True


def _cc(total, step=512, off0=0):
    out = []
    o = 0
    while o < total:
        n = min(step, total - o)
        out.append((off0 + o, n))
        o += n
    return out


class Sem:
    def __init__(self, h):
        self.h = h
        self.n = 0

    def bump(self, k=1):
        self.n += k
        return self.n


def build_program():
    nc = bass.Bass("TRN2", target_bir_lowering=False)

    def dram_in(name, shape, dtype=F32):
        return nc.dram_tensor(name, shape, dtype, kind="ExternalInput").ap()

    x_in = dram_in("x", [D, TF])
    w_qkv = dram_in("w_qkv", [D, 8192])
    w_z = dram_in("w_z", [D, 4096])
    w_ab = dram_in("w_ab", [D, 64])
    w_out0 = dram_in("w_out0", [4096, D])
    conv_w = dram_in("conv_w", [8192, KW])
    vec_col = dram_in("vec_col", [128, 8])
    g_ffn0 = dram_in("g_ffn0", [D, 1])
    wT_ffn0 = dram_in("wT_ffn0", [D, NSH0])
    w_q1 = dram_in("w_q1", [D, D])
    w_k1 = dram_in("w_k1", [D, D])
    w_v1 = dram_in("w_v1", [D, D])
    w_b1 = dram_in("w_b1", [D, L1_H])
    w_o1 = dram_in("w_o1", [D, D])
    wT_up = dram_in("wT_up", [D, NSHE])
    wT_gate = dram_in("wT_gate", [D, NSHE])
    wT_down = dram_in("wT_down", [DE, NSHD])
    pmask = dram_in("pmask", [128, NCORES])
    hsel0_in = dram_in("hsel0", [32, 32 * 128])

    y_out = nc.dram_tensor("y", [D, TC], F32, kind="ExternalOutput").ap()

    taps = {}

    tapd_any = [False]

    def tap(name, shape, dtype=F32):
        if name in DEBUG_TAPS:
            tapd_any[0] = True
            taps[name] = nc.dram_tensor("tap_" + name, shape, dtype,
                                        kind="ExternalOutput").ap()
            return taps[name]
        return None

    tern_in = nc.dram_tensor("tern_in", [1, 8], F32)
    tern_out = nc.dram_tensor("tern_out", [1, 8], F32)
    tq_in = {
        "ffn0": nc.dram_tensor("tq_ffn0_in", [D, NSH0], BF16),
        "up": nc.dram_tensor("tq_up_in", [D, NSHE], BF16),
        "gate": nc.dram_tensor("tq_gate_in", [D, NSHE], BF16),
        "down": nc.dram_tensor("tq_down_in", [DE, NSHD], BF16),
    }
    tq_out = {
        "ffn0": nc.dram_tensor("tq_ffn0_out", [NCORES, D, NSH0], BF16),
        "up": nc.dram_tensor("tq_up_out", [NCORES, D, NSHE], BF16),
        "gate": nc.dram_tensor("tq_gate_out", [NCORES, D, NSHE], BF16),
        "down": nc.dram_tensor("tq_down_out", [NCORES, DE, NSHD], BF16),
    }
    agg_in = {0: nc.dram_tensor("agg0_in", [128, L0_HV * 384], F32),
              1: nc.dram_tensor("agg1_in", [128, L1_H * 384], F32)}
    agg_out = {0: nc.dram_tensor("agg0_out", [NCORES, 128, L0_HV * 384], F32),
               1: nc.dram_tensor("agg1_out", [NCORES, 128, L1_H * 384], F32)}
    amax_in = nc.dram_tensor("amax_in", [1, 8], F32)
    amax_out = nc.dram_tensor("amax_out", [1, 8], F32)
    o0_dram = {0: nc.dram_tensor("o0_l0", [L0_HV, DV, TC], BF16),
               1: nc.dram_tensor("o0_l1", [L1_H, DV, TC], BF16)}
    cc_dram = {0: nc.dram_tensor("cc_l0", [L0_HV, NCH, DK, CH], BF16),
               1: nc.dram_tensor("cc_l1", [L1_H, NCH, DK, CH], BF16)}

    streams = {e: [] for e in ("sync", "scalar", "vector", "tensor", "gpsimd")}
    incable = {e: [] for e in ("sync", "scalar", "vector", "tensor", "gpsimd")}

    def rec(eng, fn, can_inc=True):
        streams[eng].append(fn)
        incable[eng].append(can_inc)

    stack = contextlib.ExitStack()
    E = stack.enter_context

    s_ld = Sem(E(nc.semaphore("s_ld")))
    s_gd = Sem(E(nc.semaphore("s_gd")))
    s_cc = Sem(E(nc.semaphore("s_cc")))
    s_mm = Sem(E(nc.semaphore("s_mm")))
    s_v = Sem(E(nc.semaphore("s_v")))
    s_s = Sem(E(nc.semaphore("s_s")))
    s_g = Sem(E(nc.semaphore("s_g")))
    s_st = Sem(E(nc.semaphore("s_st")))
    s_sy = Sem(E(nc.semaphore("s_sy")))

    ENG_SEM = {"tensor": s_mm, "vector": s_v, "scalar": s_s, "gpsimd": s_g,
               "sync": s_sy}

    def dma(q, out_ap, in_ap, sem=None):
        s = sem if sem is not None else (s_ld if q == "sync" else s_gd)
        v = s.bump(16)
        rec(q, lambda e, o=out_ap, i=in_ap, sh=s.h: e.dma_start(out=o, in_=i)
            .then_inc(sh, 16), can_inc=False)
        return v

    def qfence(q):
        s = s_ld if q == "sync" else s_gd
        wait(q, s, s.n)
        return s.n

    def wait(eng, s, v):
        rec(eng, lambda e, sh=s.h, vv=v: e.wait_ge(sh, vv), can_inc=False)

    def sig(eng):
        s = ENG_SEM[eng]
        v = s.bump(1)
        if streams[eng] and incable[eng][-1]:
            fn = streams[eng][-1]
            streams[eng][-1] = (lambda e, fn=fn, sh=s.h: fn(e).then_inc(sh, 1))
            incable[eng][-1] = False
        else:
            rec(eng, lambda e, sh=s.h: e.drain().then_inc(sh, 1), can_inc=False)
        return v

    def gfence():
        wait("gpsimd", s_gd, s_gd.n)
        return sig("gpsimd")

    COMPUTE = ("scalar", "vector", "tensor", "gpsimd")

    def barrier(engines=COMPUTE + ("sync",)):
        if "sync" in engines:
            wait("sync", s_ld, s_ld.n)
            wait("sync", s_st, s_st.n)
        if "gpsimd" in engines:
            wait("gpsimd", s_gd, s_gd.n)
        vals = {e: sig(e) for e in engines}
        for e in engines:
            for o in engines:
                if o != e:
                    wait(e, ENG_SEM[o], vals[o])
        return vals

    def sbh(st_, name, p, f, dt=F32):
        return st_.enter_context(nc.sbuf_tensor(name, [p, f], dt))

    def sb(name, p, f, dt=F32):
        return sbh(stack, name, p, f, dt)

    c_onec = sb("c_onec", 128, 1)
    c_oner = sb("c_oner", 1, 128)
    c_striln = sb("c_striln", 128, 128)
    c_striun = sb("c_striun", 128, 128)
    c_triu1 = sb("c_triu1", 128, 128)
    c_idn = sb("c_idn", 128, 128, BF16)
    c_idn32 = sb("c_idn32", 128, 128)
    c_pmask = sb("c_pmask", 128, NCORES)
    c_eps = sb("c_eps", 128, 8)
    vcol = sb("vcol", 128, 8)
    expA = sb("expA", 32, 1)
    gnw0 = sb("gnw0", 128, 1)
    gnw1 = sb("gnw1", 128, 1)
    zer128 = sb("zer128", 128, 128)
    c_onecb = sb("c_onecb", 128, 1, BF16)
    cw_sb = sb("cw_sb", 128, 4 * 64)

    x_sb = [sb(f"x{i}", 128, TF) for i in range(KT)]
    xln = [sb(f"xln{i}", 128, TF, BF16) for i in range(KT)]
    r_mu = sb("r_mu", 1, TF)
    r_iv = sb("r_iv", 1, TF)
    r_t0 = sb("r_t0", 1, TF)
    f_scr = sb("f_scr", 128, TF)
    wb_t = [sb(f"wbt{b}", 128, KT * 256, BF16) for b in range(2)]
    wbuf = [[wb_t[b][:, 256 * k:256 * (k + 1)] for k in range(KT)]
            for b in range(2)]

    pp = [E(nc.psum_tensor(f"pp{i}", [128, 512], F32)) for i in range(6)]
    ps_tb = E(nc.psum_tensor("ps_tb", [128, 1024], BF16))
    ps_m = E(nc.psum_tensor("ps_m", [128, 512], F32))

    g0 = lambda f: rec("gpsimd", f)
    g0(lambda e: e.memset(c_onec[:], 1.0))
    g0(lambda e: e.memset(c_onecb[:], 1.0))
    g0(lambda e: e.memset(c_oner[:], 1.0))
    g0(lambda e: e.memset(c_striln[:], -1.0))
    g0(lambda e: e.memset(c_striun[:], -1.0))
    g0(lambda e: e.memset(c_triu1[:], 1.0))
    g0(lambda e: e.memset(c_idn[:], 1.0))
    g0(lambda e: e.memset(c_idn32[:], 1.0))
    g0(lambda e: e.memset(zer128[:], 0.0))
    g0(lambda e: e.memset(c_eps[:, 0:1], 1e-6))
    g0(lambda e: e.memset(c_eps[:, 1:2], 1e-5))
    g0(lambda e: e.memset(c_eps[:, 2:3], 1.0))
    g0(lambda e: e.memset(c_eps[:, 3:4], 1e-12))
    g0(lambda e: e.memset(c_eps[:, 4:5], 0.5))
    g0(lambda e: e.drain()) if SIM_DRAINS else None
    g0(lambda e: e.affine_select(c_striln[:], c_striln[:], [[-1, 128]], ALU.is_gt,
                                 0.0, base=0, channel_multiplier=1))
    g0(lambda e: e.affine_select(c_striun[:], c_striun[:], [[1, 128]], ALU.is_gt,
                                 0.0, base=0, channel_multiplier=-1))
    g0(lambda e: e.affine_select(c_triu1[:], c_triu1[:], [[1, 128]], ALU.is_ge,
                                 0.0, base=0, channel_multiplier=-1))
    g0(lambda e: e.affine_select(c_idn[:], c_idn[:], [[-1, 128]], ALU.is_equal,
                                 0.0, base=0, channel_multiplier=1))
    g0(lambda e: e.affine_select(c_idn32[:], c_idn32[:], [[-1, 128]], ALU.is_equal,
                                 0.0, base=0, channel_multiplier=1))
    sig("gpsimd")

    for i in range(KT):
        dma("sync", x_sb[i][:], x_in[i * 128:(i + 1) * 128, :])
    dma("sync", c_pmask[:], pmask)
    dma("sync", vcol[:], vec_col)
    for t in range(64):
        dma("sync", cw_sb[:, 4 * t:4 * t + 4], conv_w[t * 128:(t + 1) * 128, :])
    barrier()
    rec("scalar", lambda e: e.activation(expA[:, 0:1], vcol[0:32, 0:1], ACT.Exp))
    rec("scalar", lambda e: e.copy(gnw0[:], vcol[:, 2:3]))
    rec("scalar", lambda e: e.copy(gnw1[:], vcol[:, 3:4]))
    sig("scalar")
    barrier()

    def emit_ln(col0, ncols):
        chunks = _cc(ncols, 512, col0)
        for (off, n) in chunks:
            for i in range(KT):
                rec("tensor", lambda e, i=i, off=off, n=n:
                    e.matmul(ps_m[0:1, 0:n], c_onec[:], x_sb[i][:, off:off + n],
                             start=(i == 0), stop=(i == KT - 1)))
            vm = sig("tensor")
            wait("vector", s_mm, vm)
            rec("vector", lambda e, off=off, n=n:
                e.tensor_copy(r_mu[:, off:off + n], ps_m[0:1, 0:n]))
            vv = sig("vector")
            wait("tensor", s_v, vv)
        for i in range(KT):
            rec("scalar", lambda e, i=i: e.activation(f_scr[:, col0:col0 + ncols],
                                                      x_sb[i][:, col0:col0 + ncols],
                                                      ACT.Square))
            vsq = sig("scalar")
            wait("tensor", s_s, vsq)
            for ci, (off, n) in enumerate(chunks):
                rec("tensor", lambda e, ci=ci, off=off, n=n, i=i:
                    e.matmul(pp[3 + ci][0:1, 0:n], c_onec[:], f_scr[:, off:off + n],
                             start=(i == 0), stop=(i == KT - 1)))
            vm = sig("tensor")
            wait("scalar", s_mm, vm)
        wait("vector", s_mm, s_mm.n)
        for ci, (off, n) in enumerate(chunks):
            rec("vector", lambda e, ci=ci, off=off, n=n:
                e.tensor_copy(r_t0[:, off:off + n], pp[3 + ci][0:1, 0:n]))
        rec("vector", lambda e: e.drain()) if VEC_DRAINS else None
        rec("vector", lambda e: e.tensor_scalar(r_mu[:, col0:col0 + ncols],
                                                r_mu[:, col0:col0 + ncols],
                                                1.0 / D, None, ALU.mult))
        rec("vector", lambda e: e.drain()) if VEC_DRAINS else None
        rec("vector", lambda e: e.tensor_tensor(r_iv[:, col0:col0 + ncols],
                                                r_mu[:, col0:col0 + ncols],
                                                r_mu[:, col0:col0 + ncols], ALU.mult))
        rec("vector", lambda e: e.drain()) if VEC_DRAINS else None
        rec("vector", lambda e: e.scalar_tensor_tensor(r_t0[:, col0:col0 + ncols],
                                                       r_t0[:, col0:col0 + ncols],
                                                       1.0 / D,
                                                       r_iv[:, col0:col0 + ncols],
                                                       ALU.mult, ALU.subtract))
        vv = sig("vector")
        wait("scalar", s_v, vv)
        rec("scalar", lambda e: e.activation(r_iv[:, col0:col0 + ncols],
                                             r_t0[:, col0:col0 + ncols],
                                             ACT.Sqrt, bias=c_eps[0:1, 1:2]))
        vs = sig("scalar")
        wait("vector", s_s, vs)
        rec("vector", lambda e: e.reciprocal(r_iv[:, col0:col0 + ncols],
                                             r_iv[:, col0:col0 + ncols]))
        vv = sig("vector")
        wait("tensor", s_v, vv)
        sig("vector")

    def emit_xln(col0, ncols):
        for (off, n) in _cc(ncols, 512, col0):
            rec("tensor", lambda e, off=off, n=n:
                e.matmul(pp[4][:, 0:n], c_oner[:], r_mu[0:1, off:off + n],
                         start=True, stop=True))
            rec("tensor", lambda e, off=off, n=n:
                e.matmul(pp[5][:, 0:n], c_oner[:], r_iv[0:1, off:off + n],
                         start=True, stop=True))
            vm = sig("tensor")
            wait("vector", s_mm, vm)
            for i in range(KT):
                rec("vector", lambda e, i=i, off=off, n=n: e.tensor_tensor(
                    f_scr[:, off:off + n], x_sb[i][:, off:off + n],
                    pp[4][:, 0:n], ALU.subtract))
                rec("vector", lambda e: e.drain()) if VEC_DRAINS else None
                rec("vector", lambda e, i=i, off=off, n=n: e.tensor_tensor(
                    xln[i][:, off:off + n], f_scr[:, off:off + n],
                    pp[5][:, 0:n], ALU.mult))
                rec("vector", lambda e: e.drain()) if VEC_DRAINS else None
            vv = sig("vector")
            wait("tensor", s_v, vv)
        return sig("vector")

    def load_wslab(w_dram, c0, ncol, buf, kt=KT, q="gpsimd", krange=None,
                   flat=None):
        if flat is None:
            flat = wb_t[0] if buf is wbuf[0] else (
                wb_t[1] if buf is wbuf[1] else None)
        if flat is not None and krange is None:
            dst = flat[:].rearrange("p (k c) -> p k c", c=256)[:, 0:kt, 0:ncol]
            srcr = w_dram[0:kt * 128, :].rearrange("(k p) c -> p k c", p=128)
            dma(q, dst, srcr[:, :, c0:c0 + ncol])
        else:
            ks = krange or range(kt)
            for k in ks:
                dma(q, buf[k][:, 0:ncol],
                    w_dram[k * 128:(k + 1) * 128, c0:c0 + ncol])
        return qfence(q)

    def proj_cols(w_dram, c0, ncols, tok_chunks, evac, kt=KT, src=None,
                  slab_of=2, evac_eng="vector", src_off=0):
        src = src or xln
        slab_of = min(slab_of, 2)
        n_of = ncols // 128
        slabs = [(s, min(slab_of, n_of - s)) for s in range(0, n_of, slab_of)]
        grp_sigs = []
        slab_mm_end = {}
        vlds = {0: load_wslab(w_dram, c0, 128 * slabs[0][1], wbuf[0], kt=kt)}
        for si, (s0, sn) in enumerate(slabs):
            b = si % 2
            if si + 1 < len(slabs):
                if si >= 1:
                    wait("gpsimd", s_mm, slab_mm_end[si - 1])
                ns0, nsn = slabs[si + 1]
                vlds[si + 1] = load_wslab(w_dram, c0 + 128 * ns0, 128 * nsn,
                                          wbuf[(si + 1) % 2], kt=kt)
            first_grp = True
            for j in range(sn):
                for (toff, tn) in tok_chunks:
                    gi = len(grp_sigs)
                    bank = pp[gi % 4]
                    if gi >= 4:
                        wait("tensor", s_v, grp_sigs[gi - 4][0])
                        wait("tensor", s_s, grp_sigs[gi - 4][1])
                    if first_grp:
                        wait("tensor", s_gd, vlds[si])
                        first_grp = False
                    for k in range(kt):
                        rec("tensor", lambda e, k=k, j=j, b=b, toff=toff + src_off,
                            tn=tn, bank=bank: e.matmul(
                                bank[:, 0:tn],
                                wbuf[b][k][:, 128 * j:128 * (j + 1)],
                                src[k][:, toff:toff + tn],
                                start=(k == 0), stop=(k == kt - 1)))
                    vm = sig("tensor")
                    wait("vector", s_mm, vm)
                    wait("scalar", s_mm, vm)
                    evac(s0 + j, bank, toff, tn)
                    grp_sigs.append((sig("vector"), sig("scalar")))
            slab_mm_end[si] = s_mm.n
        barrier(("tensor", "vector", "scalar", "gpsimd"))

    TOKS_OWN = [(HALO, 512), (HALO + 512, 512)]
    TOKS_ALL = [(0, 512), (512, 512), (1024, 4)]

    emit_ln(0, TF)
    emit_xln(0, TF)
    barrier()

    t_xln = tap("xln", [D, TF])
    if t_xln is not None:
        for i in range(KT):
            rec("scalar", lambda e, i=i: e.copy(f_scr[:], xln[i][:]))
            v = sig("scalar")
            wait("sync", s_s, v)
            dma("sync", t_xln[i * 128:(i + 1) * 128, :], f_scr[:], sem=s_st)
            wait("scalar", s_st, s_st.n)
        barrier()

    msel = contextlib.ExitStack()
    hsel0 = sbh(msel, "sb_hsel0", 32, 32 * 128)
    dma("sync", hsel0[:], hsel0_in)
    l0sc = contextlib.ExitStack()
    ab_fm = sbh(l0sc, "ab_fm", 64, TC)
    g_fm = sbh(l0sc, "g_fm", 32, TC)
    beta0 = sbh(l0sc, "beta0", 32, TC)
    bcum = sbh(l0sc, "bcum", 32, TC)
    lam = sbh(l0sc, "lam", 32, TC)
    trow = sbh(l0sc, "trow", 1, 32)
    bT = [sbh(l0sc, f"bT{i}", 128, 32) for i in range(NCH)]
    betT = [sbh(l0sc, f"betT{i}", 128, 32) for i in range(NCH)]
    gamT = [sbh(l0sc, f"gamT{i}", 128, 32) for i in range(NCH)]
    lamCr = [sbh(l0sc, f"lamC{i}", 128, 32) for i in range(NCH)]

    for k in range(KT):
        dma("gpsimd", wbuf[0][k][:, 0:64], w_ab[k * 128:(k + 1) * 128, :])
    vld = qfence("gpsimd")
    for (toff, tn) in TOKS_OWN:
        wait("tensor", s_gd, vld)
        for k in range(KT):
            rec("tensor", lambda e, k=k, toff=toff, tn=tn:
                e.matmul(pp[0][0:64, 0:tn], wbuf[0][k][:, 0:64],
                         xln[k][:, toff:toff + tn],
                         start=(k == 0), stop=(k == KT - 1)))
        vm = sig("tensor")
        wait("vector", s_mm, vm)
        rec("vector", lambda e, toff=toff, tn=tn:
            e.tensor_copy(ab_fm[:, toff - HALO:toff - HALO + tn], pp[0][0:64, 0:tn]))
        vv = sig("vector")
        wait("tensor", s_v, vv)
    barrier()

    rec("scalar", lambda e: e.activation(g_fm[:], ab_fm[0:32, :], ACT.Abs,
                                         bias=vcol[0:32, 1:2]))
    rec("scalar", lambda e: e.drain()) if SIM_DRAINS else None
    rec("scalar", lambda e: e.activation(g_fm[:], g_fm[:], ACT.Exp, scale=-1.0))
    rec("scalar", lambda e: e.drain()) if SIM_DRAINS else None
    rec("scalar", lambda e: e.activation(g_fm[:], g_fm[:], ACT.Ln,
                                         bias=c_eps[0:32, 2:3]))
    rec("scalar", lambda e: e.drain()) if SIM_DRAINS else None
    rec("scalar", lambda e: e.activation(f_scr[0:32, 0:TC], ab_fm[0:32, :],
                                         ACT.Relu, bias=vcol[0:32, 1:2]))
    rec("scalar", lambda e: e.drain()) if SIM_DRAINS else None
    rec("scalar", lambda e: e.activation(beta0[:], ab_fm[32:64, :], ACT.Sigmoid))
    vs = sig("scalar")
    wait("vector", s_s, vs)
    rec("vector", lambda e: e.tensor_add(g_fm[:], g_fm[:], f_scr[0:32, 0:TC]))
    rec("vector", lambda e: e.drain()) if VEC_DRAINS else None
    rec("vector", lambda e: e.tensor_scalar(g_fm[:], g_fm[:], expA[0:32, 0:1],
                                            -1.0, ALU.mult, ALU.mult))
    rec("vector", lambda e: e.drain()) if VEC_DRAINS else None
    for i in range(NCH):
        rec("vector", lambda e, i=i: e.tensor_tensor_scan(
            bcum[:, i * CH:(i + 1) * CH], g_fm[:, i * CH:(i + 1) * CH],
            zer128[0:32, :], 0.0, ALU.add, ALU.add))
    rec("vector", lambda e: e.drain()) if VEC_DRAINS else None
    vv = sig("vector")
    wait("scalar", s_v, vv)
    rec("scalar", lambda e: e.activation(lam[:], bcum[:], ACT.Exp))
    sig("scalar")

    for nm, t in (("g_fm", g_fm), ("beta0", beta0), ("bcum", bcum)):
        tp = tap(nm, [32, TC])
        if tp is not None:
            wait("sync", s_s, s_s.n)
            wait("sync", s_v, s_v.n)
            dma("sync", tp, t[:], sem=s_st)

    barrier()

    for i in range(NCH):
        rec("tensor", lambda e, i=i: e.transpose(ps_m[:, 0:32],
                                                 bcum[:, i * CH:(i + 1) * CH],
                                                 c_idn32[0:32, 0:32]))
        vm = sig("tensor")
        wait("vector", s_mm, vm)
        rec("vector", lambda e, i=i: e.tensor_copy(bT[i][:], ps_m[:, 0:32]))
        vv = sig("vector")
        wait("tensor", s_v, vv)
        rec("tensor", lambda e, i=i: e.transpose(ps_m[:, 0:32],
                                                 beta0[:, i * CH:(i + 1) * CH],
                                                 c_idn32[0:32, 0:32]))
        vm = sig("tensor")
        wait("vector", s_mm, vm)
        rec("vector", lambda e, i=i: e.tensor_copy(betT[i][:], ps_m[:, 0:32]))
        vv = sig("vector")
        wait("tensor", s_v, vv)
        rec("tensor", lambda e, i=i: e.matmul(ps_m[0:1, 64:96],
                                              bcum[:, (i + 1) * CH - 1:(i + 1) * CH],
                                              c_idn32[0:32, 0:32],
                                              start=True, stop=True))
        vm = sig("tensor")
        wait("vector", s_mm, vm)
        rec("vector", lambda e: e.tensor_copy(trow[:], ps_m[0:1, 64:96]))
        vv = sig("vector")
        wait("tensor", s_v, vv)
        rec("tensor", lambda e, i=i: e.matmul(ps_m[:, 32:64], c_oner[:],
                                              trow[:], start=True, stop=True))
        vm = sig("tensor")
        wait("vector", s_mm, vm)
        rec("vector", lambda e, i=i: e.tensor_tensor(gamT[i][:], ps_m[:, 32:64],
                                                     bT[i][:], ALU.subtract))
        vv = sig("vector")
        wait("scalar", s_v, vv)
        rec("scalar", lambda e, i=i: e.activation(gamT[i][:], gamT[i][:], ACT.Exp))
        rec("scalar", lambda e, i=i: e.activation(lamCr[i][:], ps_m[:, 32:64],
                                                  ACT.Exp))
        vs = sig("scalar")
        wait("tensor", s_s, vs)
        wait("vector", s_s, vs)
    barrier()

    for nm in ("bT3", "gamT3", "lamC3", "betT3"):
        tp = tap(nm, [128, 32])
        if tp is not None:
            src = {"bT3": bT, "gamT3": gamT, "lamC3": lamCr, "betT3": betT}[nm][3]
            dma("sync", tp, src[:], sem=s_st)
    barrier()

    lb0 = g_fm
    rec("vector", lambda e: e.tensor_tensor(lb0[:], lam[:], beta0[:], ALU.mult))
    sig("vector")
    barrier(("vector", "tensor", "scalar"))

    def emit_mixer(layer, gates):
        gated = (layer == 0)
        HV = L0_HV if gated else L1_H
        NG = HV // 2
        msc = contextlib.ExitStack()
        hsel = gates["hsel"]
        HSP = 32 if gated else 16
        L = f"m{layer}_"
        xsq_b = sbh(msc, L + "xsq_b", 128, TC, BF16)
        nheads_k = 1 if gated else 2
        cvi = [sbh(msc, f"{L}cvi{t}", 128, TF, BF16) for t in range(4)]
        qhat = [sbh(msc, f"{L}qhat{j}", 128, TC, BF16) for j in range(nheads_k)]
        khat = [sbh(msc, f"{L}khat{j}", 128, TC, BF16) for j in range(nheads_k)]
        vtm = ([] if gated else
               [sbh(msc, f"{L}vtm{j}", 128, TC, BF16) for j in range(2)])
        st_v = sbh(msc, L + "st_v", 128, 256, BF16)
        scPU = [sbh(msc, f"{L}scPU_{k}", 128, 512, BF16) for k in range(6)]
        scPU.append(sbh(msc, f"{L}scPU_6", 128, 256, BF16))
        scP2 = [scPU[k][:, 0:256] for k in range(7)]
        scU_of = [scPU[k][:, 256:512] for k in range(6)]
        scY2 = sbh(msc, L + "scY2", 128, 512, BF16)
        scN = [scU_of[0][:, 0:128], scU_of[0][:, 128:256]]
        scNT = [scP2[0][:, 0:128], scP2[0][:, 128:256]]
        scY = [scY2[:, 0:256], scY2[:, 256:512]]
        scPT = [sbh(msc, f"{L}scPT{h}", 128, 128, BF16) for h in range(2)]
        scKb = [sbh(msc, f"{L}scKb{h}", 128, 128, BF16) for h in range(2)]
        scQb = [sbh(msc, f"{L}scQb{h}", 128, 128, BF16) for h in range(2)]
        scKt = [sbh(msc, f"{L}scKt{h}", 128, 128, BF16) for h in range(2)]
        Sa2 = sbh(msc, L + "Sa2", 128, 512)
        Sab2 = sbh(msc, L + "Sab2", 128, 512, BF16)
        Sa = [Sa2[:, 0:256], Sa2[:, 256:512]]
        Sab = [Sab2[:, 0:256], Sab2[:, 256:512]]
        sc_m2 = sbh(msc, L + "scm2", 128, 256)
        sc_t2 = sbh(msc, L + "sct2", 128, 256)
        sc_n2 = sbh(msc, L + "scn2", 128, 256)
        sc_e1 = sbh(msc, L + "sce1", 128, 256)
        sc_e2 = sbh(msc, L + "sce2", 128, 256)
        st_o = sbh(msc, L + "st_o", 128, 256, BF16)
        st_c = sbh(msc, L + "st_c", 128, 256, BF16)
        st_p = sbh(msc, L + "st_p", 128, 128)

        w_qkv_l = w_qkv if gated else None

        def tok_all():
            return TOKS_ALL if gated else TOKS_OWN

        for g in range(NG):
            def mk_evac(tile_idx, silu=False, colbase=0):
                def evac(jg, bank, toff, tn):
                    dst = cvi[tile_idx + jg - colbase]
                    if silu:
                        rec("scalar", lambda e, dst=dst, toff=toff, tn=tn, bank=bank:
                            e.activation(dst[:, toff:toff + tn], bank[:, 0:tn],
                                         ACT.Silu))
                    else:
                        rec("vector", lambda e, dst=dst, toff=toff, tn=tn, bank=bank:
                            e.tensor_copy(dst[:, toff:toff + tn], bank[:, 0:tn]))
                return evac

            if gated:
                proj_cols(w_qkv, 128 * g, 128, tok_all(), mk_evac(0), slab_of=1)
                proj_cols(w_qkv, 2048 + 128 * g, 128, tok_all(), mk_evac(1, colbase=0),
                          slab_of=1)
                proj_cols(w_qkv, 4096 + 256 * g, 256, tok_all(), mk_evac(2, colbase=0),
                          slab_of=2)
            else:
                proj_cols(w_q1, 256 * g, 256, tok_all(), mk_evac(0, colbase=0),
                          slab_of=2)
                proj_cols(w_k1, 256 * g, 256, tok_all(), mk_evac(2, colbase=0),
                          slab_of=2)

            barrier(("tensor", "vector", "scalar", "gpsimd"))

            if gated:
                gts = [g, 16 + g, 32 + 2 * g, 33 + 2 * g]
                for t in range(4):
                    ct = gts[t]
                    rec("vector", lambda e, t=t, ct=ct: e.tensor_scalar(
                        f_scr[:, 0:TC], cvi[t][:, 1:1 + TC],
                        cw_sb[:, 4 * ct:4 * ct + 1], None, ALU.mult))
                    rec("vector", lambda e: e.drain()) if VEC_DRAINS else None
                    for i_ in range(1, KW):
                        rec("vector", lambda e, t=t, ct=ct, i_=i_:
                            e.scalar_tensor_tensor(f_scr[:, 0:TC],
                                                   cvi[t][:, 1 + i_:1 + i_ + TC],
                                                   cw_sb[:, 4 * ct + i_:4 * ct + i_ + 1],
                                                   f_scr[:, 0:TC], ALU.mult, ALU.add))
                        rec("vector", lambda e: e.drain()) if VEC_DRAINS else None
                    vv = sig("vector")
                    wait("scalar", s_v, vv)
                    rec("scalar", lambda e, t=t: e.activation(cvi[t][:, HALO:TF],
                                                              f_scr[:, 0:TC],
                                                              ACT.Silu))
                    vs = sig("scalar")
                    wait("vector", s_s, vs)
                barrier(("vector", "scalar", "tensor"))

            for j in range(nheads_k):
                qt = cvi[0 if gated else j]
                kt_ = cvi[1 if gated else 2 + j]
                for (which, tile, qscale) in (("q", qt, True), ("k", kt_, False)):
                    rec("scalar", lambda e, tile=tile: e.activation(
                        xsq_b[:, 0:TC], tile[:, HALO:TF], ACT.Square))
                    vs = sig("scalar")
                    wait("tensor", s_s, vs)
                    for (o_, n_) in _cc(TC, 512):
                        rec("tensor", lambda e, o_=o_, n_=n_: e.matmul(
                            ps_m[0:1, 0:n_], c_onecb[:], xsq_b[:, o_:o_ + n_],
                            start=True, stop=True))
                        vm = sig("tensor")
                        wait("vector", s_mm, vm)
                        rec("vector", lambda e, o_=o_, n_=n_: e.tensor_copy(
                            r_mu[0:1, o_:o_ + n_], ps_m[0:1, 0:n_]))
                        vv = sig("vector")
                        wait("tensor", s_v, vv)
                    wait("scalar", s_v, s_v.n)
                    if gated:
                        rec("scalar", lambda e: e.activation(
                            r_t0[0:1, 0:TC], r_mu[0:1, 0:TC], ACT.Sqrt,
                            bias=c_eps[0:1, 0:1]))
                        vs = sig("scalar")
                        wait("vector", s_s, vs)
                        rec("vector", lambda e: e.reciprocal(
                            r_t0[0:1, 0:TC], r_t0[0:1, 0:TC]))
                    else:
                        rec("scalar", lambda e: e.activation(
                            r_t0[0:1, 0:TC], r_mu[0:1, 0:TC], ACT.Sqrt))
                        vs = sig("scalar")
                        wait("vector", s_s, vs)
                        rec("vector", lambda e: e.tensor_scalar(
                            r_t0[0:1, 0:TC], r_t0[0:1, 0:TC], 1e-12, None,
                            ALU.max))
                        rec("vector", lambda e: e.drain()) if VEC_DRAINS else None
                        rec("vector", lambda e: e.reciprocal(
                            r_t0[0:1, 0:TC], r_t0[0:1, 0:TC]))
                    rec("vector", lambda e: e.drain()) if VEC_DRAINS else None
                    if qscale:
                        rec("vector", lambda e: e.tensor_scalar(
                            r_t0[0:1, 0:TC], r_t0[0:1, 0:TC],
                            float(DK) ** -0.5, None, ALU.mult))
                        rec("vector", lambda e: e.drain()) if VEC_DRAINS else None
                    vv = sig("vector")
                    wait("tensor", s_v, vv)
                    dst = (qhat if which == "q" else khat)[j]
                    for (o_, n_) in _cc(TC, 512):
                        rec("tensor", lambda e, o_=o_, n_=n_: e.matmul(
                            pp[0][:, 0:n_], c_oner[:], r_t0[0:1, o_:o_ + n_],
                            start=True, stop=True))
                        vm = sig("tensor")
                        wait("vector", s_mm, vm)
                        rec("vector", lambda e, o_=o_, n_=n_, tile=tile, dst=dst:
                            e.tensor_tensor(dst[:, o_:o_ + n_],
                                            tile[:, HALO + o_:HALO + o_ + n_],
                                            pp[0][:, 0:n_], ALU.mult))
                        vv = sig("vector")
                        wait("tensor", s_v, vv)
            barrier(("tensor", "vector", "scalar"))

            if not gated:
                for h in range(2):
                    vc0 = 128 * (2 * g + h)
                    if h > 0:
                        wait("gpsimd", s_mm, s_mm.n)
                    vldv = load_wslab(w_v1, vc0, 128, wbuf[0], kt=KT)
                    for tt in range(NCH):
                        wait("tensor", s_gd, vldv)
                        for k in range(KT):
                            rec("tensor", lambda e, k=k, tt=tt: e.matmul(
                                pp[1][:, 0:128], xln[k][:, HALO + tt * CH:HALO + (tt + 1) * CH],
                                wbuf[0][k][:, 0:128],
                                start=(k == 0), stop=(k == KT - 1)))
                        vm = sig("tensor")
                        wait("scalar", s_mm, vm)
                        rec("scalar", lambda e, h=h, tt=tt: e.activation(
                            vtm[h][:, tt * CH:(tt + 1) * CH], pp[1][:, 0:128],
                            ACT.Silu))
                        vs = sig("scalar")
                        wait("tensor", s_s, vs)
            barrier(("tensor", "vector", "scalar"))

            if g == 0:
                for nm_, src_ in ((f"qhat_l{layer}", qhat[0]),
                                  (f"khat_l{layer}", khat[0]),
                                  (f"v0_l{layer}", cvi[2] if gated else vtm[0]),
                                  (f"v1_l{layer}", cvi[3] if gated else vtm[1])):
                    tp_ = tap(nm_, [128, src_.shape[1] if hasattr(src_, "shape")
                                    else TF], BF16)
                    if tp_ is not None:
                        wait("sync", s_v, s_v.n)
                        wait("sync", s_s, s_s.n)
                        dma("sync", tp_, src_[:], sem=s_st)
                barrier()

            for h in range(2):
                rec("vector", lambda e, h=h: e.memset(Sa2[:, 256 * h:256 * h + 128], 0.0))
                rec("vector", lambda e, h=h: e.memset(Sab2[:, 256 * h:256 * h + 128], 0.0))
                rec("vector", lambda e, h=h: e.tensor_copy(Sa2[:, 256 * h + 128:256 * (h + 1)],
                                                           c_idn32[:]))
                rec("vector", lambda e, h=h: e.tensor_copy(Sab2[:, 256 * h + 128:256 * (h + 1)],
                                                           c_idn[:]))
            rec("vector", lambda e: e.drain()) if VEC_DRAINS else None
            sig("vector")
            barrier(("tensor", "vector", "scalar"))

            for i in range(NCH):
                hv0 = 2 * g
                tc0 = i * CH
                for j in range(nheads_k):
                    rec("tensor", lambda e, j=j, tc0=tc0: e.matmul(
                        pp[0][:, 256 * j:256 * j + 128], khat[j][:, tc0:tc0 + CH],
                        khat[j][:, tc0:tc0 + CH], start=True, stop=True))
                    rec("tensor", lambda e, j=j, tc0=tc0: e.matmul(
                        pp[0][:, 256 * j + 128:256 * j + 256], khat[j][:, tc0:tc0 + CH],
                        qhat[j][:, tc0:tc0 + CH], start=True, stop=True))
                    rec("tensor", lambda e, j=j, tc0=tc0: e.transpose(
                        ps_tb[:, 128 * j:128 * (j + 1)], khat[j][:, tc0:tc0 + CH],
                        c_idn[:]))
                if gated:
                    for h in range(2):
                        rec("tensor", lambda e, h=h, tc0=tc0: e.transpose(
                            ps_tb[:, 256 + 128 * h:256 + 128 * (h + 1)],
                            cvi[2 + h][:, HALO + tc0:HALO + tc0 + CH], c_idn[:]))
                vmA = sig("tensor")
                wait("vector", s_mm, vmA)
                if gated:
                    rec("vector", lambda e: e.tensor_copy(st_v[:], ps_tb[:, 256:512]))
                    rec("vector", lambda e: e.drain()) if VEC_DRAINS else None
                vvV = sig("vector")
                wait("tensor", s_v, vvV)

                for h in range(2):
                    hh = hv0 + h
                    j = 0 if gated else h
                    psA = pp[0][:, 256 * j:256 * j + 128]
                    psKQ = pp[0][:, 256 * j + 128:256 * j + 256]
                    if gated:
                        rec("tensor", lambda e, hh=hh, tc0=tc0, h=h: e.matmul(
                            pp[1][:, 128 * h:128 * h + 128],
                            hsel[0:HSP, 128 * hh:128 * (hh + 1)],
                            gates["bcum"][:, tc0:tc0 + CH], start=True, stop=True))
                        rec("tensor", lambda e, hh=hh, tc0=tc0, h=h: e.matmul(
                            pp[1][:, 256 + 128 * h:256 + 128 * h + 128],
                            hsel[0:HSP, 128 * hh:128 * (hh + 1)],
                            gates["beta"][:, tc0:tc0 + CH], start=True, stop=True))
                        rec("tensor", lambda e, hh=hh, tc0=tc0, h=h: e.matmul(
                            pp[2][:, 128 * h:128 * h + 128],
                            hsel[0:HSP, 128 * hh:128 * (hh + 1)],
                            gates["lb"][:, tc0:tc0 + CH], start=True, stop=True))
                        rec("tensor", lambda e, hh=hh, tc0=tc0, h=h: e.matmul(
                            pp[2][:, 256 + 128 * h:256 + 128 * h + 128],
                            hsel[0:HSP, 128 * hh:128 * (hh + 1)],
                            gates["lam"][:, tc0:tc0 + CH], start=True, stop=True))
                    else:
                        rec("tensor", lambda e, hh=hh, tc0=tc0, h=h: e.matmul(
                            pp[1][:, 256 + 128 * h:256 + 128 * h + 128],
                            hsel[0:HSP, 128 * hh:128 * (hh + 1)],
                            gates["beta"][:, tc0:tc0 + CH], start=True, stop=True))
                vmR = sig("tensor")
                wait("vector", s_mm, vmR)
                wait("scalar", s_mm, vmR)

                D2 = lambda: (rec("vector", lambda e: e.drain())
                              if VEC_DRAINS else None)
                psA_ = [pp[0][:, 256 * (0 if gated else h_):
                              256 * (0 if gated else h_) + 128] for h_ in range(2)]
                psKQ_ = [pp[0][:, 256 * (0 if gated else h_) + 128:
                               256 * (0 if gated else h_) + 256] for h_ in range(2)]
                if gated:
                    for h in range(2):
                        rec("vector", lambda e, h=h,
                            bTc=gates["bT"][i][:, (hv0 + h):(hv0 + h) + 1]:
                            e.scalar_tensor_tensor(
                                sc_m2[:, 128 * h:128 * (h + 1)],
                                pp[1][:, 128 * h:128 * h + 128], bTc,
                                zer128[:], ALU.subtract, ALU.subtract))
                    D2()
                    rec("vector", lambda e: e.tensor_scalar(
                        sc_t2[:], sc_m2[:], 0.0, None, ALU.max))
                    rec("vector", lambda e: e.tensor_scalar(
                        sc_n2[:], sc_m2[:], 0.0, None, ALU.min))
                    D2()
                    vv = sig("vector")
                    wait("scalar", s_v, vv)
                    rec("scalar", lambda e: e.activation(sc_e1[:], sc_t2[:],
                                                         ACT.Exp, scale=-1.0))
                    rec("scalar", lambda e: e.drain()) if SIM_DRAINS else None
                    rec("scalar", lambda e: e.activation(sc_e2[:], sc_n2[:],
                                                         ACT.Exp))
                    vs = sig("scalar")
                    wait("vector", s_s, vs)
                    for h in range(2):
                        rec("vector", lambda e, h=h: e.tensor_tensor(
                            sc_t2[:, 128 * h:128 * (h + 1)],
                            sc_e1[:, 128 * h:128 * (h + 1)], psA_[h], ALU.mult))
                    D2()
                    for h in range(2):
                        rec("vector", lambda e, h=h,
                            beTc=gates["betT"][i][:, (hv0 + h):(hv0 + h) + 1]:
                            e.scalar_tensor_tensor(
                                scN[h], sc_t2[:, 128 * h:128 * (h + 1)], beTc,
                                c_striln[:], ALU.mult, ALU.mult))
                    D2()
                    for h in range(2):
                        rec("vector", lambda e, h=h: e.tensor_tensor(
                            sc_n2[:, 128 * h:128 * (h + 1)],
                            sc_e2[:, 128 * h:128 * (h + 1)], psA_[h], ALU.mult))
                    D2()
                    for h in range(2):
                        rec("vector", lambda e, h=h: e.tensor_tensor(
                            sc_n2[:, 128 * h:128 * (h + 1)],
                            sc_n2[:, 128 * h:128 * (h + 1)],
                            pp[1][:, 256 + 128 * h:256 + 128 * h + 128], ALU.mult))
                    D2()
                    for h in range(2):
                        rec("vector", lambda e, h=h: e.tensor_tensor(
                            scNT[h], sc_n2[:, 128 * h:128 * (h + 1)],
                            c_striun[:], ALU.mult))
                    D2()
                    for h in range(2):
                        rec("vector", lambda e, h=h: e.tensor_tensor(
                            sc_t2[:, 128 * h:128 * (h + 1)],
                            sc_e2[:, 128 * h:128 * (h + 1)], psKQ_[h], ALU.mult))
                    D2()
                    for h in range(2):
                        rec("vector", lambda e, h=h: e.tensor_tensor(
                            scPT[h][:], sc_t2[:, 128 * h:128 * (h + 1)],
                            c_triu1[:], ALU.mult))
                    D2()
                else:
                    for h in range(2):
                        rec("vector", lambda e, h=h,
                            beTc=gates["betT"][i][:, (hv0 + h):(hv0 + h) + 1]:
                            e.scalar_tensor_tensor(scN[h], psA_[h], beTc,
                                                   c_striln[:], ALU.mult, ALU.mult))
                    D2()
                    for h in range(2):
                        rec("vector", lambda e, h=h: e.tensor_tensor(
                            sc_n2[:, 128 * h:128 * (h + 1)], psA_[h],
                            c_striun[:], ALU.mult))
                    D2()
                    for h in range(2):
                        rec("vector", lambda e, h=h: e.tensor_tensor(
                            scNT[h], sc_n2[:, 128 * h:128 * (h + 1)],
                            pp[1][:, 256 + 128 * h:256 + 128 * h + 128], ALU.mult))
                    D2()
                    for h in range(2):
                        rec("vector", lambda e, h=h: e.tensor_tensor(
                            scPT[h][:], psKQ_[h], c_triu1[:], ALU.mult))
                    D2()
                for h in range(2):
                    src_rep = (pp[2][:, 128 * h:128 * h + 128] if gated
                               else pp[1][:, 256 + 128 * h:256 + 128 * h + 128])
                    kj = khat[0 if gated else h]
                    rec("vector", lambda e, h=h, kj=kj, tc0=tc0, src_rep=src_rep:
                        e.tensor_tensor(scKb[h][:], kj[:, tc0:tc0 + CH], src_rep,
                                        ALU.mult))
                D2()
                for h in range(2):
                    jk = 0 if gated else h
                    if gated:
                        rec("vector", lambda e, h=h, jk=jk, hh=hv0 + h, i=i:
                            e.tensor_scalar(scKt[h][:],
                                            ps_tb[:, 128 * jk:128 * (jk + 1)],
                                            gates["gamT"][i][:, hh:hh + 1], None,
                                            ALU.mult))
                    else:
                        rec("vector", lambda e, h=h, jk=jk:
                            e.tensor_copy(scKt[h][:],
                                          ps_tb[:, 128 * jk:128 * (jk + 1)]))
                D2()
                if gated:
                    for h in range(2):
                        rec("vector", lambda e, h=h, qj=qhat[0], tc0=tc0:
                            e.tensor_tensor(scQb[h][:], qj[:, tc0:tc0 + CH],
                                            pp[2][:, 256 + 128 * h:256 + 128 * h + 128],
                                            ALU.mult))
                    D2()
                vv = sig("vector")
                wait("tensor", s_v, vv)

                for h in range(2):
                    rec("tensor", lambda e, h=h: e.matmul(
                        pp[4][:, 256 * h:256 * h + 256], scKb[h][:], Sab[h],
                        start=True, stop=True))
                vmRHS = sig("tensor")
                wait("vector", s_mm, vmRHS)
                for h in range(2):
                    vsrc = (st_v[:, 128 * h:128 * (h + 1)] if gated
                            else vtm[h][:, tc0:tc0 + CH])
                    rec("vector", lambda e, h=h, vsrc=vsrc,
                        bsl=gates["betT"][i][:, (2 * g + h):(2 * g + h) + 1]:
                        e.scalar_tensor_tensor(
                            scY2[:, 256 * h:256 * h + 128], vsrc, bsl,
                            pp[4][:, 256 * h:256 * h + 128], ALU.mult,
                            ALU.subtract))
                    rec("vector", lambda e, h=h: e.tensor_scalar(
                        scY2[:, 256 * h + 128:256 * (h + 1)], pp[4][:, 256 * h + 128:256 * h + 256],
                        -1.0, None, ALU.mult))
                rec("vector", lambda e: e.drain()) if VEC_DRAINS else None
                vv = sig("vector")
                wait("tensor", s_v, vv)

                if g == 0 and i == 0:
                    for nm_, src_ in ((f"scR_l{layer}", scY[0]),
                                      (f"stv_l{layer}", st_v)):
                        tp_ = tap(nm_, [128, 256], BF16)
                        if tp_ is not None:
                            wait("sync", s_v, s_v.n)
                            dma("sync", tp_, src_[:], sem=s_st)
                    if tapd_any[0]:
                        wait("sync", s_st, s_st.n)
                        barrier(("tensor", "vector", "scalar", "sync"))

                for k_ in range(6):
                    for h in range(2):
                        rec("tensor", lambda e, h=h, k_=k_: e.matmul(
                            pp[2][:, 128 * h:128 * h + 128],
                            scU_of[k_][:, 128 * h:128 * h + 128],
                            scP2[k_][:, 128 * h:128 * h + 128],
                            start=True, stop=True))
                        if k_ < 5:
                            rec("tensor", lambda e, h=h, k_=k_: e.matmul(
                                pp[2][:, 256 + 128 * h:256 + 128 * h + 128],
                                scP2[k_][:, 128 * h:128 * h + 128],
                                scU_of[k_][:, 128 * h:128 * h + 128],
                                start=True, stop=True))
                    vm = sig("tensor")
                    wait("vector", s_mm, vm)
                    rec("vector", lambda e, k_=k_: e.tensor_copy(
                        scPU[k_ + 1][:, 0:(512 if k_ < 5 else 256)],
                        pp[2][:, 0:(512 if k_ < 5 else 256)]))
                    rec("vector", lambda e: e.drain()) if VEC_DRAINS else None
                    vv = sig("vector")
                    wait("tensor", s_v, vv)
                for k_ in range(7):
                    for h in range(2):
                        rec("tensor", lambda e, h=h, k_=k_: e.matmul(
                            pp[3][:, 256 * h:256 * h + 256],
                            scP2[k_][:, 128 * h:128 * h + 128],
                            scY[h], start=True, stop=True))
                    vm = sig("tensor")
                    wait("vector", s_mm, vm)
                    rec("vector", lambda e: e.tensor_tensor(
                        scY2[:], scY2[:], pp[3][:, 0:512], ALU.add))
                    rec("vector", lambda e: e.drain()) if VEC_DRAINS else None
                    vv = sig("vector")
                    wait("tensor", s_v, vv)

                if g == 0 and i == 0:
                    for nm_, src_ in ((f"scN_l{layer}", scN[0]),
                                      (f"scNT_l{layer}", scNT[0]),
                                      (f"scPT_l{layer}", scPT[0]),
                                      (f"scY_l{layer}", scY[0]),
                                      (f"scKb_l{layer}", scKb[0]),
                                      (f"scKt_l{layer}", scKt[0]),
                                      (f"scQb_l{layer}", scQb[0])):
                        tp_ = tap(nm_, [128, 256 if src_ is scY[0] else 128], BF16)
                        if tp_ is not None:
                            wait("sync", s_v, s_v.n)
                            dma("sync", tp_, src_[:], sem=s_st)
                    if tapd_any[0]:
                        wait("sync", s_st, s_st.n)
                        barrier(("tensor", "vector", "scalar", "sync"))

                for h in range(2):
                    qb = (scQb[h][:] if gated
                          else qhat[h][:, tc0:tc0 + CH])
                    rec("tensor", lambda e, h=h, qb=qb: e.matmul(
                        ps_m[:, 128 * h:128 * h + 128],
                        Sab2[:, 256 * h:256 * h + 128], qb,
                        start=True, stop=False))
                    rec("tensor", lambda e, h=h: e.matmul(
                        ps_m[:, 128 * h:128 * h + 128],
                        scY2[:, 256 * h:256 * h + 128], scPT[h][:],
                        start=False, stop=True))
                    rec("tensor", lambda e, h=h, qb=qb: e.matmul(
                        ps_m[:, 256 + 128 * h:256 + 128 * h + 128],
                        Sab2[:, 256 * h + 128:256 * (h + 1)],
                        qb, start=True, stop=False))
                    rec("tensor", lambda e, h=h: e.matmul(
                        ps_m[:, 256 + 128 * h:256 + 128 * h + 128],
                        scY2[:, 256 * h + 128:256 * (h + 1)],
                        scPT[h][:], start=False, stop=True))
                    rec("tensor", lambda e, h=h: e.matmul(
                        pp[5][:, 256 * h:256 * h + 256], scKt[h][:], scY[h],
                        start=True, stop=True))
                vmO = sig("tensor")
                wait("vector", s_mm, vmO)
                rec("vector", lambda e: e.tensor_copy(st_o[:], ps_m[:, 0:256]))
                rec("vector", lambda e: e.tensor_copy(st_c[:], ps_m[:, 256:512]))
                if gated:
                    for h in range(2):
                        rec("vector", lambda e, h=h, i=i, hh=2 * g + h:
                            e.scalar_tensor_tensor(Sa[h], Sa[h],
                                                   gates["lamCr"][i][:, hh:hh + 1],
                                                   pp[5][:, 256 * h:256 * h + 256],
                                                   ALU.mult, ALU.add))
                else:
                    rec("vector", lambda e: e.tensor_tensor(
                        Sa2[:], Sa2[:], pp[5][:, 0:512], ALU.add))
                rec("vector", lambda e: e.drain()) if VEC_DRAINS else None
                rec("vector", lambda e: e.tensor_copy(Sab2[:], Sa2[:]))
                rec("vector", lambda e: e.drain()) if VEC_DRAINS else None
                vv = sig("vector")
                wait("sync", s_v, vv)
                for h in range(2):
                    hh = 2 * g + h
                    dma("sync", o0_dram[layer].ap()[hh, :, tc0:tc0 + CH],
                        st_o[:, 128 * h:128 * h + 128], sem=s_st)
                    dma("sync", cc_dram[layer].ap()[hh, i, :, :],
                        st_c[:, 128 * h:128 * h + 128], sem=s_st)
                wait("tensor", s_st, s_st.n)
                wait("vector", s_st, s_st.n)

            for h in range(2):
                hh = 2 * g + h
                rec("tensor", lambda e, h=h: e.transpose(
                    ps_m[:, 0:128], Sa2[:, 256 * h + 128:256 * (h + 1)], c_idn32[:]))
                vm = sig("tensor")
                wait("vector", s_mm, vm)
                rec("vector", lambda e: e.tensor_copy(st_p[:], ps_m[:, 0:128]))
                vv = sig("vector")
                wait("sync", s_v, vv)
                wait("tensor", s_v, vv)
                dma("sync", agg_in[layer].ap()[:, 384 * hh:384 * hh + 256],
                    Sa[h], sem=s_st)
                dma("sync", agg_in[layer].ap()[:, 384 * hh + 256:384 * (hh + 1)],
                    st_p[:], sem=s_st)
                wait("vector", s_st, s_st.n)
            barrier(("tensor", "vector", "scalar", "gpsimd", "sync"))

        msc.close()
        return None


    def emit_scan_out(layer):
        gated = (layer == 0)
        HV = L0_HV if gated else L1_H
        eps_sl = c_eps[0:1, 0:1] if gated else c_eps[0:1, 1:2]
        gnw = gnw0 if gated else gnw1
        wout = w_out0 if gated else w_o1
        ssc = contextlib.ExitStack()
        L = f"s{layer}_"
        Tbuf = sbh(ssc, L + "Tbuf", 128, HV * 128, BF16)
        Jsl = sbh(ssc, L + "Jsl", 128, 512, BF16)
        Psl = sbh(ssc, L + "Psl", 128, 512, BF16)
        bl1 = sbh(ssc, L + "bl1", 128, 512)
        bl2 = sbh(ssc, L + "bl2", 128, 512)

        wait("gpsimd", s_st, s_st.n)
        vcc = s_cc.bump(1)
        rec("gpsimd", lambda e: e.collective_compute(
            "AllGather", ALU.bypass, replica_groups=[list(range(NCORES))],
            ins=[agg_in[layer].ap().opt()], outs=[agg_out[layer].ap().opt()])
            .then_inc(s_cc.h, 1), can_inc=False)
        wait("gpsimd", s_cc, vcc)
        sig("gpsimd")
        barrier()

        rec("vector", lambda e: e.memset(Tbuf[:], 0.0))
        sig("vector")
        barrier(("vector", "tensor", "gpsimd"))
        ag3 = agg_out[layer].ap().rearrange("r a (h c) -> r a h c", c=384)
        for p_ in range(NCORES - 1):
            for g4 in range(HV // 4):
                vj = dma("gpsimd", Jsl[:].rearrange("a (h c) -> a h c", c=128),
                         ag3[p_, :, 4 * g4:4 * g4 + 4, 0:128])
                vph = dma("gpsimd", Psl[:].rearrange("a (h c) -> a h c", c=128),
                          ag3[p_, :, 4 * g4:4 * g4 + 4, 256:384])
                wait("tensor", s_gd, vph)
                for r in range(4):
                    hh = 4 * g4 + r
                    rec("tensor", lambda e, hh=hh, r=r: e.matmul(
                        pp[0][:, 128 * r:128 * (r + 1)],
                        Psl[:, 128 * r:128 * (r + 1)],
                        Tbuf[:, 128 * hh:128 * (hh + 1)], start=True, stop=True))
                vm = sig("tensor")
                wait("vector", s_mm, vm)
                c0_, c1_ = 512 * g4, 512 * (g4 + 1)
                rec("vector", lambda e, c0_=c0_, c1_=c1_: e.tensor_tensor(
                    bl1[:], pp[0][:, 0:512], Jsl[:], ALU.add))
                rec("vector", lambda e: e.drain()) if VEC_DRAINS else None
                rec("vector", lambda e, c0_=c0_, c1_=c1_: e.tensor_tensor(
                    bl2[:], bl1[:], Tbuf[:, c0_:c1_], ALU.subtract))
                rec("vector", lambda e: e.drain()) if VEC_DRAINS else None
                rec("vector", lambda e, p_=p_, c0_=c0_, c1_=c1_:
                    e.scalar_tensor_tensor(Tbuf[:, c0_:c1_], bl2[:],
                                           c_pmask[:, p_:p_ + 1],
                                           Tbuf[:, c0_:c1_], ALU.mult, ALU.add))
                rec("vector", lambda e: e.drain()) if VEC_DRAINS else None
                vv = sig("vector")
                wait("tensor", s_v, vv)
                wait("gpsimd", s_v, vv)
            barrier(("tensor", "vector", "gpsimd"))

        ob = sbh(ssc, L + "ob", 128, TC, BF16)
        ccb = sbh(ssc, L + "ccb", 128, TC, BF16)
        o_wrk = sbh(ssc, L + "o_wrk", 128, TC)
        onrm = [sbh(ssc, L + f"onrm{j}", 128, TC, BF16) for j in range(8)]
        zsil = sbh(ssc, L + "zsil", 128, 512, BF16)
        cc3 = cc_dram[layer].ap()

        for b8 in range(HV // 8):
            for j in range(8):
                hh = 8 * b8 + j
                v1 = dma("sync", ob[:], o0_dram[layer].ap()[hh])
                v2 = dma("sync", ccb[:].rearrange("a (i c) -> a i c", c=128),
                         cc3[hh].rearrange("i a c -> a i c"))
                wait("tensor", s_ld, v2)
                for i in range(NCH):
                    rec("tensor", lambda e, i=i, hh=hh: e.matmul(
                        pp[i // 4][:, 128 * (i % 4):128 * (i % 4 + 1)],
                        Tbuf[:, 128 * hh:128 * (hh + 1)],
                        ccb[:, 128 * i:128 * (i + 1)], start=True, stop=True))
                vm = sig("tensor")
                wait("vector", s_mm, vm)
                wait("vector", s_ld, v2)
                rec("vector", lambda e: e.tensor_tensor(
                    o_wrk[:, 0:512], ob[:, 0:512], pp[0][:, 0:512], ALU.add))
                rec("vector", lambda e: e.tensor_tensor(
                    o_wrk[:, 512:1024], ob[:, 512:1024], pp[1][:, 0:512], ALU.add))
                rec("vector", lambda e: e.drain()) if VEC_DRAINS else None
                vv = sig("vector")
                wait("scalar", s_v, vv)
                wait("sync", s_v, vv)
                rec("scalar", lambda e: e.activation(f_scr[:, 0:TC], o_wrk[:],
                                                     ACT.Square))
                vs = sig("scalar")
                wait("tensor", s_s, vs)
                for ci, (o_, n_) in enumerate(_cc(TC, 512)):
                    rec("tensor", lambda e, o_=o_, n_=n_, ci=ci: e.matmul(
                        pp[2][0:1, 256 * ci:256 * ci + n_ if False else 0:n_]
                        if False else pp[2 + ci][0:1, 0:n_],
                        c_onec[:], f_scr[:, o_:o_ + n_], start=True, stop=True))
                vm = sig("tensor")
                wait("vector", s_mm, vm)
                rec("vector", lambda e: e.tensor_copy(r_mu[0:1, 0:512], pp[2][0:1, 0:512]))
                rec("vector", lambda e: e.tensor_copy(r_mu[0:1, 512:1024],
                                                      pp[3][0:1, 0:512]))
                rec("vector", lambda e: e.drain()) if VEC_DRAINS else None
                rec("vector", lambda e: e.tensor_scalar(r_mu[0:1, 0:TC], r_mu[0:1, 0:TC],
                                                        1.0 / DV, None, ALU.mult))
                vv = sig("vector")
                wait("scalar", s_v, vv)
                rec("scalar", lambda e: e.activation(r_t0[0:1, 0:TC],
                                                     r_mu[0:1, 0:TC], ACT.Sqrt,
                                                     bias=eps_sl))
                vs = sig("scalar")
                wait("vector", s_s, vs)
                rec("vector", lambda e: e.reciprocal(r_t0[0:1, 0:TC],
                                                     r_t0[0:1, 0:TC]))
                rec("vector", lambda e: e.drain()) if VEC_DRAINS else None
                vv = sig("vector")
                wait("tensor", s_v, vv)
                for ci, (o_, n_) in enumerate(_cc(TC, 512)):
                    rec("tensor", lambda e, o_=o_, n_=n_, ci=ci: e.matmul(
                        pp[2 + ci][:, 0:n_], c_oner[:],
                        r_t0[0:1, o_:o_ + n_], start=True, stop=True))
                vm = sig("tensor")
                wait("vector", s_mm, vm)
                for ci, (o_, n_) in enumerate(_cc(TC, 512)):
                    rec("vector", lambda e, j=j, o_=o_, n_=n_, ci=ci:
                        e.scalar_tensor_tensor(onrm[j][:, o_:o_ + n_],
                                               o_wrk[:, o_:o_ + n_], gnw[:],
                                               pp[2 + ci][:, 0:n_],
                                               ALU.mult, ALU.mult))
                rec("vector", lambda e: e.drain()) if VEC_DRAINS else None
                vv = sig("vector")
                wait("tensor", s_v, vv)
                wait("scalar", s_v, vv)
            barrier(("tensor", "vector", "scalar", "gpsimd"))

            if gated:
                def evac_z(jg, bank, toff, tn):
                    wait("scalar", s_v, s_v.n)
                    rec("scalar", lambda e, bank=bank, tn=tn: e.activation(
                        zsil[:, 0:tn], bank[:, 0:tn], ACT.Silu))
                    vs_ = sig("scalar")
                    wait("vector", s_s, vs_)
                    rec("vector", lambda e, jg=jg, toff=toff, tn=tn: e.tensor_tensor(
                        onrm[jg][:, toff - HALO:toff - HALO + tn],
                        onrm[jg][:, toff - HALO:toff - HALO + tn],
                        zsil[:, 0:tn], ALU.mult))
                proj_cols(w_z, 1024 * b8, 1024, TOKS_OWN, evac_z, slab_of=4)

            wrows = wout[1024 * b8:1024 * (b8 + 1), :] if gated else \
                wout[1024 * b8:1024 * (b8 + 1), :]
            def evac_out(jg, bank, toff, tn):
                rec("vector", lambda e, jg=jg, bank=bank, toff=toff, tn=tn:
                    e.tensor_tensor(x_sb[jg][:, toff:toff + tn],
                                    x_sb[jg][:, toff:toff + tn],
                                    bank[:, 0:tn], ALU.add))
            proj_cols(wrows, 0, D, TOKS_OWN, evac_out, kt=8, src=onrm,
                      slab_of=4, src_off=-HALO)
            barrier(("tensor", "vector", "scalar", "gpsimd"))
        ssc.close()

    tern_cc = [0]

    def emit_ternary():
        tsc = contextlib.ExitStack()
        tw = sbh(tsc, "t_w", 128, 1024)
        tq = sbh(tsc, "t_q", 128, 1024, BF16)
        tcmp = sbh(tsc, "t_c", 128, 1024)
        tacc = sbh(tsc, "t_acc", 128, 8)
        tred = sbh(tsc, "t_red", 128, 1)
        t32a = sbh(tsc, "t32a", 32, 32)
        t32b = sbh(tsc, "t32b", 32, 32)
        throw = sbh(tsc, "throw", 1, 8)
        scrow = sbh(tsc, "scrow", 1, 8)
        thcol = {w: sbh(tsc, f"th_{w}", 128, 2)
                 for w in ("ffn0", "up", "gate", "down")}
        WDEFS = [("ffn0", wT_ffn0, D, NSH0), ("up", wT_up, D, NSHE),
                 ("gate", wT_gate, D, NSHE), ("down", wT_down, DE, NSHD)]
        rec("vector", lambda e: e.memset(tacc[:], 0.0))
        rec("vector", lambda e: e.memset(t32a[:], 0.0))
        rec("vector", lambda e: e.memset(t32b[:], 0.0))
        rec("vector", lambda e: e.memset(throw[:], 0.0))
        rec("vector", lambda e: e.memset(scrow[:, 4:8], 0.0))
        rec("vector", lambda e: e.memset(scrow[:, 0:1], 1.0 / (2048.0 * 2048.0)))
        rec("vector", lambda e: e.memset(scrow[:, 1:2], 1.0 / (DE * D)))
        rec("vector", lambda e: e.memset(scrow[:, 2:3], 1.0 / (DE * D)))
        rec("vector", lambda e: e.memset(scrow[:, 3:4], 1.0 / (D * DE)))
        sig("vector")
        barrier(("vector", "sync", "gpsimd", "scalar", "tensor"))
        for wi, (nm, wd, nr, ncol) in enumerate(WDEFS):
            for kr in range(nr // 128):
                for (o_, n_) in _cc(ncol, 1024):
                    v1 = dma("sync", tw[:, 0:n_],
                             wd[kr * 128:(kr + 1) * 128, o_:o_ + n_])
                    wait("vector", s_ld, v1)
                    rec("vector", lambda e, n_=n_: e.tensor_reduce(
                        tred[:], tw[:, 0:n_], AX, ALU.add,
                        apply_absolute_value=True))
                    rec("vector", lambda e: e.drain()) if VEC_DRAINS else None
                    rec("vector", lambda e, wi=wi: e.tensor_tensor(
                        tacc[:, wi:wi + 1], tacc[:, wi:wi + 1], tred[:], ALU.add))
                    rec("vector", lambda e: e.drain()) if VEC_DRAINS else None
                    vv = sig("vector")
                    wait("sync", s_v, vv)
        vv = sig("vector")
        wait("tensor", s_v, vv)
        rec("tensor", lambda e: e.matmul(ps_m[0:1, 0:8], c_onec[:], tacc[:, 0:8],
                                         start=True, stop=True))
        vm = sig("tensor")
        wait("vector", s_mm, vm)
        rec("vector", lambda e: e.tensor_copy(throw[:], ps_m[0:1, 0:8]))
        rec("vector", lambda e: e.drain()) if VEC_DRAINS else None
        vv = sig("vector")
        wait("sync", s_v, vv)
        dma("sync", tern_in.ap(), throw[:])
        wait("gpsimd", s_ld, s_ld.n)
        vcc = s_cc.bump(1)
        rec("gpsimd", lambda e: e.collective_compute(
            "AllReduce", ALU.add, replica_groups=[list(range(NCORES))],
            ins=[tern_in.ap().opt()], outs=[tern_out.ap().opt()])
            .then_inc(s_cc.h, 1), can_inc=False)
        wait("gpsimd", s_cc, vcc)
        dma("gpsimd", throw[:], tern_out.ap())
        vg = gfence()
        wait("vector", s_g, vg)
        rec("vector", lambda e: e.tensor_tensor(throw[:, 0:4], throw[:, 0:4],
                                                scrow[:, 0:4], ALU.mult))
        rec("vector", lambda e: e.drain()) if VEC_DRAINS else None
        rec("vector", lambda e: e.tensor_scalar(throw[:, 0:1], throw[:, 0:1],
                                                1e-5, None, ALU.max))
        rec("vector", lambda e: e.drain()) if VEC_DRAINS else None
        vv = sig("vector")
        wait("tensor", s_v, vv)
        for wi, (nm, wd, nr, ncol) in enumerate(WDEFS):
            rec("tensor", lambda e, wi=wi: e.matmul(
                ps_m[:, 2 * wi:2 * wi + 1], c_oner[:], throw[0:1, wi:wi + 1],
                start=True, stop=True))
        vm = sig("tensor")
        wait("vector", s_mm, vm)
        for wi, (nm, wd, nr, ncol) in enumerate(WDEFS):
            eps_add = 0.0 if wi == 0 else 1e-8
            rec("vector", lambda e, wi=wi, nm=nm, eps_add=eps_add:
                e.tensor_scalar(thcol[nm][:, 0:1], ps_m[:, 2 * wi:2 * wi + 1],
                                eps_add, 0.5, ALU.add, ALU.mult))
            rec("vector", lambda e: e.drain()) if VEC_DRAINS else None
            rec("vector", lambda e, nm=nm: e.tensor_scalar(
                thcol[nm][:, 1:2], thcol[nm][:, 0:1], -1.0, None, ALU.mult))
            rec("vector", lambda e: e.drain()) if VEC_DRAINS else None
        vv = sig("vector")
        wait("sync", s_v, vv)
        barrier()
        for wi, (nm, wd, nr, ncol) in enumerate(WDEFS):
            for kr in range(nr // 128):
                for (o_, n_) in _cc(ncol, 1024):
                    v1 = dma("sync", tw[:, 0:n_],
                             wd[kr * 128:(kr + 1) * 128, o_:o_ + n_])
                    wait("vector", s_ld, v1)
                    rec("vector", lambda e, nm=nm, n_=n_: e.tensor_scalar(
                        tcmp[:, 0:n_], tw[:, 0:n_], thcol[nm][:, 1:2], None,
                        ALU.is_lt))
                    rec("vector", lambda e: e.drain()) if VEC_DRAINS else None
                    rec("vector", lambda e, nm=nm, n_=n_: e.scalar_tensor_tensor(
                        tq[:, 0:n_], tw[:, 0:n_], thcol[nm][:, 0:1],
                        tcmp[:, 0:n_], ALU.is_gt, ALU.subtract))
                    rec("vector", lambda e: e.drain()) if VEC_DRAINS else None
                    vv = sig("vector")
                    wait("sync", s_v, vv)
                    dma("sync", tq_in[nm].ap()[kr * 128:(kr + 1) * 128, o_:o_ + n_],
                        tq[:, 0:n_], sem=s_st)
                    wait("vector", s_st, s_st.n)
        wait("gpsimd", s_st, s_st.n)
        for nm in ("ffn0", "up", "gate", "down"):
            vcc = s_cc.bump(1)
            rec("gpsimd", lambda e, nm=nm: e.collective_compute(
                "AllGather", ALU.bypass, replica_groups=[list(range(NCORES))],
                ins=[tq_in[nm].ap().opt()], outs=[tq_out[nm].ap().opt()])
                .then_inc(s_cc.h, 1), can_inc=False)
        tern_cc[0] = s_cc.n
        sig("gpsimd")
        barrier()
        tsc.close()

    def emit_ffn0():
        fsc = contextlib.ExitStack()
        gcol = sbh(fsc, "f0_g", 128, KT)
        xnrow = sbh(fsc, "f0_row", 1, TC)
        amrow = sbh(fsc, "f0_am", 128, 2)
        arow8 = sbh(fsc, "f0_a8", 1, 8)
        t32a = sbh(fsc, "f0_32a", 32, 32)
        t32b = sbh(fsc, "f0_32b", 32, 32)
        sc2 = sbh(fsc, "f0_sc", 128, 2)
        xq = sbh(fsc, "f0_xq", 128, TC, BF16)
        rs1 = sbh(fsc, "f0_rs1", 128, TC)
        rs2 = sbh(fsc, "f0_rs2", 128, TC)
        wtile = sbh(fsc, "f0_wt", 128, 512, BF16)
        for i in range(KT):
            v1 = dma("sync", gcol[:, i:i + 1], g_ffn0[i * 128:(i + 1) * 128, :])
        emit_ln(HALO, TC)
        emit_xln(HALO, TC)
        barrier()
        rec("vector", lambda e: e.tensor_scalar(xnrow[:], r_t0[0:1, HALO:TF],
                                                1e-5, None, ALU.add))
        rec("vector", lambda e: e.drain()) if VEC_DRAINS else None
        rec("vector", lambda e: e.reciprocal(xnrow[:], xnrow[:]))
        rec("vector", lambda e: e.drain()) if VEC_DRAINS else None
        rec("vector", lambda e: e.tensor_tensor(xnrow[:], xnrow[:],
                                                r_t0[0:1, HALO:TF], ALU.mult))
        rec("vector", lambda e: e.drain()) if VEC_DRAINS else None
        vv = sig("vector")
        wait("scalar", s_v, vv)
        rec("scalar", lambda e: e.activation(xnrow[:], xnrow[:], ACT.Sqrt,
                                             bias=c_eps[0:1, 0:1]))
        vs = sig("scalar")
        wait("vector", s_s, vs)
        rec("vector", lambda e: e.reciprocal(xnrow[:], xnrow[:]))
        rec("vector", lambda e: e.drain()) if VEC_DRAINS else None
        vv = sig("vector")
        wait("tensor", s_v, vv)
        rec("vector", lambda e: e.memset(amrow[:, 0:1], 0.0))
        sig("vector")
        for ci, (o_, n_) in enumerate(_cc(TC, 512, HALO)):
            rec("tensor", lambda e, o_=o_, n_=n_, ci=ci: e.matmul(
                pp[4 + ci][:, 0:n_], c_oner[:], xnrow[0:1, o_ - HALO:o_ - HALO + n_],
                start=True, stop=True))
        vm = sig("tensor")
        wait("vector", s_mm, vm)
        for i in range(KT):
            for ci, (o_, n_) in enumerate(_cc(TC, 512, HALO)):
                rec("vector", lambda e, i=i, o_=o_, n_=n_, ci=ci: e.tensor_tensor(
                    f_scr[:, o_:o_ + n_], xln[i][:, o_:o_ + n_],
                    pp[4 + ci][:, 0:n_], ALU.mult))
            rec("vector", lambda e: e.drain()) if VEC_DRAINS else None
            rec("vector", lambda e, i=i: e.tensor_scalar(
                f_scr[:, HALO:TF], f_scr[:, HALO:TF], gcol[:, i:i + 1], None,
                ALU.mult))
            rec("vector", lambda e: e.drain()) if VEC_DRAINS else None
            rec("vector", lambda e: e.tensor_reduce(
                amrow[:, 1:2], f_scr[:, HALO:TF], AX, ALU.max,
                apply_absolute_value=True))
            rec("vector", lambda e: e.drain()) if VEC_DRAINS else None
            rec("vector", lambda e: e.tensor_tensor(
                amrow[:, 0:1], amrow[:, 0:1], amrow[:, 1:2], ALU.max))
            rec("vector", lambda e: e.drain()) if VEC_DRAINS else None
        vv = sig("vector")
        wait("tensor", s_v, vv)
        rec("tensor", lambda e: e.transpose(ps_m[0:1, 0:128], amrow[:, 0:1],
                                            c_idn32[:]))
        vm = sig("tensor")
        wait("vector", s_mm, vm)
        rec("vector", lambda e: e.memset(arow8[:], 0.0))
        rec("vector", lambda e: e.drain()) if VEC_DRAINS else None
        rec("vector", lambda e: e.tensor_reduce(arow8[:, 0:1], ps_m[0:1, 0:128],
                                                AX, ALU.max))
        rec("vector", lambda e: e.drain()) if VEC_DRAINS else None
        vv = sig("vector")
        wait("sync", s_v, vv)
        dma("sync", amax_in.ap(), arow8[:])
        wait("gpsimd", s_ld, s_ld.n)
        vcc = s_cc.bump(1)
        rec("gpsimd", lambda e: e.collective_compute(
            "AllReduce", ALU.max, replica_groups=[list(range(NCORES))],
            ins=[amax_in.ap().opt()], outs=[amax_out.ap().opt()])
            .then_inc(s_cc.h, 1), can_inc=False)
        wait("gpsimd", s_cc, vcc)
        dma("gpsimd", arow8[:], amax_out.ap())
        vg = gfence()
        wait("vector", s_g, vg)
        rec("vector", lambda e: e.tensor_scalar(arow8[:, 0:1], arow8[:, 0:1],
                                                1e-5, None, ALU.max))
        rec("vector", lambda e: e.drain()) if VEC_DRAINS else None
        rec("vector", lambda e: e.reciprocal(arow8[:, 1:2], arow8[:, 0:1]))
        rec("vector", lambda e: e.drain()) if VEC_DRAINS else None
        rec("vector", lambda e: e.tensor_scalar(arow8[:, 1:2], arow8[:, 1:2],
                                                127.0, None, ALU.mult))
        rec("vector", lambda e: e.drain()) if VEC_DRAINS else None
        vv = sig("vector")
        wait("tensor", s_v, vv)
        rec("tensor", lambda e: e.matmul(ps_m[:, 0:2], c_oner[:], arow8[0:1, 0:2],
                                         start=True, stop=True))
        vm = sig("tensor")
        wait("vector", s_mm, vm)
        rec("vector", lambda e: e.tensor_copy(sc2[:], ps_m[:, 0:2]))
        rec("vector", lambda e: e.drain()) if VEC_DRAINS else None
        vv = sig("vector")
        wait("tensor", s_v, vv)

        for ci2, (o2, n2) in enumerate(_cc(TC, 512, HALO)):
            pass
        for i in range(KT):
            for ci, (o_, n_) in enumerate(_cc(TC, 512, HALO)):
                rec("vector", lambda e, i=i, o_=o_, n_=n_, ci=ci: e.tensor_tensor(
                    f_scr[:, o_:o_ + n_], xln[i][:, o_:o_ + n_],
                    pp[4 + ci][:, 0:n_], ALU.mult))
            rec("vector", lambda e: e.drain()) if VEC_DRAINS else None
            rec("vector", lambda e, i=i: e.tensor_scalar(
                f_scr[:, HALO:TF], f_scr[:, HALO:TF], gcol[:, i:i + 1], None,
                ALU.mult))
            rec("vector", lambda e: e.drain()) if VEC_DRAINS else None
            rec("vector", lambda e: e.tensor_scalar(
                f_scr[:, HALO:TF], f_scr[:, HALO:TF], sc2[:, 1:2], None,
                ALU.mult))
            rec("vector", lambda e: e.drain()) if VEC_DRAINS else None
            rec("vector", lambda e: e.tensor_scalar(
                f_scr[:, HALO:TF], f_scr[:, HALO:TF], 127.0, -128.0,
                ALU.min, ALU.max))
            rec("vector", lambda e: e.drain()) if VEC_DRAINS else None
            rec("vector", lambda e: e.tensor_scalar(
                rs1[:, 0:TC], f_scr[:, HALO:TF], 384.5, None, ALU.add))
            rec("vector", lambda e: e.drain()) if VEC_DRAINS else None
            rec("vector", lambda e: e.tensor_scalar(
                rs1[:, 0:TC].bitcast(mybir.dt.int32),
                rs1[:, 0:TC].bitcast(mybir.dt.int32), -32768, None,
                ALU.bitwise_and))
            rec("vector", lambda e: e.drain()) if VEC_DRAINS else None
            rec("vector", lambda e: e.tensor_scalar(
                xq[:, 0:TC], rs1[:, 0:TC], -384.0, None, ALU.add))
            rec("vector", lambda e: e.drain()) if VEC_DRAINS else None
            vv = sig("vector")
            wait("scalar", s_v, vv)
            rec("scalar", lambda e, i=i: e.copy(xln[i][:, HALO:TF], xq[:, 0:TC]))
            vs = sig("scalar")
            wait("vector", s_s, vs)
        barrier()
        dma("gpsimd", arow8[:, 2:3], tern_out.ap()[0:1, 0:1])
        vg = gfence()
        wait("vector", s_g, vg)
        rec("vector", lambda e: e.tensor_scalar(arow8[:, 2:3], arow8[:, 2:3],
                                                1.0 / (2048.0 * 2048.0), None,
                                                ALU.mult))
        rec("vector", lambda e: e.drain()) if VEC_DRAINS else None
        rec("vector", lambda e: e.tensor_scalar(arow8[:, 2:3], arow8[:, 2:3],
                                                1e-5, None, ALU.max))
        rec("vector", lambda e: e.drain()) if VEC_DRAINS else None
        vv = sig("vector")
        wait("tensor", s_v, vv)
        rec("tensor", lambda e: e.matmul(ps_m[:, 4:5], c_oner[:], arow8[0:1, 2:3],
                                         start=True, stop=True))
        vm = sig("tensor")
        wait("vector", s_mm, vm)
        rec("vector", lambda e: e.tensor_scalar(sc2[:, 1:2], sc2[:, 0:1],
                                                1.0 / 127.0, None, ALU.mult))
        rec("vector", lambda e: e.drain()) if VEC_DRAINS else None
        rec("vector", lambda e: e.tensor_tensor(sc2[:, 1:2], sc2[:, 1:2],
                                                ps_m[:, 4:5], ALU.mult))
        rec("vector", lambda e: e.drain()) if VEC_DRAINS else None
        vv = sig("vector")
        wait("tensor", s_v, vv)

        tqf = tq_out["ffn0"].ap()
        mm_end = {}
        ev_sigs = []
        vldqs = {}
        wait("gpsimd", s_cc, tern_cc[0])
        for j in range(KT + 1):
            if j < KT:
                r0 = (128 * j) // NSH0
                c0_ = (128 * j) % NSH0
                if j >= 2:
                    wait("gpsimd", s_mm, mm_end[j - 2])
                vq = None
                dma("gpsimd",
                    wb_t[j % 2][:].rearrange("p (k c) -> p k c", c=256)
                    [:, :, 0:128],
                    tqf[r0].rearrange("(k p) c -> p k c", p=128)
                    [:, :, c0_:c0_ + 128])
                vldqs[j] = qfence("gpsimd")
            if j == 0:
                continue
            jj = j - 1
            for (toff, tn) in TOKS_OWN:
                gi = len(ev_sigs)
                bank = pp[gi % 4]
                if gi >= 4:
                    wait("tensor", s_v, ev_sigs[gi - 4])
                wait("tensor", s_gd, vldqs[jj])
                for k in range(KT):
                    rec("tensor", lambda e, k=k, jj=jj, toff=toff, tn=tn, bank=bank:
                        e.matmul(bank[:, 0:tn], wbuf[jj % 2][k][:, 0:128],
                                 xln[k][:, toff:toff + tn],
                                 start=(k == 0), stop=(k == KT - 1)))
                vm = sig("tensor")
                wait("vector", s_mm, vm)
                rec("vector", lambda e, jj=jj, toff=toff, tn=tn, bank=bank:
                    e.scalar_tensor_tensor(x_sb[jj][:, toff:toff + tn],
                                           bank[:, 0:tn], sc2[:, 1:2],
                                           x_sb[jj][:, toff:toff + tn],
                                           ALU.mult, ALU.add))
                ev_sigs.append(sig("vector"))
            mm_end[jj] = s_mm.n
        barrier()
        fsc.close()

    def emit_ffn1():
        fsc = contextlib.ExitStack()
        hb = [sbh(fsc, f"f1_h{j}", 128, 512, BF16) for j in range(64)]
        sup = sbh(fsc, "f1_su", 128, 4)
        srow = sbh(fsc, "f1_srow", 1, 8)
        rec("vector", lambda e: e.memset(sup[:], 0.0))
        sig("vector")
        barrier(("vector", "gpsimd", "sync"))
        dma("gpsimd", srow[:], tern_out.ap())
        vg = gfence()
        wait("vector", s_g, vg)
        rec("vector", lambda e: e.tensor_scalar(srow[:, 1:4], srow[:, 1:4],
                                                1.0 / (DE * D), 1e-8,
                                                ALU.mult, ALU.add))
        rec("vector", lambda e: e.drain()) if VEC_DRAINS else None
        vv = sig("vector")
        wait("tensor", s_v, vv)
        rec("tensor", lambda e: e.matmul(ps_m[:, 0:3], c_oner[:],
                                         srow[0:1, 1:4], start=True, stop=True))
        vm = sig("tensor")
        wait("vector", s_mm, vm)
        rec("vector", lambda e: e.tensor_copy(sup[:, 0:3], ps_m[:, 0:3]))
        rec("vector", lambda e: e.drain()) if VEC_DRAINS else None
        vv = sig("vector")
        wait("scalar", s_v, vv)
        wait("tensor", s_v, vv)

        emit_ln(HALO, TC)
        emit_xln(HALO, TC)
        barrier()
        wait("gpsimd", s_cc, tern_cc[0])
        tqu = tq_out["up"].ap()
        tqg = tq_out["gate"].ap()
        tqd = tq_out["down"].ap()
        for (toff, tn) in [(HALO, 512), (HALO + 512, 512)]:
            ug_mm_end = {}
            ug_sigs = []
            vlds = {}
            for j in range(65):
                if j < 64:
                    r0 = (128 * j) // NSHE
                    c0_ = (128 * j) % NSHE
                    if j >= 2:
                        wait("gpsimd", s_mm, ug_mm_end[j - 2])
                    dma("gpsimd",
                        wb_t[j % 2][:].rearrange("p (k c) -> p k c", c=256)
                        [:, :, 0:128],
                        tqu[r0].rearrange("(k p) c -> p k c", p=128)
                        [:, :, c0_:c0_ + 128])
                    dma("gpsimd",
                        wb_t[j % 2][:].rearrange("p (k c) -> p k c", c=256)
                        [:, :, 128:256],
                        tqg[r0].rearrange("(k p) c -> p k c", p=128)
                        [:, :, c0_:c0_ + 128])
                    vlds[j] = qfence("gpsimd")
                if j == 0:
                    continue
                jj = j - 1
                gi = len(ug_sigs)
                if gi >= 2:
                    wait("tensor", s_v, ug_sigs[gi - 2][0])
                    wait("tensor", s_s, ug_sigs[gi - 2][1])
                wait("tensor", s_gd, vlds[jj])
                for k in range(KT):
                    rec("tensor", lambda e, k=k, jj=jj, toff=toff, tn=tn:
                        e.matmul(pp[(jj % 2) * 2][:, 0:tn],
                                 wbuf[jj % 2][k][:, 0:128],
                                 xln[k][:, toff:toff + tn],
                                 start=(k == 0), stop=(k == KT - 1)))
                for k in range(KT):
                    rec("tensor", lambda e, k=k, jj=jj, toff=toff, tn=tn:
                        e.matmul(pp[(jj % 2) * 2 + 1][:, 0:tn],
                                 wbuf[jj % 2][k][:, 128:256],
                                 xln[k][:, toff:toff + tn],
                                 start=(k == 0), stop=(k == KT - 1)))
                vm = sig("tensor")
                wait("scalar", s_mm, vm)
                wait("scalar", s_v, s_v.n)
                rec("scalar", lambda e, jj=jj, tn=tn: e.activation(
                    f_scr[:, 0:tn], pp[(jj % 2) * 2][:, 0:tn], ACT.Silu,
                    scale=sup[:, 0:1]))
                vs = sig("scalar")
                wait("vector", s_s, vs)
                rec("vector", lambda e, jj=jj, tn=tn: e.scalar_tensor_tensor(
                    hb[jj][:, 0:tn], pp[(jj % 2) * 2 + 1][:, 0:tn], sup[:, 1:2],
                    f_scr[:, 0:tn], ALU.mult, ALU.mult))
                ug_sigs.append((sig("vector"), sig("scalar")))
                ug_mm_end[jj] = s_mm.n
            barrier(("tensor", "vector", "scalar", "gpsimd"))
            dn_sigs = []
            for j in range(KT):
                r0 = (128 * j) // NSHD
                c0_ = (128 * j) % NSHD
                if j >= 1:
                    wait("gpsimd", s_mm, s_mm.n)
                for kb in range(4):
                    dma("gpsimd",
                        wb_t[kb // 2][:].rearrange("p (k c) -> p k c", c=256)
                        [:, :, 128 * (kb % 2):128 * (kb % 2) + 128],
                        tqd[r0, 2048 * kb:2048 * (kb + 1), :]
                        .rearrange("(k p) c -> p k c", p=128)
                        [:, :, c0_:c0_ + 128])
                vldd = qfence("gpsimd")
                gi = len(dn_sigs)
                bank = pp[4 + (gi % 2)]
                if gi >= 2:
                    wait("tensor", s_v, dn_sigs[gi - 2])
                wait("tensor", s_gd, vldd)
                for k in range(64):
                    rec("tensor", lambda e, k=k, tn=tn, bank=bank:
                        e.matmul(bank[:, 0:tn],
                                 wbuf[k // 32][k % 16][:, 128 * ((k // 16) % 2):
                                                       128 * ((k // 16) % 2) + 128],
                                 hb[k][:, 0:tn],
                                 start=(k == 0), stop=(k == 63)))
                vm = sig("tensor")
                wait("vector", s_mm, vm)
                rec("vector", lambda e, j=j, toff=toff, tn=tn, bank=bank:
                    e.scalar_tensor_tensor(x_sb[j][:, toff:toff + tn],
                                           bank[:, 0:tn], sup[:, 2:3],
                                           x_sb[j][:, toff:toff + tn],
                                           ALU.mult, ALU.add))
                dn_sigs.append(sig("vector"))
            barrier(("tensor", "vector", "scalar", "gpsimd"))
        fsc.close()


    def tap_x(nm):
        tp = tap(nm, [D, TC])
        if tp is not None:
            barrier()
            for i in range(KT):
                dma("sync", tp[i * 128:(i + 1) * 128, :], x_sb[i][:, HALO:TF],
                    sem=s_st)
            wait("sync", s_st, s_st.n)
            barrier()

    emit_ternary()
    gates0 = {"bcum": bcum, "lam": lam, "beta": beta0, "lb": lb0, "bT": bT,
              "betT": betT, "gamT": gamT, "lamCr": lamCr, "hsel": hsel0}
    emit_mixer(0, gates0)
    l0sc.close()
    emit_scan_out(0)
    barrier()
    tap_x("x1")
    emit_ffn0()
    barrier()
    tap_x("x2")

    l1sc = contextlib.ExitStack()
    beta1 = sbh(l1sc, "beta1", 16, TC)
    betT1 = [sbh(l1sc, f"betT1_{i}", 128, 16) for i in range(NCH)]
    emit_ln(HALO, TC)
    emit_xln(HALO, TC)
    barrier()
    for k in range(KT):
        dma("gpsimd", wbuf[0][k][:, 0:16], w_b1[k * 128:(k + 1) * 128, :])
    vldb = qfence("gpsimd")
    for (toff, tn) in TOKS_OWN:
        wait("tensor", s_gd, vldb)
        for k in range(KT):
            rec("tensor", lambda e, k=k, toff=toff, tn=tn:
                e.matmul(pp[0][0:16, 0:tn], wbuf[0][k][:, 0:16],
                         xln[k][:, toff:toff + tn],
                         start=(k == 0), stop=(k == KT - 1)))
        vm = sig("tensor")
        wait("scalar", s_mm, vm)
        rec("scalar", lambda e, toff=toff, tn=tn: e.activation(
            beta1[:, toff - HALO:toff - HALO + tn], pp[0][0:16, 0:tn],
            ACT.Sigmoid))
        vs = sig("scalar")
        wait("tensor", s_s, vs)
    barrier()
    for i in range(NCH):
        rec("tensor", lambda e, i=i: e.transpose(ps_m[:, 0:16],
                                                 beta1[:, i * CH:(i + 1) * CH],
                                                 c_idn32[0:16, 0:16]))
        vm = sig("tensor")
        wait("vector", s_mm, vm)
        rec("vector", lambda e, i=i: e.tensor_copy(betT1[i][:], ps_m[:, 0:16]))
        vv = sig("vector")
        wait("tensor", s_v, vv)
    barrier()
    gates1 = {"beta": beta1, "betT": betT1, "hsel": hsel0}
    emit_mixer(1, gates1)
    l1sc.close()
    emit_scan_out(1)
    barrier()
    msel.close()
    tap_x("x3")
    emit_ffn1()
    barrier()

    for nm, dt_, shp, srcap in (
        ("o0_l0", BF16, [L0_HV, DV, TC], o0_dram[0].ap()),
        ("cc_l0", BF16, [L0_HV, NCH, DK, CH], cc_dram[0].ap()),
        ("agg0_in", F32, [128, L0_HV * 384], agg_in[0].ap()),
        ("o0_l1", BF16, [L1_H, DV, TC], o0_dram[1].ap()),
    ):
        tp = tap(nm, shp, dt_)
        if tp is not None:
            dma("sync", tp, srcap, sem=s_st)
    wait("sync", s_st, s_st.n)

    for i in range(KT):
        dma("sync", y_out[i * 128:(i + 1) * 128, :], x_sb[i][:, HALO:TF], sem=s_st)
    wait("sync", s_st, s_st.n)

    block = E(nc.Block())

    @block.sync
    def _(e):
        for fn in streams["sync"]:
            fn(e)

    @block.scalar
    def _(e):
        for fn in streams["scalar"]:
            fn(e)

    @block.vector
    def _(e):
        for fn in streams["vector"]:
            fn(e)

    @block.tensor
    def _(e):
        for fn in streams["tensor"]:
            fn(e)

    @block.gpsimd
    def _(e):
        for fn in streams["gpsimd"]:
            fn(e)

    stack.close()
    return nc, taps


def _hsel(nh):
    m = np.zeros((nh, nh * 128), np.float32)
    for h in range(nh):
        m[h, h * 128:(h + 1) * 128] = 1.0
    return m


def make_inputs(inputs, core):
    x = np.asarray(inputs["L0_attn_x"])[0]
    xt = np.ascontiguousarray(x.T)
    c0 = core * TC
    if core == 0:
        halo = np.zeros((D, HALO), np.float32)
    else:
        halo = xt[:, c0 - HALO:c0]
    xs = np.concatenate([halo, xt[:, c0:c0 + TC]], axis=1)

    vc = np.zeros((128, 8), np.float32)
    vc[:32, 0] = np.asarray(inputs["L0_attn_A_log"])
    vc[:32, 1] = np.asarray(inputs["L0_attn_dt_bias"])
    vc[:, 2] = np.asarray(inputs["L0_attn_norm_weight"])
    vc[:, 3] = np.asarray(inputs["L1_attn_o_norm_weight"])
    vc[0, 4] = 1.0 / (2048 * 2048)
    vc[1, 4] = 1.0 / (DE * D)
    vc[2, 4] = 1.0 / (DE * D)
    vc[3, 4] = 1.0 / (D * DE)

    pm = np.zeros((128, NCORES), np.float32)
    pm[:, :core] = 1.0

    return {
        "x": np.ascontiguousarray(xs, np.float32),
        "w_qkv": np.asarray(inputs["L0_attn_W_qkv"], np.float32),
        "w_z": np.asarray(inputs["L0_attn_W_z"], np.float32),
        "w_ab": np.ascontiguousarray(np.concatenate(
            [np.asarray(inputs["L0_attn_W_a"]), np.asarray(inputs["L0_attn_W_b"])],
            axis=1), np.float32),
        "w_out0": np.asarray(inputs["L0_attn_W_out"], np.float32),
        "conv_w": np.asarray(inputs["L0_attn_conv_w"], np.float32),
        "vec_col": vc,
        "g_ffn0": np.ascontiguousarray(
            np.asarray(inputs["L0_ffn_norm_weight"], np.float32).reshape(D, 1)),
        "wT_ffn0": np.ascontiguousarray(
            np.asarray(inputs["L0_ffn_weight"], np.float32).T
            [:, core * NSH0:(core + 1) * NSH0]),
        "w_q1": np.asarray(inputs["L1_attn_Wq"], np.float32),
        "w_k1": np.asarray(inputs["L1_attn_Wk"], np.float32),
        "w_v1": np.asarray(inputs["L1_attn_Wv"], np.float32),
        "w_b1": np.asarray(inputs["L1_attn_Wb"], np.float32),
        "w_o1": np.asarray(inputs["L1_attn_Wo"], np.float32),
        "wT_up": np.ascontiguousarray(
            np.asarray(inputs["L1_ffn_W_up"], np.float32).T
            [:, core * NSHE:(core + 1) * NSHE]),
        "wT_gate": np.ascontiguousarray(
            np.asarray(inputs["L1_ffn_W_gate"], np.float32).T
            [:, core * NSHE:(core + 1) * NSHE]),
        "wT_down": np.ascontiguousarray(
            np.asarray(inputs["L1_ffn_W_down"], np.float32).T
            [:, core * NSHD:(core + 1) * NSHD]),
        "pmask": pm,
        "hsel0": _hsel(32),
    }


def kernel(**inputs):
    nc, taps = build_program()
    in_maps = [make_inputs(inputs, c) for c in range(NCORES)]
    res = run_bass_kernel_spmd(nc, in_maps, core_ids=list(range(NCORES)))
    outs = [np.asarray(r["y"]) for r in res.results]
    full = np.concatenate(outs, axis=1)
    return np.ascontiguousarray(full.T)[None]


# version 22
# speedup vs baseline: 1.0120x; 1.0120x over previous
import contextlib
import numpy as np

import concourse.bass as bass
import concourse.mybir as mybir
from concourse.bass_utils import run_bass_kernel_spmd

F32 = mybir.dt.float32
BF16 = mybir.dt.bfloat16
ALU = mybir.AluOpType
ACT = mybir.ActivationFunctionType
AX = mybir.AxisListType.X

NCORES = 8
D = 2048
T = 8192
TC = 1024
HALO = 4
TF = TC + HALO
KT = D // 128
CH = 128
NCH = TC // CH

L0_HV, L0_HK, DK, DV = 32, 16, 128, 128
KW = 4
L1_H = 16
DE = 8192
NSH0 = 2048 // NCORES
NSHE = DE // NCORES
NSHD = D // NCORES

DEBUG_TAPS = ()
SIM_DRAINS = True
VEC_DRAINS = True


def _cc(total, step=512, off0=0):
    out = []
    o = 0
    while o < total:
        n = min(step, total - o)
        out.append((off0 + o, n))
        o += n
    return out


class Sem:
    def __init__(self, h):
        self.h = h
        self.n = 0

    def bump(self, k=1):
        self.n += k
        return self.n


def build_program():
    nc = bass.Bass("TRN2", target_bir_lowering=False)

    def dram_in(name, shape, dtype=F32):
        return nc.dram_tensor(name, shape, dtype, kind="ExternalInput").ap()

    x_in = dram_in("x", [D, TF])
    w_qkv = dram_in("w_qkv", [D, 8192])
    w_z = dram_in("w_z", [D, 4096])
    w_ab = dram_in("w_ab", [D, 64])
    w_out0 = dram_in("w_out0", [4096, D])
    conv_w = dram_in("conv_w", [8192, KW])
    vec_col = dram_in("vec_col", [128, 8])
    g_ffn0 = dram_in("g_ffn0", [D, 1])
    wT_ffn0 = dram_in("wT_ffn0", [D, NSH0])
    w_q1 = dram_in("w_q1", [D, D])
    w_k1 = dram_in("w_k1", [D, D])
    w_v1 = dram_in("w_v1", [D, D])
    w_b1 = dram_in("w_b1", [D, L1_H])
    w_o1 = dram_in("w_o1", [D, D])
    wT_up = dram_in("wT_up", [D, NSHE])
    wT_gate = dram_in("wT_gate", [D, NSHE])
    wT_down = dram_in("wT_down", [DE, NSHD])
    pmask = dram_in("pmask", [128, NCORES])
    hsel0_in = dram_in("hsel0", [32, 32 * 128])

    y_out = nc.dram_tensor("y", [D, TC], F32, kind="ExternalOutput").ap()

    taps = {}

    tapd_any = [False]

    def tap(name, shape, dtype=F32):
        if name in DEBUG_TAPS:
            tapd_any[0] = True
            taps[name] = nc.dram_tensor("tap_" + name, shape, dtype,
                                        kind="ExternalOutput").ap()
            return taps[name]
        return None

    tern_in = nc.dram_tensor("tern_in", [1, 8], F32)
    tern_out = nc.dram_tensor("tern_out", [1, 8], F32)
    tq_in = {
        "ffn0": nc.dram_tensor("tq_ffn0_in", [D, NSH0], BF16),
        "up": nc.dram_tensor("tq_up_in", [D, NSHE], BF16),
        "gate": nc.dram_tensor("tq_gate_in", [D, NSHE], BF16),
        "down": nc.dram_tensor("tq_down_in", [DE, NSHD], BF16),
    }
    tq_out = {
        "ffn0": nc.dram_tensor("tq_ffn0_out", [NCORES, D, NSH0], BF16),
        "up": nc.dram_tensor("tq_up_out", [NCORES, D, NSHE], BF16),
        "gate": nc.dram_tensor("tq_gate_out", [NCORES, D, NSHE], BF16),
        "down": nc.dram_tensor("tq_down_out", [NCORES, DE, NSHD], BF16),
    }
    agg_in = {0: nc.dram_tensor("agg0_in", [128, L0_HV * 384], F32),
              1: nc.dram_tensor("agg1_in", [128, L1_H * 384], F32)}
    agg_out = {0: nc.dram_tensor("agg0_out", [NCORES, 128, L0_HV * 384], F32),
               1: nc.dram_tensor("agg1_out", [NCORES, 128, L1_H * 384], F32)}
    amax_in = nc.dram_tensor("amax_in", [1, 8], F32)
    amax_out = nc.dram_tensor("amax_out", [1, 8], F32)
    o0_dram = {0: nc.dram_tensor("o0_l0", [L0_HV, DV, TC], BF16),
               1: nc.dram_tensor("o0_l1", [L1_H, DV, TC], BF16)}
    cc_dram = {0: nc.dram_tensor("cc_l0", [L0_HV, NCH, DK, CH], BF16),
               1: nc.dram_tensor("cc_l1", [L1_H, NCH, DK, CH], BF16)}

    streams = {e: [] for e in ("sync", "scalar", "vector", "tensor", "gpsimd")}
    incable = {e: [] for e in ("sync", "scalar", "vector", "tensor", "gpsimd")}

    def rec(eng, fn, can_inc=True):
        streams[eng].append(fn)
        incable[eng].append(can_inc)

    stack = contextlib.ExitStack()
    E = stack.enter_context

    s_ld = Sem(E(nc.semaphore("s_ld")))
    s_gd = Sem(E(nc.semaphore("s_gd")))
    s_cc = Sem(E(nc.semaphore("s_cc")))
    s_mm = Sem(E(nc.semaphore("s_mm")))
    s_v = Sem(E(nc.semaphore("s_v")))
    s_s = Sem(E(nc.semaphore("s_s")))
    s_g = Sem(E(nc.semaphore("s_g")))
    s_st = Sem(E(nc.semaphore("s_st")))
    s_sy = Sem(E(nc.semaphore("s_sy")))

    ENG_SEM = {"tensor": s_mm, "vector": s_v, "scalar": s_s, "gpsimd": s_g,
               "sync": s_sy}

    def dma(q, out_ap, in_ap, sem=None):
        s = sem if sem is not None else (s_ld if q == "sync" else s_gd)
        v = s.bump(16)
        rec(q, lambda e, o=out_ap, i=in_ap, sh=s.h: e.dma_start(out=o, in_=i)
            .then_inc(sh, 16), can_inc=False)
        return v

    def qfence(q):
        s = s_ld if q == "sync" else s_gd
        wait(q, s, s.n)
        return s.n

    def wait(eng, s, v):
        rec(eng, lambda e, sh=s.h, vv=v: e.wait_ge(sh, vv), can_inc=False)

    def sig(eng):
        s = ENG_SEM[eng]
        v = s.bump(1)
        if streams[eng] and incable[eng][-1]:
            fn = streams[eng][-1]
            streams[eng][-1] = (lambda e, fn=fn, sh=s.h: fn(e).then_inc(sh, 1))
            incable[eng][-1] = False
        else:
            rec(eng, lambda e, sh=s.h: e.drain().then_inc(sh, 1), can_inc=False)
        return v

    def gfence():
        wait("gpsimd", s_gd, s_gd.n)
        return sig("gpsimd")

    COMPUTE = ("scalar", "vector", "tensor", "gpsimd")

    def barrier(engines=COMPUTE + ("sync",)):
        if "sync" in engines:
            wait("sync", s_ld, s_ld.n)
            wait("sync", s_st, s_st.n)
        if "gpsimd" in engines:
            wait("gpsimd", s_gd, s_gd.n)
        vals = {e: sig(e) for e in engines}
        for e in engines:
            for o in engines:
                if o != e:
                    wait(e, ENG_SEM[o], vals[o])
        return vals

    def sbh(st_, name, p, f, dt=F32):
        return st_.enter_context(nc.sbuf_tensor(name, [p, f], dt))

    def sb(name, p, f, dt=F32):
        return sbh(stack, name, p, f, dt)

    c_onec = sb("c_onec", 128, 1)
    c_oner = sb("c_oner", 1, 128)
    c_striln = sb("c_striln", 128, 128)
    c_striun = sb("c_striun", 128, 128)
    c_triu1 = sb("c_triu1", 128, 128)
    c_idn = sb("c_idn", 128, 128, BF16)
    c_idn32 = sb("c_idn32", 128, 128)
    c_pmask = sb("c_pmask", 128, NCORES)
    c_eps = sb("c_eps", 128, 8)
    vcol = sb("vcol", 128, 8)
    expA = sb("expA", 32, 1)
    gnw0 = sb("gnw0", 128, 1)
    gnw1 = sb("gnw1", 128, 1)
    zer128 = sb("zer128", 128, 128)
    c_onecb = sb("c_onecb", 128, 1, BF16)
    cw_sb = sb("cw_sb", 128, 4 * 64)

    x_sb = [sb(f"x{i}", 128, TF) for i in range(KT)]
    xln = [sb(f"xln{i}", 128, TF, BF16) for i in range(KT)]
    r_mu = sb("r_mu", 1, TF)
    r_iv = sb("r_iv", 1, TF)
    r_t0 = sb("r_t0", 1, TF)
    f_scr = sb("f_scr", 128, TF)
    wb_t = [sb(f"wbt{b}", 128, KT * 256, BF16) for b in range(2)]
    wbuf = [[wb_t[b][:, 256 * k:256 * (k + 1)] for k in range(KT)]
            for b in range(2)]

    pp = [E(nc.psum_tensor(f"pp{i}", [128, 512], F32)) for i in range(6)]
    ps_tb = E(nc.psum_tensor("ps_tb", [128, 1024], BF16))
    ps_m = E(nc.psum_tensor("ps_m", [128, 512], F32))

    g0 = lambda f: rec("gpsimd", f)
    g0(lambda e: e.memset(c_onec[:], 1.0))
    g0(lambda e: e.memset(c_onecb[:], 1.0))
    g0(lambda e: e.memset(c_oner[:], 1.0))
    g0(lambda e: e.memset(c_striln[:], -1.0))
    g0(lambda e: e.memset(c_striun[:], -1.0))
    g0(lambda e: e.memset(c_triu1[:], 1.0))
    g0(lambda e: e.memset(c_idn[:], 1.0))
    g0(lambda e: e.memset(c_idn32[:], 1.0))
    g0(lambda e: e.memset(zer128[:], 0.0))
    g0(lambda e: e.memset(c_eps[:, 0:1], 1e-6))
    g0(lambda e: e.memset(c_eps[:, 1:2], 1e-5))
    g0(lambda e: e.memset(c_eps[:, 2:3], 1.0))
    g0(lambda e: e.memset(c_eps[:, 3:4], 1e-12))
    g0(lambda e: e.memset(c_eps[:, 4:5], 0.5))
    g0(lambda e: e.drain()) if SIM_DRAINS else None
    g0(lambda e: e.affine_select(c_striln[:], c_striln[:], [[-1, 128]], ALU.is_gt,
                                 0.0, base=0, channel_multiplier=1))
    g0(lambda e: e.affine_select(c_striun[:], c_striun[:], [[1, 128]], ALU.is_gt,
                                 0.0, base=0, channel_multiplier=-1))
    g0(lambda e: e.affine_select(c_triu1[:], c_triu1[:], [[1, 128]], ALU.is_ge,
                                 0.0, base=0, channel_multiplier=-1))
    g0(lambda e: e.affine_select(c_idn[:], c_idn[:], [[-1, 128]], ALU.is_equal,
                                 0.0, base=0, channel_multiplier=1))
    g0(lambda e: e.affine_select(c_idn32[:], c_idn32[:], [[-1, 128]], ALU.is_equal,
                                 0.0, base=0, channel_multiplier=1))
    sig("gpsimd")

    for i in range(KT):
        dma("sync", x_sb[i][:], x_in[i * 128:(i + 1) * 128, :])
    dma("sync", c_pmask[:], pmask)
    dma("sync", vcol[:], vec_col)
    for t in range(64):
        dma("sync", cw_sb[:, 4 * t:4 * t + 4], conv_w[t * 128:(t + 1) * 128, :])
    barrier()
    rec("scalar", lambda e: e.activation(expA[:, 0:1], vcol[0:32, 0:1], ACT.Exp))
    rec("scalar", lambda e: e.copy(gnw0[:], vcol[:, 2:3]))
    rec("scalar", lambda e: e.copy(gnw1[:], vcol[:, 3:4]))
    sig("scalar")
    barrier()

    def emit_ln(col0, ncols):
        chunks = _cc(ncols, 512, col0)
        for (off, n) in chunks:
            for i in range(KT):
                rec("tensor", lambda e, i=i, off=off, n=n:
                    e.matmul(ps_m[0:1, 0:n], c_onec[:], x_sb[i][:, off:off + n],
                             start=(i == 0), stop=(i == KT - 1)))
            vm = sig("tensor")
            wait("vector", s_mm, vm)
            rec("vector", lambda e, off=off, n=n:
                e.tensor_copy(r_mu[:, off:off + n], ps_m[0:1, 0:n]))
            vv = sig("vector")
            wait("tensor", s_v, vv)
        for i in range(KT):
            rec("scalar", lambda e, i=i: e.activation(f_scr[:, col0:col0 + ncols],
                                                      x_sb[i][:, col0:col0 + ncols],
                                                      ACT.Square))
            vsq = sig("scalar")
            wait("tensor", s_s, vsq)
            for ci, (off, n) in enumerate(chunks):
                rec("tensor", lambda e, ci=ci, off=off, n=n, i=i:
                    e.matmul(pp[3 + ci][0:1, 0:n], c_onec[:], f_scr[:, off:off + n],
                             start=(i == 0), stop=(i == KT - 1)))
            vm = sig("tensor")
            wait("scalar", s_mm, vm)
        wait("vector", s_mm, s_mm.n)
        for ci, (off, n) in enumerate(chunks):
            rec("vector", lambda e, ci=ci, off=off, n=n:
                e.tensor_copy(r_t0[:, off:off + n], pp[3 + ci][0:1, 0:n]))
        rec("vector", lambda e: e.drain()) if VEC_DRAINS else None
        rec("vector", lambda e: e.tensor_scalar(r_mu[:, col0:col0 + ncols],
                                                r_mu[:, col0:col0 + ncols],
                                                1.0 / D, None, ALU.mult))
        rec("vector", lambda e: e.drain()) if VEC_DRAINS else None
        rec("vector", lambda e: e.tensor_tensor(r_iv[:, col0:col0 + ncols],
                                                r_mu[:, col0:col0 + ncols],
                                                r_mu[:, col0:col0 + ncols], ALU.mult))
        rec("vector", lambda e: e.drain()) if VEC_DRAINS else None
        rec("vector", lambda e: e.scalar_tensor_tensor(r_t0[:, col0:col0 + ncols],
                                                       r_t0[:, col0:col0 + ncols],
                                                       1.0 / D,
                                                       r_iv[:, col0:col0 + ncols],
                                                       ALU.mult, ALU.subtract))
        vv = sig("vector")
        wait("scalar", s_v, vv)
        rec("scalar", lambda e: e.activation(r_iv[:, col0:col0 + ncols],
                                             r_t0[:, col0:col0 + ncols],
                                             ACT.Sqrt, bias=c_eps[0:1, 1:2]))
        vs = sig("scalar")
        wait("vector", s_s, vs)
        rec("vector", lambda e: e.reciprocal(r_iv[:, col0:col0 + ncols],
                                             r_iv[:, col0:col0 + ncols]))
        vv = sig("vector")
        wait("tensor", s_v, vv)
        sig("vector")

    def emit_xln(col0, ncols):
        for (off, n) in _cc(ncols, 512, col0):
            rec("tensor", lambda e, off=off, n=n:
                e.matmul(pp[4][:, 0:n], c_oner[:], r_mu[0:1, off:off + n],
                         start=True, stop=True))
            rec("tensor", lambda e, off=off, n=n:
                e.matmul(pp[5][:, 0:n], c_oner[:], r_iv[0:1, off:off + n],
                         start=True, stop=True))
            vm = sig("tensor")
            wait("vector", s_mm, vm)
            for i in range(KT):
                rec("vector", lambda e, i=i, off=off, n=n: e.tensor_tensor(
                    f_scr[:, off:off + n], x_sb[i][:, off:off + n],
                    pp[4][:, 0:n], ALU.subtract))
                rec("vector", lambda e: e.drain()) if VEC_DRAINS else None
                rec("vector", lambda e, i=i, off=off, n=n: e.tensor_tensor(
                    xln[i][:, off:off + n], f_scr[:, off:off + n],
                    pp[5][:, 0:n], ALU.mult))
                rec("vector", lambda e: e.drain()) if VEC_DRAINS else None
            vv = sig("vector")
            wait("tensor", s_v, vv)
        return sig("vector")

    def load_wslab(w_dram, c0, ncol, buf, kt=KT, q="gpsimd", krange=None,
                   flat=None):
        if flat is None:
            flat = wb_t[0] if buf is wbuf[0] else (
                wb_t[1] if buf is wbuf[1] else None)
        if flat is not None and krange is None:
            dst = flat[:].rearrange("p (k c) -> p k c", c=256)[:, 0:kt, 0:ncol]
            srcr = w_dram[0:kt * 128, :].rearrange("(k p) c -> p k c", p=128)
            dma(q, dst, srcr[:, :, c0:c0 + ncol])
        else:
            ks = krange or range(kt)
            for k in ks:
                dma(q, buf[k][:, 0:ncol],
                    w_dram[k * 128:(k + 1) * 128, c0:c0 + ncol])
        return qfence(q)

    def proj_cols(w_dram, c0, ncols, tok_chunks, evac, kt=KT, src=None,
                  slab_of=2, evac_eng="vector", src_off=0):
        src = src or xln
        slab_of = min(slab_of, 2)
        n_of = ncols // 128
        slabs = [(s, min(slab_of, n_of - s)) for s in range(0, n_of, slab_of)]
        grp_sigs = []
        slab_mm_end = {}
        vlds = {0: load_wslab(w_dram, c0, 128 * slabs[0][1], wbuf[0], kt=kt)}
        for si, (s0, sn) in enumerate(slabs):
            b = si % 2
            if si + 1 < len(slabs):
                if si >= 1:
                    wait("gpsimd", s_mm, slab_mm_end[si - 1])
                ns0, nsn = slabs[si + 1]
                vlds[si + 1] = load_wslab(w_dram, c0 + 128 * ns0, 128 * nsn,
                                          wbuf[(si + 1) % 2], kt=kt)
            first_grp = True
            for j in range(sn):
                for (toff, tn) in tok_chunks:
                    gi = len(grp_sigs)
                    bank = pp[gi % 4]
                    if gi >= 4:
                        wait("tensor", s_v, grp_sigs[gi - 4][0])
                        wait("tensor", s_s, grp_sigs[gi - 4][1])
                    if first_grp:
                        wait("tensor", s_gd, vlds[si])
                        first_grp = False
                    for k in range(kt):
                        rec("tensor", lambda e, k=k, j=j, b=b, toff=toff + src_off,
                            tn=tn, bank=bank: e.matmul(
                                bank[:, 0:tn],
                                wbuf[b][k][:, 128 * j:128 * (j + 1)],
                                src[k][:, toff:toff + tn],
                                start=(k == 0), stop=(k == kt - 1)))
                    vm = sig("tensor")
                    wait("vector", s_mm, vm)
                    wait("scalar", s_mm, vm)
                    evac(s0 + j, bank, toff, tn)
                    grp_sigs.append((sig("vector"), sig("scalar")))
            slab_mm_end[si] = s_mm.n
        barrier(("tensor", "vector", "scalar", "gpsimd"))

    TOKS_OWN = [(HALO, 512), (HALO + 512, 512)]
    TOKS_ALL = [(0, 512), (512, 512), (1024, 4)]

    emit_ln(0, TF)
    emit_xln(0, TF)
    barrier()

    t_xln = tap("xln", [D, TF])
    if t_xln is not None:
        for i in range(KT):
            rec("scalar", lambda e, i=i: e.copy(f_scr[:], xln[i][:]))
            v = sig("scalar")
            wait("sync", s_s, v)
            dma("sync", t_xln[i * 128:(i + 1) * 128, :], f_scr[:], sem=s_st)
            wait("scalar", s_st, s_st.n)
        barrier()

    msel = contextlib.ExitStack()
    hsel0 = sbh(msel, "sb_hsel0", 32, 32 * 128)
    dma("sync", hsel0[:], hsel0_in)
    l0sc = contextlib.ExitStack()
    ab_fm = sbh(l0sc, "ab_fm", 64, TC)
    g_fm = sbh(l0sc, "g_fm", 32, TC)
    beta0 = sbh(l0sc, "beta0", 32, TC)
    bcum = sbh(l0sc, "bcum", 32, TC)
    lam = sbh(l0sc, "lam", 32, TC)
    trow = sbh(l0sc, "trow", 1, 32)
    bT = [sbh(l0sc, f"bT{i}", 128, 32) for i in range(NCH)]
    betT = [sbh(l0sc, f"betT{i}", 128, 32) for i in range(NCH)]
    gamT = [sbh(l0sc, f"gamT{i}", 128, 32) for i in range(NCH)]
    lamCr = [sbh(l0sc, f"lamC{i}", 128, 32) for i in range(NCH)]

    for k in range(KT):
        dma("gpsimd", wbuf[0][k][:, 0:64], w_ab[k * 128:(k + 1) * 128, :])
    vld = qfence("gpsimd")
    for (toff, tn) in TOKS_OWN:
        wait("tensor", s_gd, vld)
        for k in range(KT):
            rec("tensor", lambda e, k=k, toff=toff, tn=tn:
                e.matmul(pp[0][0:64, 0:tn], wbuf[0][k][:, 0:64],
                         xln[k][:, toff:toff + tn],
                         start=(k == 0), stop=(k == KT - 1)))
        vm = sig("tensor")
        wait("vector", s_mm, vm)
        rec("vector", lambda e, toff=toff, tn=tn:
            e.tensor_copy(ab_fm[:, toff - HALO:toff - HALO + tn], pp[0][0:64, 0:tn]))
        vv = sig("vector")
        wait("tensor", s_v, vv)
    barrier()

    rec("scalar", lambda e: e.activation(g_fm[:], ab_fm[0:32, :], ACT.Abs,
                                         bias=vcol[0:32, 1:2]))
    rec("scalar", lambda e: e.drain()) if SIM_DRAINS else None
    rec("scalar", lambda e: e.activation(g_fm[:], g_fm[:], ACT.Exp, scale=-1.0))
    rec("scalar", lambda e: e.drain()) if SIM_DRAINS else None
    rec("scalar", lambda e: e.activation(g_fm[:], g_fm[:], ACT.Ln,
                                         bias=c_eps[0:32, 2:3]))
    rec("scalar", lambda e: e.drain()) if SIM_DRAINS else None
    rec("scalar", lambda e: e.activation(f_scr[0:32, 0:TC], ab_fm[0:32, :],
                                         ACT.Relu, bias=vcol[0:32, 1:2]))
    rec("scalar", lambda e: e.drain()) if SIM_DRAINS else None
    rec("scalar", lambda e: e.activation(beta0[:], ab_fm[32:64, :], ACT.Sigmoid))
    vs = sig("scalar")
    wait("vector", s_s, vs)
    rec("vector", lambda e: e.tensor_add(g_fm[:], g_fm[:], f_scr[0:32, 0:TC]))
    rec("vector", lambda e: e.drain()) if VEC_DRAINS else None
    rec("vector", lambda e: e.tensor_scalar(g_fm[:], g_fm[:], expA[0:32, 0:1],
                                            -1.0, ALU.mult, ALU.mult))
    rec("vector", lambda e: e.drain()) if VEC_DRAINS else None
    for i in range(NCH):
        rec("vector", lambda e, i=i: e.tensor_tensor_scan(
            bcum[:, i * CH:(i + 1) * CH], g_fm[:, i * CH:(i + 1) * CH],
            zer128[0:32, :], 0.0, ALU.add, ALU.add))
    rec("vector", lambda e: e.drain()) if VEC_DRAINS else None
    vv = sig("vector")
    wait("scalar", s_v, vv)
    rec("scalar", lambda e: e.activation(lam[:], bcum[:], ACT.Exp))
    sig("scalar")

    for nm, t in (("g_fm", g_fm), ("beta0", beta0), ("bcum", bcum)):
        tp = tap(nm, [32, TC])
        if tp is not None:
            wait("sync", s_s, s_s.n)
            wait("sync", s_v, s_v.n)
            dma("sync", tp, t[:], sem=s_st)

    barrier()

    for i in range(NCH):
        rec("tensor", lambda e, i=i: e.transpose(ps_m[:, 0:32],
                                                 bcum[:, i * CH:(i + 1) * CH],
                                                 c_idn32[0:32, 0:32]))
        vm = sig("tensor")
        wait("vector", s_mm, vm)
        rec("vector", lambda e, i=i: e.tensor_copy(bT[i][:], ps_m[:, 0:32]))
        vv = sig("vector")
        wait("tensor", s_v, vv)
        rec("tensor", lambda e, i=i: e.transpose(ps_m[:, 0:32],
                                                 beta0[:, i * CH:(i + 1) * CH],
                                                 c_idn32[0:32, 0:32]))
        vm = sig("tensor")
        wait("vector", s_mm, vm)
        rec("vector", lambda e, i=i: e.tensor_copy(betT[i][:], ps_m[:, 0:32]))
        vv = sig("vector")
        wait("tensor", s_v, vv)
        rec("tensor", lambda e, i=i: e.matmul(ps_m[0:1, 64:96],
                                              bcum[:, (i + 1) * CH - 1:(i + 1) * CH],
                                              c_idn32[0:32, 0:32],
                                              start=True, stop=True))
        vm = sig("tensor")
        wait("vector", s_mm, vm)
        rec("vector", lambda e: e.tensor_copy(trow[:], ps_m[0:1, 64:96]))
        vv = sig("vector")
        wait("tensor", s_v, vv)
        rec("tensor", lambda e, i=i: e.matmul(ps_m[:, 32:64], c_oner[:],
                                              trow[:], start=True, stop=True))
        vm = sig("tensor")
        wait("vector", s_mm, vm)
        rec("vector", lambda e, i=i: e.tensor_tensor(gamT[i][:], ps_m[:, 32:64],
                                                     bT[i][:], ALU.subtract))
        vv = sig("vector")
        wait("scalar", s_v, vv)
        rec("scalar", lambda e, i=i: e.activation(gamT[i][:], gamT[i][:], ACT.Exp))
        rec("scalar", lambda e, i=i: e.activation(lamCr[i][:], ps_m[:, 32:64],
                                                  ACT.Exp))
        vs = sig("scalar")
        wait("tensor", s_s, vs)
        wait("vector", s_s, vs)
    barrier()

    for nm in ("bT3", "gamT3", "lamC3", "betT3"):
        tp = tap(nm, [128, 32])
        if tp is not None:
            src = {"bT3": bT, "gamT3": gamT, "lamC3": lamCr, "betT3": betT}[nm][3]
            dma("sync", tp, src[:], sem=s_st)
    barrier()

    lb0 = g_fm
    rec("vector", lambda e: e.tensor_tensor(lb0[:], lam[:], beta0[:], ALU.mult))
    sig("vector")
    barrier(("vector", "tensor", "scalar"))

    def emit_mixer(layer, gates):
        gated = (layer == 0)
        HV = L0_HV if gated else L1_H
        NG = HV // 2
        msc = contextlib.ExitStack()
        hsel = gates["hsel"]
        HSP = 32 if gated else 16
        L = f"m{layer}_"
        xsq_b = sbh(msc, L + "xsq_b", 128, TC, BF16)
        nheads_k = 1 if gated else 2
        cvi = [sbh(msc, f"{L}cvi{t}", 128, TF, BF16) for t in range(4)]
        qhat = [sbh(msc, f"{L}qhat{j}", 128, TC, BF16) for j in range(nheads_k)]
        khat = [sbh(msc, f"{L}khat{j}", 128, TC, BF16) for j in range(nheads_k)]
        vtm = ([] if gated else
               [sbh(msc, f"{L}vtm{j}", 128, TC, BF16) for j in range(2)])
        st_v = sbh(msc, L + "st_v", 128, 256, BF16)
        scPU = [sbh(msc, f"{L}scPU_{k}", 128, 512, BF16) for k in range(6)]
        scPU.append(sbh(msc, f"{L}scPU_6", 128, 256, BF16))
        scP2 = [scPU[k][:, 0:256] for k in range(7)]
        scU_of = [scPU[k][:, 256:512] for k in range(6)]
        scY2 = sbh(msc, L + "scY2", 128, 512, BF16)
        scN = [scU_of[0][:, 0:128], scU_of[0][:, 128:256]]
        scNT = [scP2[0][:, 0:128], scP2[0][:, 128:256]]
        scY = [scY2[:, 0:256], scY2[:, 256:512]]
        scPT = [sbh(msc, f"{L}scPT{h}", 128, 128, BF16) for h in range(2)]
        scKb = [sbh(msc, f"{L}scKb{h}", 128, 128, BF16) for h in range(2)]
        scQb = [sbh(msc, f"{L}scQb{h}", 128, 128, BF16) for h in range(2)]
        scKt = [sbh(msc, f"{L}scKt{h}", 128, 128, BF16) for h in range(2)]
        Sa2 = sbh(msc, L + "Sa2", 128, 512)
        Sab2 = sbh(msc, L + "Sab2", 128, 512, BF16)
        Sa = [Sa2[:, 0:256], Sa2[:, 256:512]]
        Sab = [Sab2[:, 0:256], Sab2[:, 256:512]]
        sc_m2 = sbh(msc, L + "scm2", 128, 256)
        sc_t2 = sbh(msc, L + "sct2", 128, 256)
        sc_n2 = sbh(msc, L + "scn2", 128, 256)
        sc_e1 = sbh(msc, L + "sce1", 128, 256)
        sc_e2 = sbh(msc, L + "sce2", 128, 256)
        st_o = sbh(msc, L + "st_o", 128, 256, BF16)
        st_c = sbh(msc, L + "st_c", 128, 256, BF16)
        st_p = sbh(msc, L + "st_p", 128, 128)

        w_qkv_l = w_qkv if gated else None

        def tok_all():
            return TOKS_ALL if gated else TOKS_OWN

        for g in range(NG):
            def mk_evac(tile_idx, silu=False, colbase=0):
                def evac(jg, bank, toff, tn):
                    dst = cvi[tile_idx + jg - colbase]
                    if silu:
                        rec("scalar", lambda e, dst=dst, toff=toff, tn=tn, bank=bank:
                            e.activation(dst[:, toff:toff + tn], bank[:, 0:tn],
                                         ACT.Silu))
                    else:
                        rec("vector", lambda e, dst=dst, toff=toff, tn=tn, bank=bank:
                            e.tensor_copy(dst[:, toff:toff + tn], bank[:, 0:tn]))
                return evac

            if gated:
                proj_cols(w_qkv, 128 * g, 128, tok_all(), mk_evac(0), slab_of=1)
                proj_cols(w_qkv, 2048 + 128 * g, 128, tok_all(), mk_evac(1, colbase=0),
                          slab_of=1)
                proj_cols(w_qkv, 4096 + 256 * g, 256, tok_all(), mk_evac(2, colbase=0),
                          slab_of=2)
            else:
                proj_cols(w_q1, 256 * g, 256, tok_all(), mk_evac(0, colbase=0),
                          slab_of=2)
                proj_cols(w_k1, 256 * g, 256, tok_all(), mk_evac(2, colbase=0),
                          slab_of=2)

            barrier(("tensor", "vector", "scalar", "gpsimd"))

            if gated:
                gts = [g, 16 + g, 32 + 2 * g, 33 + 2 * g]
                for t in range(4):
                    ct = gts[t]
                    rec("vector", lambda e, t=t, ct=ct: e.tensor_scalar(
                        f_scr[:, 0:TC], cvi[t][:, 1:1 + TC],
                        cw_sb[:, 4 * ct:4 * ct + 1], None, ALU.mult))
                    rec("vector", lambda e: e.drain()) if VEC_DRAINS else None
                    for i_ in range(1, KW):
                        rec("vector", lambda e, t=t, ct=ct, i_=i_:
                            e.scalar_tensor_tensor(f_scr[:, 0:TC],
                                                   cvi[t][:, 1 + i_:1 + i_ + TC],
                                                   cw_sb[:, 4 * ct + i_:4 * ct + i_ + 1],
                                                   f_scr[:, 0:TC], ALU.mult, ALU.add))
                        rec("vector", lambda e: e.drain()) if VEC_DRAINS else None
                    vv = sig("vector")
                    wait("scalar", s_v, vv)
                    rec("scalar", lambda e, t=t: e.activation(cvi[t][:, HALO:TF],
                                                              f_scr[:, 0:TC],
                                                              ACT.Silu))
                    vs = sig("scalar")
                    wait("vector", s_s, vs)
                barrier(("vector", "scalar", "tensor"))

            for j in range(nheads_k):
                qt = cvi[0 if gated else j]
                kt_ = cvi[1 if gated else 2 + j]
                for (which, tile, qscale) in (("q", qt, True), ("k", kt_, False)):
                    rec("scalar", lambda e, tile=tile: e.activation(
                        xsq_b[:, 0:TC], tile[:, HALO:TF], ACT.Square))
                    vs = sig("scalar")
                    wait("tensor", s_s, vs)
                    for (o_, n_) in _cc(TC, 512):
                        rec("tensor", lambda e, o_=o_, n_=n_: e.matmul(
                            ps_m[0:1, 0:n_], c_onecb[:], xsq_b[:, o_:o_ + n_],
                            start=True, stop=True))
                        vm = sig("tensor")
                        wait("vector", s_mm, vm)
                        rec("vector", lambda e, o_=o_, n_=n_: e.tensor_copy(
                            r_mu[0:1, o_:o_ + n_], ps_m[0:1, 0:n_]))
                        vv = sig("vector")
                        wait("tensor", s_v, vv)
                    wait("scalar", s_v, s_v.n)
                    if gated:
                        rec("scalar", lambda e: e.activation(
                            r_t0[0:1, 0:TC], r_mu[0:1, 0:TC], ACT.Sqrt,
                            bias=c_eps[0:1, 0:1]))
                        vs = sig("scalar")
                        wait("vector", s_s, vs)
                        rec("vector", lambda e: e.reciprocal(
                            r_t0[0:1, 0:TC], r_t0[0:1, 0:TC]))
                    else:
                        rec("scalar", lambda e: e.activation(
                            r_t0[0:1, 0:TC], r_mu[0:1, 0:TC], ACT.Sqrt))
                        vs = sig("scalar")
                        wait("vector", s_s, vs)
                        rec("vector", lambda e: e.tensor_scalar(
                            r_t0[0:1, 0:TC], r_t0[0:1, 0:TC], 1e-12, None,
                            ALU.max))
                        rec("vector", lambda e: e.drain()) if VEC_DRAINS else None
                        rec("vector", lambda e: e.reciprocal(
                            r_t0[0:1, 0:TC], r_t0[0:1, 0:TC]))
                    rec("vector", lambda e: e.drain()) if VEC_DRAINS else None
                    if qscale:
                        rec("vector", lambda e: e.tensor_scalar(
                            r_t0[0:1, 0:TC], r_t0[0:1, 0:TC],
                            float(DK) ** -0.5, None, ALU.mult))
                        rec("vector", lambda e: e.drain()) if VEC_DRAINS else None
                    vv = sig("vector")
                    wait("tensor", s_v, vv)
                    dst = (qhat if which == "q" else khat)[j]
                    for (o_, n_) in _cc(TC, 512):
                        rec("tensor", lambda e, o_=o_, n_=n_: e.matmul(
                            pp[0][:, 0:n_], c_oner[:], r_t0[0:1, o_:o_ + n_],
                            start=True, stop=True))
                        vm = sig("tensor")
                        wait("vector", s_mm, vm)
                        rec("vector", lambda e, o_=o_, n_=n_, tile=tile, dst=dst:
                            e.tensor_tensor(dst[:, o_:o_ + n_],
                                            tile[:, HALO + o_:HALO + o_ + n_],
                                            pp[0][:, 0:n_], ALU.mult))
                        vv = sig("vector")
                        wait("tensor", s_v, vv)
            barrier(("tensor", "vector", "scalar"))

            if not gated:
                for h in range(2):
                    vc0 = 128 * (2 * g + h)
                    if h > 0:
                        wait("gpsimd", s_mm, s_mm.n)
                    vldv = load_wslab(w_v1, vc0, 128, wbuf[0], kt=KT)
                    for tt in range(NCH):
                        wait("tensor", s_gd, vldv)
                        for k in range(KT):
                            rec("tensor", lambda e, k=k, tt=tt: e.matmul(
                                pp[1][:, 0:128], xln[k][:, HALO + tt * CH:HALO + (tt + 1) * CH],
                                wbuf[0][k][:, 0:128],
                                start=(k == 0), stop=(k == KT - 1)))
                        vm = sig("tensor")
                        wait("scalar", s_mm, vm)
                        rec("scalar", lambda e, h=h, tt=tt: e.activation(
                            vtm[h][:, tt * CH:(tt + 1) * CH], pp[1][:, 0:128],
                            ACT.Silu))
                        vs = sig("scalar")
                        wait("tensor", s_s, vs)
            barrier(("tensor", "vector", "scalar"))

            if g == 0:
                for nm_, src_ in ((f"qhat_l{layer}", qhat[0]),
                                  (f"khat_l{layer}", khat[0]),
                                  (f"v0_l{layer}", cvi[2] if gated else vtm[0]),
                                  (f"v1_l{layer}", cvi[3] if gated else vtm[1])):
                    tp_ = tap(nm_, [128, src_.shape[1] if hasattr(src_, "shape")
                                    else TF], BF16)
                    if tp_ is not None:
                        wait("sync", s_v, s_v.n)
                        wait("sync", s_s, s_s.n)
                        dma("sync", tp_, src_[:], sem=s_st)
                barrier()

            for h in range(2):
                rec("vector", lambda e, h=h: e.memset(Sa2[:, 256 * h:256 * h + 128], 0.0))
                rec("vector", lambda e, h=h: e.memset(Sab2[:, 256 * h:256 * h + 128], 0.0))
                rec("vector", lambda e, h=h: e.tensor_copy(Sa2[:, 256 * h + 128:256 * (h + 1)],
                                                           c_idn32[:]))
                rec("vector", lambda e, h=h: e.tensor_copy(Sab2[:, 256 * h + 128:256 * (h + 1)],
                                                           c_idn[:]))
            rec("vector", lambda e: e.drain()) if VEC_DRAINS else None
            sig("vector")
            barrier(("tensor", "vector", "scalar"))

            for i in range(NCH):
                hv0 = 2 * g
                tc0 = i * CH
                for j in range(nheads_k):
                    rec("tensor", lambda e, j=j, tc0=tc0: e.matmul(
                        pp[0][:, 256 * j:256 * j + 128], khat[j][:, tc0:tc0 + CH],
                        khat[j][:, tc0:tc0 + CH], start=True, stop=True))
                    rec("tensor", lambda e, j=j, tc0=tc0: e.matmul(
                        pp[0][:, 256 * j + 128:256 * j + 256], khat[j][:, tc0:tc0 + CH],
                        qhat[j][:, tc0:tc0 + CH], start=True, stop=True))
                    rec("tensor", lambda e, j=j, tc0=tc0: e.transpose(
                        ps_tb[:, 128 * j:128 * (j + 1)], khat[j][:, tc0:tc0 + CH],
                        c_idn[:]))
                if gated:
                    for h in range(2):
                        rec("tensor", lambda e, h=h, tc0=tc0: e.transpose(
                            ps_tb[:, 256 + 128 * h:256 + 128 * (h + 1)],
                            cvi[2 + h][:, HALO + tc0:HALO + tc0 + CH], c_idn[:]))
                vmA = sig("tensor")
                wait("vector", s_mm, vmA)
                if gated:
                    rec("vector", lambda e: e.tensor_copy(st_v[:], ps_tb[:, 256:512]))
                    rec("vector", lambda e: e.drain()) if VEC_DRAINS else None
                vvV = sig("vector")
                wait("tensor", s_v, vvV)

                for h in range(2):
                    hh = hv0 + h
                    j = 0 if gated else h
                    psA = pp[0][:, 256 * j:256 * j + 128]
                    psKQ = pp[0][:, 256 * j + 128:256 * j + 256]
                    if gated:
                        rec("tensor", lambda e, hh=hh, tc0=tc0, h=h: e.matmul(
                            pp[1][:, 128 * h:128 * h + 128],
                            hsel[0:HSP, 128 * hh:128 * (hh + 1)],
                            gates["bcum"][:, tc0:tc0 + CH], start=True, stop=True))
                        rec("tensor", lambda e, hh=hh, tc0=tc0, h=h: e.matmul(
                            pp[1][:, 256 + 128 * h:256 + 128 * h + 128],
                            hsel[0:HSP, 128 * hh:128 * (hh + 1)],
                            gates["beta"][:, tc0:tc0 + CH], start=True, stop=True))
                        rec("tensor", lambda e, hh=hh, tc0=tc0, h=h: e.matmul(
                            pp[2][:, 128 * h:128 * h + 128],
                            hsel[0:HSP, 128 * hh:128 * (hh + 1)],
                            gates["lb"][:, tc0:tc0 + CH], start=True, stop=True))
                        rec("tensor", lambda e, hh=hh, tc0=tc0, h=h: e.matmul(
                            pp[2][:, 256 + 128 * h:256 + 128 * h + 128],
                            hsel[0:HSP, 128 * hh:128 * (hh + 1)],
                            gates["lam"][:, tc0:tc0 + CH], start=True, stop=True))
                    else:
                        rec("tensor", lambda e, hh=hh, tc0=tc0, h=h: e.matmul(
                            pp[1][:, 256 + 128 * h:256 + 128 * h + 128],
                            hsel[0:HSP, 128 * hh:128 * (hh + 1)],
                            gates["beta"][:, tc0:tc0 + CH], start=True, stop=True))
                vmR = sig("tensor")
                wait("vector", s_mm, vmR)
                wait("scalar", s_mm, vmR)

                D2 = lambda: (rec("vector", lambda e: e.drain())
                              if VEC_DRAINS else None)
                psA_ = [pp[0][:, 256 * (0 if gated else h_):
                              256 * (0 if gated else h_) + 128] for h_ in range(2)]
                psKQ_ = [pp[0][:, 256 * (0 if gated else h_) + 128:
                               256 * (0 if gated else h_) + 256] for h_ in range(2)]
                if gated:
                    for h in range(2):
                        rec("vector", lambda e, h=h,
                            bTc=gates["bT"][i][:, (hv0 + h):(hv0 + h) + 1]:
                            e.scalar_tensor_tensor(
                                sc_m2[:, 128 * h:128 * (h + 1)],
                                pp[1][:, 128 * h:128 * h + 128], bTc,
                                zer128[:], ALU.subtract, ALU.subtract))
                    D2()
                    rec("vector", lambda e: e.tensor_scalar(
                        sc_t2[:], sc_m2[:], 0.0, None, ALU.max))
                    rec("vector", lambda e: e.tensor_scalar(
                        sc_n2[:], sc_m2[:], 0.0, None, ALU.min))
                    D2()
                    vv = sig("vector")
                    wait("scalar", s_v, vv)
                    rec("scalar", lambda e: e.activation(sc_e1[:], sc_t2[:],
                                                         ACT.Exp, scale=-1.0))
                    rec("scalar", lambda e: e.drain()) if SIM_DRAINS else None
                    rec("scalar", lambda e: e.activation(sc_e2[:], sc_n2[:],
                                                         ACT.Exp))
                    vs = sig("scalar")
                    wait("vector", s_s, vs)
                    for h in range(2):
                        rec("vector", lambda e, h=h: e.tensor_tensor(
                            sc_t2[:, 128 * h:128 * (h + 1)],
                            sc_e1[:, 128 * h:128 * (h + 1)], psA_[h], ALU.mult))
                    for h in range(2):
                        rec("vector", lambda e, h=h: e.tensor_tensor(
                            sc_n2[:, 128 * h:128 * (h + 1)],
                            sc_e2[:, 128 * h:128 * (h + 1)], psA_[h], ALU.mult))
                    D2()
                    for h in range(2):
                        rec("vector", lambda e, h=h,
                            beTc=gates["betT"][i][:, (hv0 + h):(hv0 + h) + 1]:
                            e.scalar_tensor_tensor(
                                scN[h], sc_t2[:, 128 * h:128 * (h + 1)], beTc,
                                c_striln[:], ALU.mult, ALU.mult))
                    for h in range(2):
                        rec("vector", lambda e, h=h: e.tensor_tensor(
                            sc_n2[:, 128 * h:128 * (h + 1)],
                            sc_n2[:, 128 * h:128 * (h + 1)],
                            pp[1][:, 256 + 128 * h:256 + 128 * h + 128], ALU.mult))
                    D2()
                    for h in range(2):
                        rec("vector", lambda e, h=h: e.tensor_tensor(
                            scNT[h], sc_n2[:, 128 * h:128 * (h + 1)],
                            c_striun[:], ALU.mult))
                    for h in range(2):
                        rec("vector", lambda e, h=h: e.tensor_tensor(
                            sc_t2[:, 128 * h:128 * (h + 1)],
                            sc_e2[:, 128 * h:128 * (h + 1)], psKQ_[h], ALU.mult))
                    D2()
                    for h in range(2):
                        rec("vector", lambda e, h=h: e.tensor_tensor(
                            scPT[h][:], sc_t2[:, 128 * h:128 * (h + 1)],
                            c_triu1[:], ALU.mult))
                else:
                    for h in range(2):
                        rec("vector", lambda e, h=h,
                            beTc=gates["betT"][i][:, (hv0 + h):(hv0 + h) + 1]:
                            e.scalar_tensor_tensor(scN[h], psA_[h], beTc,
                                                   c_striln[:], ALU.mult, ALU.mult))
                    for h in range(2):
                        rec("vector", lambda e, h=h: e.tensor_tensor(
                            sc_n2[:, 128 * h:128 * (h + 1)], psA_[h],
                            c_striun[:], ALU.mult))
                    D2()
                    for h in range(2):
                        rec("vector", lambda e, h=h: e.tensor_tensor(
                            scNT[h], sc_n2[:, 128 * h:128 * (h + 1)],
                            pp[1][:, 256 + 128 * h:256 + 128 * h + 128], ALU.mult))
                    for h in range(2):
                        rec("vector", lambda e, h=h: e.tensor_tensor(
                            scPT[h][:], psKQ_[h], c_triu1[:], ALU.mult))
                for h in range(2):
                    src_rep = (pp[2][:, 128 * h:128 * h + 128] if gated
                               else pp[1][:, 256 + 128 * h:256 + 128 * h + 128])
                    kj = khat[0 if gated else h]
                    rec("vector", lambda e, h=h, kj=kj, tc0=tc0, src_rep=src_rep:
                        e.tensor_tensor(scKb[h][:], kj[:, tc0:tc0 + CH], src_rep,
                                        ALU.mult))
                D2()
                for h in range(2):
                    jk = 0 if gated else h
                    if gated:
                        rec("vector", lambda e, h=h, jk=jk, hh=hv0 + h, i=i:
                            e.tensor_scalar(scKt[h][:],
                                            ps_tb[:, 128 * jk:128 * (jk + 1)],
                                            gates["gamT"][i][:, hh:hh + 1], None,
                                            ALU.mult))
                    else:
                        rec("vector", lambda e, h=h, jk=jk:
                            e.tensor_copy(scKt[h][:],
                                          ps_tb[:, 128 * jk:128 * (jk + 1)]))
                if not gated:
                    D2()
                if gated:
                    for h in range(2):
                        rec("vector", lambda e, h=h, qj=qhat[0], tc0=tc0:
                            e.tensor_tensor(scQb[h][:], qj[:, tc0:tc0 + CH],
                                            pp[2][:, 256 + 128 * h:256 + 128 * h + 128],
                                            ALU.mult))
                    D2()
                vv = sig("vector")
                wait("tensor", s_v, vv)

                for h in range(2):
                    rec("tensor", lambda e, h=h: e.matmul(
                        pp[4][:, 256 * h:256 * h + 256], scKb[h][:], Sab[h],
                        start=True, stop=True))
                vmRHS = sig("tensor")
                wait("vector", s_mm, vmRHS)
                for h in range(2):
                    vsrc = (st_v[:, 128 * h:128 * (h + 1)] if gated
                            else vtm[h][:, tc0:tc0 + CH])
                    rec("vector", lambda e, h=h, vsrc=vsrc,
                        bsl=gates["betT"][i][:, (2 * g + h):(2 * g + h) + 1]:
                        e.scalar_tensor_tensor(
                            scY2[:, 256 * h:256 * h + 128], vsrc, bsl,
                            pp[4][:, 256 * h:256 * h + 128], ALU.mult,
                            ALU.subtract))
                    rec("vector", lambda e, h=h: e.tensor_scalar(
                        scY2[:, 256 * h + 128:256 * (h + 1)], pp[4][:, 256 * h + 128:256 * h + 256],
                        -1.0, None, ALU.mult))
                rec("vector", lambda e: e.drain()) if VEC_DRAINS else None
                vv = sig("vector")
                wait("tensor", s_v, vv)

                if g == 0 and i == 0:
                    for nm_, src_ in ((f"scR_l{layer}", scY[0]),
                                      (f"stv_l{layer}", st_v)):
                        tp_ = tap(nm_, [128, 256], BF16)
                        if tp_ is not None:
                            wait("sync", s_v, s_v.n)
                            dma("sync", tp_, src_[:], sem=s_st)
                    if tapd_any[0]:
                        wait("sync", s_st, s_st.n)
                        barrier(("tensor", "vector", "scalar", "sync"))

                for k_ in range(6):
                    for h in range(2):
                        rec("tensor", lambda e, h=h, k_=k_: e.matmul(
                            pp[2][:, 128 * h:128 * h + 128],
                            scU_of[k_][:, 128 * h:128 * h + 128],
                            scP2[k_][:, 128 * h:128 * h + 128],
                            start=True, stop=True))
                        if k_ < 5:
                            rec("tensor", lambda e, h=h, k_=k_: e.matmul(
                                pp[2][:, 256 + 128 * h:256 + 128 * h + 128],
                                scP2[k_][:, 128 * h:128 * h + 128],
                                scU_of[k_][:, 128 * h:128 * h + 128],
                                start=True, stop=True))
                    vm = sig("tensor")
                    wait("vector", s_mm, vm)
                    rec("vector", lambda e, k_=k_: e.tensor_copy(
                        scPU[k_ + 1][:, 0:(512 if k_ < 5 else 256)],
                        pp[2][:, 0:(512 if k_ < 5 else 256)]))
                    rec("vector", lambda e: e.drain()) if VEC_DRAINS else None
                    vv = sig("vector")
                    wait("tensor", s_v, vv)
                for k_ in range(7):
                    for h in range(2):
                        rec("tensor", lambda e, h=h, k_=k_: e.matmul(
                            pp[3][:, 256 * h:256 * h + 256],
                            scP2[k_][:, 128 * h:128 * h + 128],
                            scY[h], start=True, stop=True))
                    vm = sig("tensor")
                    wait("vector", s_mm, vm)
                    rec("vector", lambda e: e.tensor_tensor(
                        scY2[:], scY2[:], pp[3][:, 0:512], ALU.add))
                    rec("vector", lambda e: e.drain()) if VEC_DRAINS else None
                    vv = sig("vector")
                    wait("tensor", s_v, vv)

                if g == 0 and i == 0:
                    for nm_, src_ in ((f"scN_l{layer}", scN[0]),
                                      (f"scNT_l{layer}", scNT[0]),
                                      (f"scPT_l{layer}", scPT[0]),
                                      (f"scY_l{layer}", scY[0]),
                                      (f"scKb_l{layer}", scKb[0]),
                                      (f"scKt_l{layer}", scKt[0]),
                                      (f"scQb_l{layer}", scQb[0])):
                        tp_ = tap(nm_, [128, 256 if src_ is scY[0] else 128], BF16)
                        if tp_ is not None:
                            wait("sync", s_v, s_v.n)
                            dma("sync", tp_, src_[:], sem=s_st)
                    if tapd_any[0]:
                        wait("sync", s_st, s_st.n)
                        barrier(("tensor", "vector", "scalar", "sync"))

                for h in range(2):
                    qb = (scQb[h][:] if gated
                          else qhat[h][:, tc0:tc0 + CH])
                    rec("tensor", lambda e, h=h, qb=qb: e.matmul(
                        ps_m[:, 128 * h:128 * h + 128],
                        Sab2[:, 256 * h:256 * h + 128], qb,
                        start=True, stop=False))
                    rec("tensor", lambda e, h=h: e.matmul(
                        ps_m[:, 128 * h:128 * h + 128],
                        scY2[:, 256 * h:256 * h + 128], scPT[h][:],
                        start=False, stop=True))
                    rec("tensor", lambda e, h=h, qb=qb: e.matmul(
                        ps_m[:, 256 + 128 * h:256 + 128 * h + 128],
                        Sab2[:, 256 * h + 128:256 * (h + 1)],
                        qb, start=True, stop=False))
                    rec("tensor", lambda e, h=h: e.matmul(
                        ps_m[:, 256 + 128 * h:256 + 128 * h + 128],
                        scY2[:, 256 * h + 128:256 * (h + 1)],
                        scPT[h][:], start=False, stop=True))
                    rec("tensor", lambda e, h=h: e.matmul(
                        pp[5][:, 256 * h:256 * h + 256], scKt[h][:], scY[h],
                        start=True, stop=True))
                vmO = sig("tensor")
                wait("vector", s_mm, vmO)
                rec("vector", lambda e: e.tensor_copy(st_o[:], ps_m[:, 0:256]))
                rec("vector", lambda e: e.tensor_copy(st_c[:], ps_m[:, 256:512]))
                if gated:
                    for h in range(2):
                        rec("vector", lambda e, h=h, i=i, hh=2 * g + h:
                            e.scalar_tensor_tensor(Sa[h], Sa[h],
                                                   gates["lamCr"][i][:, hh:hh + 1],
                                                   pp[5][:, 256 * h:256 * h + 256],
                                                   ALU.mult, ALU.add))
                else:
                    rec("vector", lambda e: e.tensor_tensor(
                        Sa2[:], Sa2[:], pp[5][:, 0:512], ALU.add))
                rec("vector", lambda e: e.drain()) if VEC_DRAINS else None
                rec("vector", lambda e: e.tensor_copy(Sab2[:], Sa2[:]))
                rec("vector", lambda e: e.drain()) if VEC_DRAINS else None
                vv = sig("vector")
                wait("sync", s_v, vv)
                for h in range(2):
                    hh = 2 * g + h
                    dma("sync", o0_dram[layer].ap()[hh, :, tc0:tc0 + CH],
                        st_o[:, 128 * h:128 * h + 128], sem=s_st)
                    dma("sync", cc_dram[layer].ap()[hh, i, :, :],
                        st_c[:, 128 * h:128 * h + 128], sem=s_st)
                wait("tensor", s_st, s_st.n)
                wait("vector", s_st, s_st.n)

            for h in range(2):
                hh = 2 * g + h
                rec("tensor", lambda e, h=h: e.transpose(
                    ps_m[:, 0:128], Sa2[:, 256 * h + 128:256 * (h + 1)], c_idn32[:]))
                vm = sig("tensor")
                wait("vector", s_mm, vm)
                rec("vector", lambda e: e.tensor_copy(st_p[:], ps_m[:, 0:128]))
                vv = sig("vector")
                wait("sync", s_v, vv)
                wait("tensor", s_v, vv)
                dma("sync", agg_in[layer].ap()[:, 384 * hh:384 * hh + 256],
                    Sa[h], sem=s_st)
                dma("sync", agg_in[layer].ap()[:, 384 * hh + 256:384 * (hh + 1)],
                    st_p[:], sem=s_st)
                wait("vector", s_st, s_st.n)
            barrier(("tensor", "vector", "scalar", "gpsimd", "sync"))

        msc.close()
        return None


    def emit_scan_out(layer):
        gated = (layer == 0)
        HV = L0_HV if gated else L1_H
        eps_sl = c_eps[0:1, 0:1] if gated else c_eps[0:1, 1:2]
        gnw = gnw0 if gated else gnw1
        wout = w_out0 if gated else w_o1
        ssc = contextlib.ExitStack()
        L = f"s{layer}_"
        Tbuf = sbh(ssc, L + "Tbuf", 128, HV * 128, BF16)
        Jsl = sbh(ssc, L + "Jsl", 128, 512, BF16)
        Psl = sbh(ssc, L + "Psl", 128, 512, BF16)
        bl1 = sbh(ssc, L + "bl1", 128, 512)
        bl2 = sbh(ssc, L + "bl2", 128, 512)

        wait("gpsimd", s_st, s_st.n)
        vcc = s_cc.bump(1)
        rec("gpsimd", lambda e: e.collective_compute(
            "AllGather", ALU.bypass, replica_groups=[list(range(NCORES))],
            ins=[agg_in[layer].ap().opt()], outs=[agg_out[layer].ap().opt()])
            .then_inc(s_cc.h, 1), can_inc=False)
        wait("gpsimd", s_cc, vcc)
        sig("gpsimd")
        barrier()

        rec("vector", lambda e: e.memset(Tbuf[:], 0.0))
        sig("vector")
        barrier(("vector", "tensor", "gpsimd"))
        ag3 = agg_out[layer].ap().rearrange("r a (h c) -> r a h c", c=384)
        for p_ in range(NCORES - 1):
            for g4 in range(HV // 4):
                vj = dma("gpsimd", Jsl[:].rearrange("a (h c) -> a h c", c=128),
                         ag3[p_, :, 4 * g4:4 * g4 + 4, 0:128])
                vph = dma("gpsimd", Psl[:].rearrange("a (h c) -> a h c", c=128),
                          ag3[p_, :, 4 * g4:4 * g4 + 4, 256:384])
                wait("tensor", s_gd, vph)
                for r in range(4):
                    hh = 4 * g4 + r
                    rec("tensor", lambda e, hh=hh, r=r: e.matmul(
                        pp[0][:, 128 * r:128 * (r + 1)],
                        Psl[:, 128 * r:128 * (r + 1)],
                        Tbuf[:, 128 * hh:128 * (hh + 1)], start=True, stop=True))
                vm = sig("tensor")
                wait("vector", s_mm, vm)
                c0_, c1_ = 512 * g4, 512 * (g4 + 1)
                rec("vector", lambda e, c0_=c0_, c1_=c1_: e.tensor_tensor(
                    bl1[:], pp[0][:, 0:512], Jsl[:], ALU.add))
                rec("vector", lambda e: e.drain()) if VEC_DRAINS else None
                rec("vector", lambda e, c0_=c0_, c1_=c1_: e.tensor_tensor(
                    bl2[:], bl1[:], Tbuf[:, c0_:c1_], ALU.subtract))
                rec("vector", lambda e: e.drain()) if VEC_DRAINS else None
                rec("vector", lambda e, p_=p_, c0_=c0_, c1_=c1_:
                    e.scalar_tensor_tensor(Tbuf[:, c0_:c1_], bl2[:],
                                           c_pmask[:, p_:p_ + 1],
                                           Tbuf[:, c0_:c1_], ALU.mult, ALU.add))
                rec("vector", lambda e: e.drain()) if VEC_DRAINS else None
                vv = sig("vector")
                wait("tensor", s_v, vv)
                wait("gpsimd", s_v, vv)
            barrier(("tensor", "vector", "gpsimd"))

        ob = sbh(ssc, L + "ob", 128, TC, BF16)
        ccb = sbh(ssc, L + "ccb", 128, TC, BF16)
        o_wrk = sbh(ssc, L + "o_wrk", 128, TC)
        onrm = [sbh(ssc, L + f"onrm{j}", 128, TC, BF16) for j in range(8)]
        zsil = sbh(ssc, L + "zsil", 128, 512, BF16)
        cc3 = cc_dram[layer].ap()

        for b8 in range(HV // 8):
            for j in range(8):
                hh = 8 * b8 + j
                v1 = dma("sync", ob[:], o0_dram[layer].ap()[hh])
                v2 = dma("sync", ccb[:].rearrange("a (i c) -> a i c", c=128),
                         cc3[hh].rearrange("i a c -> a i c"))
                wait("tensor", s_ld, v2)
                for i in range(NCH):
                    rec("tensor", lambda e, i=i, hh=hh: e.matmul(
                        pp[i // 4][:, 128 * (i % 4):128 * (i % 4 + 1)],
                        Tbuf[:, 128 * hh:128 * (hh + 1)],
                        ccb[:, 128 * i:128 * (i + 1)], start=True, stop=True))
                vm = sig("tensor")
                wait("vector", s_mm, vm)
                wait("vector", s_ld, v2)
                rec("vector", lambda e: e.tensor_tensor(
                    o_wrk[:, 0:512], ob[:, 0:512], pp[0][:, 0:512], ALU.add))
                rec("vector", lambda e: e.tensor_tensor(
                    o_wrk[:, 512:1024], ob[:, 512:1024], pp[1][:, 0:512], ALU.add))
                rec("vector", lambda e: e.drain()) if VEC_DRAINS else None
                vv = sig("vector")
                wait("scalar", s_v, vv)
                wait("sync", s_v, vv)
                rec("scalar", lambda e: e.activation(f_scr[:, 0:TC], o_wrk[:],
                                                     ACT.Square))
                vs = sig("scalar")
                wait("tensor", s_s, vs)
                for ci, (o_, n_) in enumerate(_cc(TC, 512)):
                    rec("tensor", lambda e, o_=o_, n_=n_, ci=ci: e.matmul(
                        pp[2][0:1, 256 * ci:256 * ci + n_ if False else 0:n_]
                        if False else pp[2 + ci][0:1, 0:n_],
                        c_onec[:], f_scr[:, o_:o_ + n_], start=True, stop=True))
                vm = sig("tensor")
                wait("vector", s_mm, vm)
                rec("vector", lambda e: e.tensor_copy(r_mu[0:1, 0:512], pp[2][0:1, 0:512]))
                rec("vector", lambda e: e.tensor_copy(r_mu[0:1, 512:1024],
                                                      pp[3][0:1, 0:512]))
                rec("vector", lambda e: e.drain()) if VEC_DRAINS else None
                rec("vector", lambda e: e.tensor_scalar(r_mu[0:1, 0:TC], r_mu[0:1, 0:TC],
                                                        1.0 / DV, None, ALU.mult))
                vv = sig("vector")
                wait("scalar", s_v, vv)
                rec("scalar", lambda e: e.activation(r_t0[0:1, 0:TC],
                                                     r_mu[0:1, 0:TC], ACT.Sqrt,
                                                     bias=eps_sl))
                vs = sig("scalar")
                wait("vector", s_s, vs)
                rec("vector", lambda e: e.reciprocal(r_t0[0:1, 0:TC],
                                                     r_t0[0:1, 0:TC]))
                rec("vector", lambda e: e.drain()) if VEC_DRAINS else None
                vv = sig("vector")
                wait("tensor", s_v, vv)
                for ci, (o_, n_) in enumerate(_cc(TC, 512)):
                    rec("tensor", lambda e, o_=o_, n_=n_, ci=ci: e.matmul(
                        pp[2 + ci][:, 0:n_], c_oner[:],
                        r_t0[0:1, o_:o_ + n_], start=True, stop=True))
                vm = sig("tensor")
                wait("vector", s_mm, vm)
                for ci, (o_, n_) in enumerate(_cc(TC, 512)):
                    rec("vector", lambda e, j=j, o_=o_, n_=n_, ci=ci:
                        e.scalar_tensor_tensor(onrm[j][:, o_:o_ + n_],
                                               o_wrk[:, o_:o_ + n_], gnw[:],
                                               pp[2 + ci][:, 0:n_],
                                               ALU.mult, ALU.mult))
                rec("vector", lambda e: e.drain()) if VEC_DRAINS else None
                vv = sig("vector")
                wait("tensor", s_v, vv)
                wait("scalar", s_v, vv)
            barrier(("tensor", "vector", "scalar", "gpsimd"))

            if gated:
                def evac_z(jg, bank, toff, tn):
                    wait("scalar", s_v, s_v.n)
                    rec("scalar", lambda e, bank=bank, tn=tn: e.activation(
                        zsil[:, 0:tn], bank[:, 0:tn], ACT.Silu))
                    vs_ = sig("scalar")
                    wait("vector", s_s, vs_)
                    rec("vector", lambda e, jg=jg, toff=toff, tn=tn: e.tensor_tensor(
                        onrm[jg][:, toff - HALO:toff - HALO + tn],
                        onrm[jg][:, toff - HALO:toff - HALO + tn],
                        zsil[:, 0:tn], ALU.mult))
                proj_cols(w_z, 1024 * b8, 1024, TOKS_OWN, evac_z, slab_of=4)

            wrows = wout[1024 * b8:1024 * (b8 + 1), :] if gated else \
                wout[1024 * b8:1024 * (b8 + 1), :]
            def evac_out(jg, bank, toff, tn):
                rec("vector", lambda e, jg=jg, bank=bank, toff=toff, tn=tn:
                    e.tensor_tensor(x_sb[jg][:, toff:toff + tn],
                                    x_sb[jg][:, toff:toff + tn],
                                    bank[:, 0:tn], ALU.add))
            proj_cols(wrows, 0, D, TOKS_OWN, evac_out, kt=8, src=onrm,
                      slab_of=4, src_off=-HALO)
            barrier(("tensor", "vector", "scalar", "gpsimd"))
        ssc.close()

    tern_cc = [0]

    def emit_ternary():
        tsc = contextlib.ExitStack()
        tw = sbh(tsc, "t_w", 128, 1024)
        tq = sbh(tsc, "t_q", 128, 1024, BF16)
        tcmp = sbh(tsc, "t_c", 128, 1024)
        tacc = sbh(tsc, "t_acc", 128, 8)
        tred = sbh(tsc, "t_red", 128, 1)
        t32a = sbh(tsc, "t32a", 32, 32)
        t32b = sbh(tsc, "t32b", 32, 32)
        throw = sbh(tsc, "throw", 1, 8)
        scrow = sbh(tsc, "scrow", 1, 8)
        thcol = {w: sbh(tsc, f"th_{w}", 128, 2)
                 for w in ("ffn0", "up", "gate", "down")}
        WDEFS = [("ffn0", wT_ffn0, D, NSH0), ("up", wT_up, D, NSHE),
                 ("gate", wT_gate, D, NSHE), ("down", wT_down, DE, NSHD)]
        rec("vector", lambda e: e.memset(tacc[:], 0.0))
        rec("vector", lambda e: e.memset(t32a[:], 0.0))
        rec("vector", lambda e: e.memset(t32b[:], 0.0))
        rec("vector", lambda e: e.memset(throw[:], 0.0))
        rec("vector", lambda e: e.memset(scrow[:, 4:8], 0.0))
        rec("vector", lambda e: e.memset(scrow[:, 0:1], 1.0 / (2048.0 * 2048.0)))
        rec("vector", lambda e: e.memset(scrow[:, 1:2], 1.0 / (DE * D)))
        rec("vector", lambda e: e.memset(scrow[:, 2:3], 1.0 / (DE * D)))
        rec("vector", lambda e: e.memset(scrow[:, 3:4], 1.0 / (D * DE)))
        sig("vector")
        barrier(("vector", "sync", "gpsimd", "scalar", "tensor"))
        for wi, (nm, wd, nr, ncol) in enumerate(WDEFS):
            for kr in range(nr // 128):
                for (o_, n_) in _cc(ncol, 1024):
                    v1 = dma("sync", tw[:, 0:n_],
                             wd[kr * 128:(kr + 1) * 128, o_:o_ + n_])
                    wait("vector", s_ld, v1)
                    rec("vector", lambda e, n_=n_: e.tensor_reduce(
                        tred[:], tw[:, 0:n_], AX, ALU.add,
                        apply_absolute_value=True))
                    rec("vector", lambda e: e.drain()) if VEC_DRAINS else None
                    rec("vector", lambda e, wi=wi: e.tensor_tensor(
                        tacc[:, wi:wi + 1], tacc[:, wi:wi + 1], tred[:], ALU.add))
                    rec("vector", lambda e: e.drain()) if VEC_DRAINS else None
                    vv = sig("vector")
                    wait("sync", s_v, vv)
        vv = sig("vector")
        wait("tensor", s_v, vv)
        rec("tensor", lambda e: e.matmul(ps_m[0:1, 0:8], c_onec[:], tacc[:, 0:8],
                                         start=True, stop=True))
        vm = sig("tensor")
        wait("vector", s_mm, vm)
        rec("vector", lambda e: e.tensor_copy(throw[:], ps_m[0:1, 0:8]))
        rec("vector", lambda e: e.drain()) if VEC_DRAINS else None
        vv = sig("vector")
        wait("sync", s_v, vv)
        dma("sync", tern_in.ap(), throw[:])
        wait("gpsimd", s_ld, s_ld.n)
        vcc = s_cc.bump(1)
        rec("gpsimd", lambda e: e.collective_compute(
            "AllReduce", ALU.add, replica_groups=[list(range(NCORES))],
            ins=[tern_in.ap().opt()], outs=[tern_out.ap().opt()])
            .then_inc(s_cc.h, 1), can_inc=False)
        wait("gpsimd", s_cc, vcc)
        dma("gpsimd", throw[:], tern_out.ap())
        vg = gfence()
        wait("vector", s_g, vg)
        rec("vector", lambda e: e.tensor_tensor(throw[:, 0:4], throw[:, 0:4],
                                                scrow[:, 0:4], ALU.mult))
        rec("vector", lambda e: e.drain()) if VEC_DRAINS else None
        rec("vector", lambda e: e.tensor_scalar(throw[:, 0:1], throw[:, 0:1],
                                                1e-5, None, ALU.max))
        rec("vector", lambda e: e.drain()) if VEC_DRAINS else None
        vv = sig("vector")
        wait("tensor", s_v, vv)
        for wi, (nm, wd, nr, ncol) in enumerate(WDEFS):
            rec("tensor", lambda e, wi=wi: e.matmul(
                ps_m[:, 2 * wi:2 * wi + 1], c_oner[:], throw[0:1, wi:wi + 1],
                start=True, stop=True))
        vm = sig("tensor")
        wait("vector", s_mm, vm)
        for wi, (nm, wd, nr, ncol) in enumerate(WDEFS):
            eps_add = 0.0 if wi == 0 else 1e-8
            rec("vector", lambda e, wi=wi, nm=nm, eps_add=eps_add:
                e.tensor_scalar(thcol[nm][:, 0:1], ps_m[:, 2 * wi:2 * wi + 1],
                                eps_add, 0.5, ALU.add, ALU.mult))
            rec("vector", lambda e: e.drain()) if VEC_DRAINS else None
            rec("vector", lambda e, nm=nm: e.tensor_scalar(
                thcol[nm][:, 1:2], thcol[nm][:, 0:1], -1.0, None, ALU.mult))
            rec("vector", lambda e: e.drain()) if VEC_DRAINS else None
        vv = sig("vector")
        wait("sync", s_v, vv)
        barrier()
        for wi, (nm, wd, nr, ncol) in enumerate(WDEFS):
            for kr in range(nr // 128):
                for (o_, n_) in _cc(ncol, 1024):
                    v1 = dma("sync", tw[:, 0:n_],
                             wd[kr * 128:(kr + 1) * 128, o_:o_ + n_])
                    wait("vector", s_ld, v1)
                    rec("vector", lambda e, nm=nm, n_=n_: e.tensor_scalar(
                        tcmp[:, 0:n_], tw[:, 0:n_], thcol[nm][:, 1:2], None,
                        ALU.is_lt))
                    rec("vector", lambda e: e.drain()) if VEC_DRAINS else None
                    rec("vector", lambda e, nm=nm, n_=n_: e.scalar_tensor_tensor(
                        tq[:, 0:n_], tw[:, 0:n_], thcol[nm][:, 0:1],
                        tcmp[:, 0:n_], ALU.is_gt, ALU.subtract))
                    rec("vector", lambda e: e.drain()) if VEC_DRAINS else None
                    vv = sig("vector")
                    wait("sync", s_v, vv)
                    dma("sync", tq_in[nm].ap()[kr * 128:(kr + 1) * 128, o_:o_ + n_],
                        tq[:, 0:n_], sem=s_st)
                    wait("vector", s_st, s_st.n)
        wait("gpsimd", s_st, s_st.n)
        for nm in ("ffn0", "up", "gate", "down"):
            vcc = s_cc.bump(1)
            rec("gpsimd", lambda e, nm=nm: e.collective_compute(
                "AllGather", ALU.bypass, replica_groups=[list(range(NCORES))],
                ins=[tq_in[nm].ap().opt()], outs=[tq_out[nm].ap().opt()])
                .then_inc(s_cc.h, 1), can_inc=False)
        tern_cc[0] = s_cc.n
        sig("gpsimd")
        barrier()
        tsc.close()

    def emit_ffn0():
        fsc = contextlib.ExitStack()
        gcol = sbh(fsc, "f0_g", 128, KT)
        xnrow = sbh(fsc, "f0_row", 1, TC)
        amrow = sbh(fsc, "f0_am", 128, 2)
        arow8 = sbh(fsc, "f0_a8", 1, 8)
        t32a = sbh(fsc, "f0_32a", 32, 32)
        t32b = sbh(fsc, "f0_32b", 32, 32)
        sc2 = sbh(fsc, "f0_sc", 128, 2)
        xq = sbh(fsc, "f0_xq", 128, TC, BF16)
        rs1 = sbh(fsc, "f0_rs1", 128, TC)
        rs2 = sbh(fsc, "f0_rs2", 128, TC)
        wtile = sbh(fsc, "f0_wt", 128, 512, BF16)
        for i in range(KT):
            v1 = dma("sync", gcol[:, i:i + 1], g_ffn0[i * 128:(i + 1) * 128, :])
        emit_ln(HALO, TC)
        emit_xln(HALO, TC)
        barrier()
        rec("vector", lambda e: e.tensor_scalar(xnrow[:], r_t0[0:1, HALO:TF],
                                                1e-5, None, ALU.add))
        rec("vector", lambda e: e.drain()) if VEC_DRAINS else None
        rec("vector", lambda e: e.reciprocal(xnrow[:], xnrow[:]))
        rec("vector", lambda e: e.drain()) if VEC_DRAINS else None
        rec("vector", lambda e: e.tensor_tensor(xnrow[:], xnrow[:],
                                                r_t0[0:1, HALO:TF], ALU.mult))
        rec("vector", lambda e: e.drain()) if VEC_DRAINS else None
        vv = sig("vector")
        wait("scalar", s_v, vv)
        rec("scalar", lambda e: e.activation(xnrow[:], xnrow[:], ACT.Sqrt,
                                             bias=c_eps[0:1, 0:1]))
        vs = sig("scalar")
        wait("vector", s_s, vs)
        rec("vector", lambda e: e.reciprocal(xnrow[:], xnrow[:]))
        rec("vector", lambda e: e.drain()) if VEC_DRAINS else None
        vv = sig("vector")
        wait("tensor", s_v, vv)
        rec("vector", lambda e: e.memset(amrow[:, 0:1], 0.0))
        sig("vector")
        for ci, (o_, n_) in enumerate(_cc(TC, 512, HALO)):
            rec("tensor", lambda e, o_=o_, n_=n_, ci=ci: e.matmul(
                pp[4 + ci][:, 0:n_], c_oner[:], xnrow[0:1, o_ - HALO:o_ - HALO + n_],
                start=True, stop=True))
        vm = sig("tensor")
        wait("vector", s_mm, vm)
        for i in range(KT):
            for ci, (o_, n_) in enumerate(_cc(TC, 512, HALO)):
                rec("vector", lambda e, i=i, o_=o_, n_=n_, ci=ci: e.tensor_tensor(
                    f_scr[:, o_:o_ + n_], xln[i][:, o_:o_ + n_],
                    pp[4 + ci][:, 0:n_], ALU.mult))
            rec("vector", lambda e: e.drain()) if VEC_DRAINS else None
            rec("vector", lambda e, i=i: e.tensor_scalar(
                f_scr[:, HALO:TF], f_scr[:, HALO:TF], gcol[:, i:i + 1], None,
                ALU.mult))
            rec("vector", lambda e: e.drain()) if VEC_DRAINS else None
            rec("vector", lambda e: e.tensor_reduce(
                amrow[:, 1:2], f_scr[:, HALO:TF], AX, ALU.max,
                apply_absolute_value=True))
            rec("vector", lambda e: e.drain()) if VEC_DRAINS else None
            rec("vector", lambda e: e.tensor_tensor(
                amrow[:, 0:1], amrow[:, 0:1], amrow[:, 1:2], ALU.max))
            rec("vector", lambda e: e.drain()) if VEC_DRAINS else None
        vv = sig("vector")
        wait("tensor", s_v, vv)
        rec("tensor", lambda e: e.transpose(ps_m[0:1, 0:128], amrow[:, 0:1],
                                            c_idn32[:]))
        vm = sig("tensor")
        wait("vector", s_mm, vm)
        rec("vector", lambda e: e.memset(arow8[:], 0.0))
        rec("vector", lambda e: e.drain()) if VEC_DRAINS else None
        rec("vector", lambda e: e.tensor_reduce(arow8[:, 0:1], ps_m[0:1, 0:128],
                                                AX, ALU.max))
        rec("vector", lambda e: e.drain()) if VEC_DRAINS else None
        vv = sig("vector")
        wait("sync", s_v, vv)
        dma("sync", amax_in.ap(), arow8[:])
        wait("gpsimd", s_ld, s_ld.n)
        vcc = s_cc.bump(1)
        rec("gpsimd", lambda e: e.collective_compute(
            "AllReduce", ALU.max, replica_groups=[list(range(NCORES))],
            ins=[amax_in.ap().opt()], outs=[amax_out.ap().opt()])
            .then_inc(s_cc.h, 1), can_inc=False)
        wait("gpsimd", s_cc, vcc)
        dma("gpsimd", arow8[:], amax_out.ap())
        vg = gfence()
        wait("vector", s_g, vg)
        rec("vector", lambda e: e.tensor_scalar(arow8[:, 0:1], arow8[:, 0:1],
                                                1e-5, None, ALU.max))
        rec("vector", lambda e: e.drain()) if VEC_DRAINS else None
        rec("vector", lambda e: e.reciprocal(arow8[:, 1:2], arow8[:, 0:1]))
        rec("vector", lambda e: e.drain()) if VEC_DRAINS else None
        rec("vector", lambda e: e.tensor_scalar(arow8[:, 1:2], arow8[:, 1:2],
                                                127.0, None, ALU.mult))
        rec("vector", lambda e: e.drain()) if VEC_DRAINS else None
        vv = sig("vector")
        wait("tensor", s_v, vv)
        rec("tensor", lambda e: e.matmul(ps_m[:, 0:2], c_oner[:], arow8[0:1, 0:2],
                                         start=True, stop=True))
        vm = sig("tensor")
        wait("vector", s_mm, vm)
        rec("vector", lambda e: e.tensor_copy(sc2[:], ps_m[:, 0:2]))
        rec("vector", lambda e: e.drain()) if VEC_DRAINS else None
        vv = sig("vector")
        wait("tensor", s_v, vv)

        for ci2, (o2, n2) in enumerate(_cc(TC, 512, HALO)):
            pass
        for i in range(KT):
            for ci, (o_, n_) in enumerate(_cc(TC, 512, HALO)):
                rec("vector", lambda e, i=i, o_=o_, n_=n_, ci=ci: e.tensor_tensor(
                    f_scr[:, o_:o_ + n_], xln[i][:, o_:o_ + n_],
                    pp[4 + ci][:, 0:n_], ALU.mult))
            rec("vector", lambda e: e.drain()) if VEC_DRAINS else None
            rec("vector", lambda e, i=i: e.tensor_scalar(
                f_scr[:, HALO:TF], f_scr[:, HALO:TF], gcol[:, i:i + 1], None,
                ALU.mult))
            rec("vector", lambda e: e.drain()) if VEC_DRAINS else None
            rec("vector", lambda e: e.tensor_scalar(
                f_scr[:, HALO:TF], f_scr[:, HALO:TF], sc2[:, 1:2], None,
                ALU.mult))
            rec("vector", lambda e: e.drain()) if VEC_DRAINS else None
            rec("vector", lambda e: e.tensor_scalar(
                f_scr[:, HALO:TF], f_scr[:, HALO:TF], 127.0, -128.0,
                ALU.min, ALU.max))
            rec("vector", lambda e: e.drain()) if VEC_DRAINS else None
            rec("vector", lambda e: e.tensor_scalar(
                rs1[:, 0:TC], f_scr[:, HALO:TF], 384.5, None, ALU.add))
            rec("vector", lambda e: e.drain()) if VEC_DRAINS else None
            rec("vector", lambda e: e.tensor_scalar(
                rs1[:, 0:TC].bitcast(mybir.dt.int32),
                rs1[:, 0:TC].bitcast(mybir.dt.int32), -32768, None,
                ALU.bitwise_and))
            rec("vector", lambda e: e.drain()) if VEC_DRAINS else None
            rec("vector", lambda e: e.tensor_scalar(
                xq[:, 0:TC], rs1[:, 0:TC], -384.0, None, ALU.add))
            rec("vector", lambda e: e.drain()) if VEC_DRAINS else None
            vv = sig("vector")
            wait("scalar", s_v, vv)
            rec("scalar", lambda e, i=i: e.copy(xln[i][:, HALO:TF], xq[:, 0:TC]))
            vs = sig("scalar")
            wait("vector", s_s, vs)
        barrier()
        dma("gpsimd", arow8[:, 2:3], tern_out.ap()[0:1, 0:1])
        vg = gfence()
        wait("vector", s_g, vg)
        rec("vector", lambda e: e.tensor_scalar(arow8[:, 2:3], arow8[:, 2:3],
                                                1.0 / (2048.0 * 2048.0), None,
                                                ALU.mult))
        rec("vector", lambda e: e.drain()) if VEC_DRAINS else None
        rec("vector", lambda e: e.tensor_scalar(arow8[:, 2:3], arow8[:, 2:3],
                                                1e-5, None, ALU.max))
        rec("vector", lambda e: e.drain()) if VEC_DRAINS else None
        vv = sig("vector")
        wait("tensor", s_v, vv)
        rec("tensor", lambda e: e.matmul(ps_m[:, 4:5], c_oner[:], arow8[0:1, 2:3],
                                         start=True, stop=True))
        vm = sig("tensor")
        wait("vector", s_mm, vm)
        rec("vector", lambda e: e.tensor_scalar(sc2[:, 1:2], sc2[:, 0:1],
                                                1.0 / 127.0, None, ALU.mult))
        rec("vector", lambda e: e.drain()) if VEC_DRAINS else None
        rec("vector", lambda e: e.tensor_tensor(sc2[:, 1:2], sc2[:, 1:2],
                                                ps_m[:, 4:5], ALU.mult))
        rec("vector", lambda e: e.drain()) if VEC_DRAINS else None
        vv = sig("vector")
        wait("tensor", s_v, vv)

        tqf = tq_out["ffn0"].ap()
        mm_end = {}
        ev_sigs = []
        vldqs = {}
        wait("gpsimd", s_cc, tern_cc[0])
        for j in range(KT + 1):
            if j < KT:
                r0 = (128 * j) // NSH0
                c0_ = (128 * j) % NSH0
                if j >= 2:
                    wait("gpsimd", s_mm, mm_end[j - 2])
                vq = None
                dma("gpsimd",
                    wb_t[j % 2][:].rearrange("p (k c) -> p k c", c=256)
                    [:, :, 0:128],
                    tqf[r0].rearrange("(k p) c -> p k c", p=128)
                    [:, :, c0_:c0_ + 128])
                vldqs[j] = qfence("gpsimd")
            if j == 0:
                continue
            jj = j - 1
            for (toff, tn) in TOKS_OWN:
                gi = len(ev_sigs)
                bank = pp[gi % 4]
                if gi >= 4:
                    wait("tensor", s_v, ev_sigs[gi - 4])
                wait("tensor", s_gd, vldqs[jj])
                for k in range(KT):
                    rec("tensor", lambda e, k=k, jj=jj, toff=toff, tn=tn, bank=bank:
                        e.matmul(bank[:, 0:tn], wbuf[jj % 2][k][:, 0:128],
                                 xln[k][:, toff:toff + tn],
                                 start=(k == 0), stop=(k == KT - 1)))
                vm = sig("tensor")
                wait("vector", s_mm, vm)
                rec("vector", lambda e, jj=jj, toff=toff, tn=tn, bank=bank:
                    e.scalar_tensor_tensor(x_sb[jj][:, toff:toff + tn],
                                           bank[:, 0:tn], sc2[:, 1:2],
                                           x_sb[jj][:, toff:toff + tn],
                                           ALU.mult, ALU.add))
                ev_sigs.append(sig("vector"))
            mm_end[jj] = s_mm.n
        barrier()
        fsc.close()

    def emit_ffn1():
        fsc = contextlib.ExitStack()
        hb = [sbh(fsc, f"f1_h{j}", 128, 512, BF16) for j in range(64)]
        sup = sbh(fsc, "f1_su", 128, 4)
        srow = sbh(fsc, "f1_srow", 1, 8)
        rec("vector", lambda e: e.memset(sup[:], 0.0))
        sig("vector")
        barrier(("vector", "gpsimd", "sync"))
        dma("gpsimd", srow[:], tern_out.ap())
        vg = gfence()
        wait("vector", s_g, vg)
        rec("vector", lambda e: e.tensor_scalar(srow[:, 1:4], srow[:, 1:4],
                                                1.0 / (DE * D), 1e-8,
                                                ALU.mult, ALU.add))
        rec("vector", lambda e: e.drain()) if VEC_DRAINS else None
        vv = sig("vector")
        wait("tensor", s_v, vv)
        rec("tensor", lambda e: e.matmul(ps_m[:, 0:3], c_oner[:],
                                         srow[0:1, 1:4], start=True, stop=True))
        vm = sig("tensor")
        wait("vector", s_mm, vm)
        rec("vector", lambda e: e.tensor_copy(sup[:, 0:3], ps_m[:, 0:3]))
        rec("vector", lambda e: e.drain()) if VEC_DRAINS else None
        vv = sig("vector")
        wait("scalar", s_v, vv)
        wait("tensor", s_v, vv)

        emit_ln(HALO, TC)
        emit_xln(HALO, TC)
        barrier()
        wait("gpsimd", s_cc, tern_cc[0])
        tqu = tq_out["up"].ap()
        tqg = tq_out["gate"].ap()
        tqd = tq_out["down"].ap()
        for (toff, tn) in [(HALO, 512), (HALO + 512, 512)]:
            ug_mm_end = {}
            ug_sigs = []
            vlds = {}
            for j in range(65):
                if j < 64:
                    r0 = (128 * j) // NSHE
                    c0_ = (128 * j) % NSHE
                    if j >= 2:
                        wait("gpsimd", s_mm, ug_mm_end[j - 2])
                    dma("gpsimd",
                        wb_t[j % 2][:].rearrange("p (k c) -> p k c", c=256)
                        [:, :, 0:128],
                        tqu[r0].rearrange("(k p) c -> p k c", p=128)
                        [:, :, c0_:c0_ + 128])
                    dma("gpsimd",
                        wb_t[j % 2][:].rearrange("p (k c) -> p k c", c=256)
                        [:, :, 128:256],
                        tqg[r0].rearrange("(k p) c -> p k c", p=128)
                        [:, :, c0_:c0_ + 128])
                    vlds[j] = qfence("gpsimd")
                if j == 0:
                    continue
                jj = j - 1
                gi = len(ug_sigs)
                if gi >= 2:
                    wait("tensor", s_v, ug_sigs[gi - 2][0])
                    wait("tensor", s_s, ug_sigs[gi - 2][1])
                wait("tensor", s_gd, vlds[jj])
                for k in range(KT):
                    rec("tensor", lambda e, k=k, jj=jj, toff=toff, tn=tn:
                        e.matmul(pp[(jj % 2) * 2][:, 0:tn],
                                 wbuf[jj % 2][k][:, 0:128],
                                 xln[k][:, toff:toff + tn],
                                 start=(k == 0), stop=(k == KT - 1)))
                for k in range(KT):
                    rec("tensor", lambda e, k=k, jj=jj, toff=toff, tn=tn:
                        e.matmul(pp[(jj % 2) * 2 + 1][:, 0:tn],
                                 wbuf[jj % 2][k][:, 128:256],
                                 xln[k][:, toff:toff + tn],
                                 start=(k == 0), stop=(k == KT - 1)))
                vm = sig("tensor")
                wait("scalar", s_mm, vm)
                wait("scalar", s_v, s_v.n)
                rec("scalar", lambda e, jj=jj, tn=tn: e.activation(
                    f_scr[:, 0:tn], pp[(jj % 2) * 2][:, 0:tn], ACT.Silu,
                    scale=sup[:, 0:1]))
                vs = sig("scalar")
                wait("vector", s_s, vs)
                rec("vector", lambda e, jj=jj, tn=tn: e.scalar_tensor_tensor(
                    hb[jj][:, 0:tn], pp[(jj % 2) * 2 + 1][:, 0:tn], sup[:, 1:2],
                    f_scr[:, 0:tn], ALU.mult, ALU.mult))
                ug_sigs.append((sig("vector"), sig("scalar")))
                ug_mm_end[jj] = s_mm.n
            barrier(("tensor", "vector", "scalar", "gpsimd"))
            dn_sigs = []
            for j in range(KT):
                r0 = (128 * j) // NSHD
                c0_ = (128 * j) % NSHD
                if j >= 1:
                    wait("gpsimd", s_mm, s_mm.n)
                for kb in range(4):
                    dma("gpsimd",
                        wb_t[kb // 2][:].rearrange("p (k c) -> p k c", c=256)
                        [:, :, 128 * (kb % 2):128 * (kb % 2) + 128],
                        tqd[r0, 2048 * kb:2048 * (kb + 1), :]
                        .rearrange("(k p) c -> p k c", p=128)
                        [:, :, c0_:c0_ + 128])
                vldd = qfence("gpsimd")
                gi = len(dn_sigs)
                bank = pp[4 + (gi % 2)]
                if gi >= 2:
                    wait("tensor", s_v, dn_sigs[gi - 2])
                wait("tensor", s_gd, vldd)
                for k in range(64):
                    rec("tensor", lambda e, k=k, tn=tn, bank=bank:
                        e.matmul(bank[:, 0:tn],
                                 wbuf[k // 32][k % 16][:, 128 * ((k // 16) % 2):
                                                       128 * ((k // 16) % 2) + 128],
                                 hb[k][:, 0:tn],
                                 start=(k == 0), stop=(k == 63)))
                vm = sig("tensor")
                wait("vector", s_mm, vm)
                rec("vector", lambda e, j=j, toff=toff, tn=tn, bank=bank:
                    e.scalar_tensor_tensor(x_sb[j][:, toff:toff + tn],
                                           bank[:, 0:tn], sup[:, 2:3],
                                           x_sb[j][:, toff:toff + tn],
                                           ALU.mult, ALU.add))
                dn_sigs.append(sig("vector"))
            barrier(("tensor", "vector", "scalar", "gpsimd"))
        fsc.close()


    def tap_x(nm):
        tp = tap(nm, [D, TC])
        if tp is not None:
            barrier()
            for i in range(KT):
                dma("sync", tp[i * 128:(i + 1) * 128, :], x_sb[i][:, HALO:TF],
                    sem=s_st)
            wait("sync", s_st, s_st.n)
            barrier()

    emit_ternary()
    gates0 = {"bcum": bcum, "lam": lam, "beta": beta0, "lb": lb0, "bT": bT,
              "betT": betT, "gamT": gamT, "lamCr": lamCr, "hsel": hsel0}
    emit_mixer(0, gates0)
    l0sc.close()
    emit_scan_out(0)
    barrier()
    tap_x("x1")
    emit_ffn0()
    barrier()
    tap_x("x2")

    l1sc = contextlib.ExitStack()
    beta1 = sbh(l1sc, "beta1", 16, TC)
    betT1 = [sbh(l1sc, f"betT1_{i}", 128, 16) for i in range(NCH)]
    emit_ln(HALO, TC)
    emit_xln(HALO, TC)
    barrier()
    for k in range(KT):
        dma("gpsimd", wbuf[0][k][:, 0:16], w_b1[k * 128:(k + 1) * 128, :])
    vldb = qfence("gpsimd")
    for (toff, tn) in TOKS_OWN:
        wait("tensor", s_gd, vldb)
        for k in range(KT):
            rec("tensor", lambda e, k=k, toff=toff, tn=tn:
                e.matmul(pp[0][0:16, 0:tn], wbuf[0][k][:, 0:16],
                         xln[k][:, toff:toff + tn],
                         start=(k == 0), stop=(k == KT - 1)))
        vm = sig("tensor")
        wait("scalar", s_mm, vm)
        rec("scalar", lambda e, toff=toff, tn=tn: e.activation(
            beta1[:, toff - HALO:toff - HALO + tn], pp[0][0:16, 0:tn],
            ACT.Sigmoid))
        vs = sig("scalar")
        wait("tensor", s_s, vs)
    barrier()
    for i in range(NCH):
        rec("tensor", lambda e, i=i: e.transpose(ps_m[:, 0:16],
                                                 beta1[:, i * CH:(i + 1) * CH],
                                                 c_idn32[0:16, 0:16]))
        vm = sig("tensor")
        wait("vector", s_mm, vm)
        rec("vector", lambda e, i=i: e.tensor_copy(betT1[i][:], ps_m[:, 0:16]))
        vv = sig("vector")
        wait("tensor", s_v, vv)
    barrier()
    gates1 = {"beta": beta1, "betT": betT1, "hsel": hsel0}
    emit_mixer(1, gates1)
    l1sc.close()
    emit_scan_out(1)
    barrier()
    msel.close()
    tap_x("x3")
    emit_ffn1()
    barrier()

    for nm, dt_, shp, srcap in (
        ("o0_l0", BF16, [L0_HV, DV, TC], o0_dram[0].ap()),
        ("cc_l0", BF16, [L0_HV, NCH, DK, CH], cc_dram[0].ap()),
        ("agg0_in", F32, [128, L0_HV * 384], agg_in[0].ap()),
        ("o0_l1", BF16, [L1_H, DV, TC], o0_dram[1].ap()),
    ):
        tp = tap(nm, shp, dt_)
        if tp is not None:
            dma("sync", tp, srcap, sem=s_st)
    wait("sync", s_st, s_st.n)

    for i in range(KT):
        dma("sync", y_out[i * 128:(i + 1) * 128, :], x_sb[i][:, HALO:TF], sem=s_st)
    wait("sync", s_st, s_st.n)

    block = E(nc.Block())

    @block.sync
    def _(e):
        for fn in streams["sync"]:
            fn(e)

    @block.scalar
    def _(e):
        for fn in streams["scalar"]:
            fn(e)

    @block.vector
    def _(e):
        for fn in streams["vector"]:
            fn(e)

    @block.tensor
    def _(e):
        for fn in streams["tensor"]:
            fn(e)

    @block.gpsimd
    def _(e):
        for fn in streams["gpsimd"]:
            fn(e)

    stack.close()
    return nc, taps


def _hsel(nh):
    m = np.zeros((nh, nh * 128), np.float32)
    for h in range(nh):
        m[h, h * 128:(h + 1) * 128] = 1.0
    return m


def make_inputs(inputs, core):
    x = np.asarray(inputs["L0_attn_x"])[0]
    xt = np.ascontiguousarray(x.T)
    c0 = core * TC
    if core == 0:
        halo = np.zeros((D, HALO), np.float32)
    else:
        halo = xt[:, c0 - HALO:c0]
    xs = np.concatenate([halo, xt[:, c0:c0 + TC]], axis=1)

    vc = np.zeros((128, 8), np.float32)
    vc[:32, 0] = np.asarray(inputs["L0_attn_A_log"])
    vc[:32, 1] = np.asarray(inputs["L0_attn_dt_bias"])
    vc[:, 2] = np.asarray(inputs["L0_attn_norm_weight"])
    vc[:, 3] = np.asarray(inputs["L1_attn_o_norm_weight"])
    vc[0, 4] = 1.0 / (2048 * 2048)
    vc[1, 4] = 1.0 / (DE * D)
    vc[2, 4] = 1.0 / (DE * D)
    vc[3, 4] = 1.0 / (D * DE)

    pm = np.zeros((128, NCORES), np.float32)
    pm[:, :core] = 1.0

    return {
        "x": np.ascontiguousarray(xs, np.float32),
        "w_qkv": np.asarray(inputs["L0_attn_W_qkv"], np.float32),
        "w_z": np.asarray(inputs["L0_attn_W_z"], np.float32),
        "w_ab": np.ascontiguousarray(np.concatenate(
            [np.asarray(inputs["L0_attn_W_a"]), np.asarray(inputs["L0_attn_W_b"])],
            axis=1), np.float32),
        "w_out0": np.asarray(inputs["L0_attn_W_out"], np.float32),
        "conv_w": np.asarray(inputs["L0_attn_conv_w"], np.float32),
        "vec_col": vc,
        "g_ffn0": np.ascontiguousarray(
            np.asarray(inputs["L0_ffn_norm_weight"], np.float32).reshape(D, 1)),
        "wT_ffn0": np.ascontiguousarray(
            np.asarray(inputs["L0_ffn_weight"], np.float32).T
            [:, core * NSH0:(core + 1) * NSH0]),
        "w_q1": np.asarray(inputs["L1_attn_Wq"], np.float32),
        "w_k1": np.asarray(inputs["L1_attn_Wk"], np.float32),
        "w_v1": np.asarray(inputs["L1_attn_Wv"], np.float32),
        "w_b1": np.asarray(inputs["L1_attn_Wb"], np.float32),
        "w_o1": np.asarray(inputs["L1_attn_Wo"], np.float32),
        "wT_up": np.ascontiguousarray(
            np.asarray(inputs["L1_ffn_W_up"], np.float32).T
            [:, core * NSHE:(core + 1) * NSHE]),
        "wT_gate": np.ascontiguousarray(
            np.asarray(inputs["L1_ffn_W_gate"], np.float32).T
            [:, core * NSHE:(core + 1) * NSHE]),
        "wT_down": np.ascontiguousarray(
            np.asarray(inputs["L1_ffn_W_down"], np.float32).T
            [:, core * NSHD:(core + 1) * NSHD]),
        "pmask": pm,
        "hsel0": _hsel(32),
    }


def kernel(**inputs):
    nc, taps = build_program()
    in_maps = [make_inputs(inputs, c) for c in range(NCORES)]
    res = run_bass_kernel_spmd(nc, in_maps, core_ids=list(range(NCORES)))
    outs = [np.asarray(r["y"]) for r in res.results]
    full = np.concatenate(outs, axis=1)
    return np.ascontiguousarray(full.T)[None]


# version 25 (final)
# speedup vs baseline: 1.0260x; 1.0138x over previous
import contextlib
import numpy as np

import concourse.bass as bass
import concourse.mybir as mybir
from concourse.bass_utils import run_bass_kernel_spmd

F32 = mybir.dt.float32
BF16 = mybir.dt.bfloat16
ALU = mybir.AluOpType
ACT = mybir.ActivationFunctionType
AX = mybir.AxisListType.X

NCORES = 8
D = 2048
T = 8192
TC = 1024
HALO = 4
TF = TC + HALO
KT = D // 128
CH = 128
NCH = TC // CH

L0_HV, L0_HK, DK, DV = 32, 16, 128, 128
KW = 4
L1_H = 16
DE = 8192
NSH0 = 2048 // NCORES
NSHE = DE // NCORES
NSHD = D // NCORES

DEBUG_TAPS = ()
SIM_DRAINS = True
VEC_DRAINS = True


def _cc(total, step=512, off0=0):
    out = []
    o = 0
    while o < total:
        n = min(step, total - o)
        out.append((off0 + o, n))
        o += n
    return out


class Sem:
    def __init__(self, h):
        self.h = h
        self.n = 0

    def bump(self, k=1):
        self.n += k
        return self.n


def build_program():
    nc = bass.Bass("TRN2", target_bir_lowering=False)

    def dram_in(name, shape, dtype=F32):
        return nc.dram_tensor(name, shape, dtype, kind="ExternalInput").ap()

    x_in = dram_in("x", [D, TF])
    w_qkv = dram_in("w_qkv", [D, 8192])
    w_z = dram_in("w_z", [D, 4096])
    w_ab = dram_in("w_ab", [D, 64])
    w_out0 = dram_in("w_out0", [4096, D])
    conv_w = dram_in("conv_w", [8192, KW])
    vec_col = dram_in("vec_col", [128, 8])
    g_ffn0 = dram_in("g_ffn0", [D, 1])
    wT_ffn0 = dram_in("wT_ffn0", [D, NSH0])
    w_q1 = dram_in("w_q1", [D, D])
    w_k1 = dram_in("w_k1", [D, D])
    w_v1 = dram_in("w_v1", [D, D])
    w_b1 = dram_in("w_b1", [D, L1_H])
    w_o1 = dram_in("w_o1", [D, D])
    wT_up = dram_in("wT_up", [D, NSHE])
    wT_gate = dram_in("wT_gate", [D, NSHE])
    wT_down = dram_in("wT_down", [DE, NSHD])
    pmask = dram_in("pmask", [128, NCORES])
    hsel0_in = dram_in("hsel0", [32, 32 * 128])

    y_out = nc.dram_tensor("y", [D, TC], F32, kind="ExternalOutput").ap()

    taps = {}

    tapd_any = [False]

    def tap(name, shape, dtype=F32):
        if name in DEBUG_TAPS:
            tapd_any[0] = True
            taps[name] = nc.dram_tensor("tap_" + name, shape, dtype,
                                        kind="ExternalOutput").ap()
            return taps[name]
        return None

    tern_in = nc.dram_tensor("tern_in", [1, 8], F32)
    tern_out = nc.dram_tensor("tern_out", [1, 8], F32)
    tq_in = {
        "ffn0": nc.dram_tensor("tq_ffn0_in", [D, NSH0], BF16),
        "up": nc.dram_tensor("tq_up_in", [D, NSHE], BF16),
        "gate": nc.dram_tensor("tq_gate_in", [D, NSHE], BF16),
        "down": nc.dram_tensor("tq_down_in", [DE, NSHD], BF16),
    }
    tq_out = {
        "ffn0": nc.dram_tensor("tq_ffn0_out", [NCORES, D, NSH0], BF16),
        "up": nc.dram_tensor("tq_up_out", [NCORES, D, NSHE], BF16),
        "gate": nc.dram_tensor("tq_gate_out", [NCORES, D, NSHE], BF16),
        "down": nc.dram_tensor("tq_down_out", [NCORES, DE, NSHD], BF16),
    }
    agg_in = {0: nc.dram_tensor("agg0_in", [128, L0_HV * 384], F32),
              1: nc.dram_tensor("agg1_in", [128, L1_H * 384], F32)}
    agg_out = {0: nc.dram_tensor("agg0_out", [NCORES, 128, L0_HV * 384], F32),
               1: nc.dram_tensor("agg1_out", [NCORES, 128, L1_H * 384], F32)}
    amax_in = nc.dram_tensor("amax_in", [1, 8], F32)
    amax_out = nc.dram_tensor("amax_out", [1, 8], F32)
    o0_dram = {0: nc.dram_tensor("o0_l0", [L0_HV, DV, TC], BF16),
               1: nc.dram_tensor("o0_l1", [L1_H, DV, TC], BF16)}
    cc_dram = {0: nc.dram_tensor("cc_l0", [L0_HV, NCH, DK, CH], BF16),
               1: nc.dram_tensor("cc_l1", [L1_H, NCH, DK, CH], BF16)}

    streams = {e: [] for e in ("sync", "scalar", "vector", "tensor", "gpsimd")}
    incable = {e: [] for e in ("sync", "scalar", "vector", "tensor", "gpsimd")}

    def rec(eng, fn, can_inc=True):
        streams[eng].append(fn)
        incable[eng].append(can_inc)

    stack = contextlib.ExitStack()
    E = stack.enter_context

    s_ld = Sem(E(nc.semaphore("s_ld")))
    s_gd = Sem(E(nc.semaphore("s_gd")))
    s_cc = Sem(E(nc.semaphore("s_cc")))
    s_mm = Sem(E(nc.semaphore("s_mm")))
    s_v = Sem(E(nc.semaphore("s_v")))
    s_s = Sem(E(nc.semaphore("s_s")))
    s_g = Sem(E(nc.semaphore("s_g")))
    s_st = Sem(E(nc.semaphore("s_st")))
    s_sy = Sem(E(nc.semaphore("s_sy")))

    ENG_SEM = {"tensor": s_mm, "vector": s_v, "scalar": s_s, "gpsimd": s_g,
               "sync": s_sy}

    def dma(q, out_ap, in_ap, sem=None):
        s = sem if sem is not None else (s_ld if q == "sync" else s_gd)
        v = s.bump(16)
        rec(q, lambda e, o=out_ap, i=in_ap, sh=s.h: e.dma_start(out=o, in_=i)
            .then_inc(sh, 16), can_inc=False)
        return v

    def qfence(q):
        s = s_ld if q == "sync" else s_gd
        wait(q, s, s.n)
        return s.n

    def wait(eng, s, v):
        rec(eng, lambda e, sh=s.h, vv=v: e.wait_ge(sh, vv), can_inc=False)

    def sig(eng):
        s = ENG_SEM[eng]
        v = s.bump(1)
        if streams[eng] and incable[eng][-1]:
            fn = streams[eng][-1]
            streams[eng][-1] = (lambda e, fn=fn, sh=s.h: fn(e).then_inc(sh, 1))
            incable[eng][-1] = False
        else:
            rec(eng, lambda e, sh=s.h: e.drain().then_inc(sh, 1), can_inc=False)
        return v

    def gfence():
        wait("gpsimd", s_gd, s_gd.n)
        return sig("gpsimd")

    COMPUTE = ("scalar", "vector", "tensor", "gpsimd")

    def barrier(engines=COMPUTE + ("sync",)):
        if "sync" in engines:
            wait("sync", s_ld, s_ld.n)
            wait("sync", s_st, s_st.n)
        if "gpsimd" in engines:
            wait("gpsimd", s_gd, s_gd.n)
        vals = {e: sig(e) for e in engines}
        for e in engines:
            for o in engines:
                if o != e:
                    wait(e, ENG_SEM[o], vals[o])
        return vals

    def sbh(st_, name, p, f, dt=F32):
        return st_.enter_context(nc.sbuf_tensor(name, [p, f], dt))

    def sb(name, p, f, dt=F32):
        return sbh(stack, name, p, f, dt)

    c_onec = sb("c_onec", 128, 1)
    c_oner = sb("c_oner", 1, 128)
    c_striln = sb("c_striln", 128, 128)
    c_striun = sb("c_striun", 128, 128)
    c_triu1 = sb("c_triu1", 128, 128)
    c_idn = sb("c_idn", 128, 128, BF16)
    c_idn32 = sb("c_idn32", 128, 128)
    c_pmask = sb("c_pmask", 128, NCORES)
    c_eps = sb("c_eps", 128, 8)
    vcol = sb("vcol", 128, 8)
    expA = sb("expA", 32, 1)
    gnw0 = sb("gnw0", 128, 1)
    gnw1 = sb("gnw1", 128, 1)
    zer128 = sb("zer128", 128, 128)
    c_onecb = sb("c_onecb", 128, 1, BF16)
    cw_sb = sb("cw_sb", 128, 4 * 64)

    x_sb = [sb(f"x{i}", 128, TF) for i in range(KT)]
    xln = [sb(f"xln{i}", 128, TF, BF16) for i in range(KT)]
    r_mu = sb("r_mu", 1, TF)
    r_iv = sb("r_iv", 1, TF)
    r_t0 = sb("r_t0", 1, TF)
    f_scr = sb("f_scr", 128, TF)
    wb_t = [sb(f"wbt{b}", 128, KT * 256, BF16) for b in range(2)]
    wbuf = [[wb_t[b][:, 256 * k:256 * (k + 1)] for k in range(KT)]
            for b in range(2)]

    pp = [E(nc.psum_tensor(f"pp{i}", [128, 512], F32)) for i in range(6)]
    ps_tb = E(nc.psum_tensor("ps_tb", [128, 1024], BF16))
    ps_m = E(nc.psum_tensor("ps_m", [128, 512], F32))

    g0 = lambda f: rec("gpsimd", f)
    g0(lambda e: e.memset(c_onec[:], 1.0))
    g0(lambda e: e.memset(c_onecb[:], 1.0))
    g0(lambda e: e.memset(c_oner[:], 1.0))
    g0(lambda e: e.memset(c_striln[:], -1.0))
    g0(lambda e: e.memset(c_striun[:], -1.0))
    g0(lambda e: e.memset(c_triu1[:], 1.0))
    g0(lambda e: e.memset(c_idn[:], 1.0))
    g0(lambda e: e.memset(c_idn32[:], 1.0))
    g0(lambda e: e.memset(zer128[:], 0.0))
    g0(lambda e: e.memset(c_eps[:, 0:1], 1e-6))
    g0(lambda e: e.memset(c_eps[:, 1:2], 1e-5))
    g0(lambda e: e.memset(c_eps[:, 2:3], 1.0))
    g0(lambda e: e.memset(c_eps[:, 3:4], 1e-12))
    g0(lambda e: e.memset(c_eps[:, 4:5], 0.5))
    g0(lambda e: e.drain()) if SIM_DRAINS else None
    g0(lambda e: e.affine_select(c_striln[:], c_striln[:], [[-1, 128]], ALU.is_gt,
                                 0.0, base=0, channel_multiplier=1))
    g0(lambda e: e.affine_select(c_striun[:], c_striun[:], [[1, 128]], ALU.is_gt,
                                 0.0, base=0, channel_multiplier=-1))
    g0(lambda e: e.affine_select(c_triu1[:], c_triu1[:], [[1, 128]], ALU.is_ge,
                                 0.0, base=0, channel_multiplier=-1))
    g0(lambda e: e.affine_select(c_idn[:], c_idn[:], [[-1, 128]], ALU.is_equal,
                                 0.0, base=0, channel_multiplier=1))
    g0(lambda e: e.affine_select(c_idn32[:], c_idn32[:], [[-1, 128]], ALU.is_equal,
                                 0.0, base=0, channel_multiplier=1))
    sig("gpsimd")

    for i in range(KT):
        dma("sync", x_sb[i][:], x_in[i * 128:(i + 1) * 128, :])
    dma("sync", c_pmask[:], pmask)
    dma("sync", vcol[:], vec_col)
    for t in range(64):
        dma("sync", cw_sb[:, 4 * t:4 * t + 4], conv_w[t * 128:(t + 1) * 128, :])
    barrier()
    rec("scalar", lambda e: e.activation(expA[:, 0:1], vcol[0:32, 0:1], ACT.Exp))
    rec("scalar", lambda e: e.copy(gnw0[:], vcol[:, 2:3]))
    rec("scalar", lambda e: e.copy(gnw1[:], vcol[:, 3:4]))
    sig("scalar")
    barrier()

    def emit_ln(col0, ncols):
        chunks = _cc(ncols, 512, col0)
        for (off, n) in chunks:
            for i in range(KT):
                rec("tensor", lambda e, i=i, off=off, n=n:
                    e.matmul(ps_m[0:1, 0:n], c_onec[:], x_sb[i][:, off:off + n],
                             start=(i == 0), stop=(i == KT - 1)))
            vm = sig("tensor")
            wait("vector", s_mm, vm)
            rec("vector", lambda e, off=off, n=n:
                e.tensor_copy(r_mu[:, off:off + n], ps_m[0:1, 0:n]))
            vv = sig("vector")
            wait("tensor", s_v, vv)
        for i in range(KT):
            rec("scalar", lambda e, i=i: e.activation(f_scr[:, col0:col0 + ncols],
                                                      x_sb[i][:, col0:col0 + ncols],
                                                      ACT.Square))
            vsq = sig("scalar")
            wait("tensor", s_s, vsq)
            for ci, (off, n) in enumerate(chunks):
                rec("tensor", lambda e, ci=ci, off=off, n=n, i=i:
                    e.matmul(pp[3 + ci][0:1, 0:n], c_onec[:], f_scr[:, off:off + n],
                             start=(i == 0), stop=(i == KT - 1)))
            vm = sig("tensor")
            wait("scalar", s_mm, vm)
        wait("vector", s_mm, s_mm.n)
        for ci, (off, n) in enumerate(chunks):
            rec("vector", lambda e, ci=ci, off=off, n=n:
                e.tensor_copy(r_t0[:, off:off + n], pp[3 + ci][0:1, 0:n]))
        rec("vector", lambda e: e.drain()) if VEC_DRAINS else None
        rec("vector", lambda e: e.tensor_scalar(r_mu[:, col0:col0 + ncols],
                                                r_mu[:, col0:col0 + ncols],
                                                1.0 / D, None, ALU.mult))
        rec("vector", lambda e: e.drain()) if VEC_DRAINS else None
        rec("vector", lambda e: e.tensor_tensor(r_iv[:, col0:col0 + ncols],
                                                r_mu[:, col0:col0 + ncols],
                                                r_mu[:, col0:col0 + ncols], ALU.mult))
        rec("vector", lambda e: e.drain()) if VEC_DRAINS else None
        rec("vector", lambda e: e.scalar_tensor_tensor(r_t0[:, col0:col0 + ncols],
                                                       r_t0[:, col0:col0 + ncols],
                                                       1.0 / D,
                                                       r_iv[:, col0:col0 + ncols],
                                                       ALU.mult, ALU.subtract))
        vv = sig("vector")
        wait("scalar", s_v, vv)
        rec("scalar", lambda e: e.activation(r_iv[:, col0:col0 + ncols],
                                             r_t0[:, col0:col0 + ncols],
                                             ACT.Sqrt, bias=c_eps[0:1, 1:2]))
        vs = sig("scalar")
        wait("vector", s_s, vs)
        rec("vector", lambda e: e.reciprocal(r_iv[:, col0:col0 + ncols],
                                             r_iv[:, col0:col0 + ncols]))
        vv = sig("vector")
        wait("tensor", s_v, vv)
        sig("vector")

    def emit_xln(col0, ncols):
        for (off, n) in _cc(ncols, 512, col0):
            rec("tensor", lambda e, off=off, n=n:
                e.matmul(pp[4][:, 0:n], c_oner[:], r_mu[0:1, off:off + n],
                         start=True, stop=True))
            rec("tensor", lambda e, off=off, n=n:
                e.matmul(pp[5][:, 0:n], c_oner[:], r_iv[0:1, off:off + n],
                         start=True, stop=True))
            vm = sig("tensor")
            wait("vector", s_mm, vm)
            for i in range(KT):
                rec("vector", lambda e, i=i, off=off, n=n: e.tensor_tensor(
                    f_scr[:, off:off + n], x_sb[i][:, off:off + n],
                    pp[4][:, 0:n], ALU.subtract))
                rec("vector", lambda e: e.drain()) if VEC_DRAINS else None
                rec("vector", lambda e, i=i, off=off, n=n: e.tensor_tensor(
                    xln[i][:, off:off + n], f_scr[:, off:off + n],
                    pp[5][:, 0:n], ALU.mult))
                rec("vector", lambda e: e.drain()) if VEC_DRAINS else None
            vv = sig("vector")
            wait("tensor", s_v, vv)
        return sig("vector")

    def load_wslab(w_dram, c0, ncol, buf, kt=KT, q="gpsimd", krange=None,
                   flat=None):
        if flat is None:
            flat = wb_t[0] if buf is wbuf[0] else (
                wb_t[1] if buf is wbuf[1] else None)
        if flat is not None and krange is None:
            dst = flat[:].rearrange("p (k c) -> p k c", c=256)[:, 0:kt, 0:ncol]
            srcr = w_dram[0:kt * 128, :].rearrange("(k p) c -> p k c", p=128)
            dma(q, dst, srcr[:, :, c0:c0 + ncol])
        else:
            ks = krange or range(kt)
            for k in ks:
                dma(q, buf[k][:, 0:ncol],
                    w_dram[k * 128:(k + 1) * 128, c0:c0 + ncol])
        return qfence(q)

    def proj_cols(w_dram, c0, ncols, tok_chunks, evac, kt=KT, src=None,
                  slab_of=2, evac_eng="vector", src_off=0):
        src = src or xln
        slab_of = min(slab_of, 2)
        n_of = ncols // 128
        slabs = [(s, min(slab_of, n_of - s)) for s in range(0, n_of, slab_of)]
        grp_sigs = []
        slab_mm_end = {}
        vlds = {0: load_wslab(w_dram, c0, 128 * slabs[0][1], wbuf[0], kt=kt)}
        for si, (s0, sn) in enumerate(slabs):
            b = si % 2
            if si + 1 < len(slabs):
                if si >= 1:
                    wait("gpsimd", s_mm, slab_mm_end[si - 1])
                ns0, nsn = slabs[si + 1]
                vlds[si + 1] = load_wslab(w_dram, c0 + 128 * ns0, 128 * nsn,
                                          wbuf[(si + 1) % 2], kt=kt)
            first_grp = True
            for j in range(sn):
                for (toff, tn) in tok_chunks:
                    gi = len(grp_sigs)
                    bank = pp[gi % 4]
                    if gi >= 4:
                        wait("tensor", s_v, grp_sigs[gi - 4][0])
                        wait("tensor", s_s, grp_sigs[gi - 4][1])
                    if first_grp:
                        wait("tensor", s_gd, vlds[si])
                        first_grp = False
                    for k in range(kt):
                        rec("tensor", lambda e, k=k, j=j, b=b, toff=toff + src_off,
                            tn=tn, bank=bank: e.matmul(
                                bank[:, 0:tn],
                                wbuf[b][k][:, 128 * j:128 * (j + 1)],
                                src[k][:, toff:toff + tn],
                                start=(k == 0), stop=(k == kt - 1)))
                    vm = sig("tensor")
                    wait("vector", s_mm, vm)
                    wait("scalar", s_mm, vm)
                    evac(s0 + j, bank, toff, tn)
                    grp_sigs.append((sig("vector"), sig("scalar")))
            slab_mm_end[si] = s_mm.n
        barrier(("tensor", "vector", "scalar", "gpsimd"))

    TOKS_OWN = [(HALO, 512), (HALO + 512, 512)]
    TOKS_ALL = [(0, 512), (512, 512), (1024, 4)]

    emit_ln(0, TF)
    emit_xln(0, TF)
    barrier()

    t_xln = tap("xln", [D, TF])
    if t_xln is not None:
        for i in range(KT):
            rec("scalar", lambda e, i=i: e.copy(f_scr[:], xln[i][:]))
            v = sig("scalar")
            wait("sync", s_s, v)
            dma("sync", t_xln[i * 128:(i + 1) * 128, :], f_scr[:], sem=s_st)
            wait("scalar", s_st, s_st.n)
        barrier()

    msel = contextlib.ExitStack()
    hsel0 = sbh(msel, "sb_hsel0", 32, 32 * 128)
    dma("sync", hsel0[:], hsel0_in)
    l0sc = contextlib.ExitStack()
    ab_fm = sbh(l0sc, "ab_fm", 64, TC)
    g_fm = sbh(l0sc, "g_fm", 32, TC)
    beta0 = sbh(l0sc, "beta0", 32, TC)
    bcum = sbh(l0sc, "bcum", 32, TC)
    lam = sbh(l0sc, "lam", 32, TC)
    trow = sbh(l0sc, "trow", 1, 32)
    bT = [sbh(l0sc, f"bT{i}", 128, 32) for i in range(NCH)]
    betT = [sbh(l0sc, f"betT{i}", 128, 32) for i in range(NCH)]
    gamT = [sbh(l0sc, f"gamT{i}", 128, 32) for i in range(NCH)]
    lamCr = [sbh(l0sc, f"lamC{i}", 128, 32) for i in range(NCH)]

    for k in range(KT):
        dma("gpsimd", wbuf[0][k][:, 0:64], w_ab[k * 128:(k + 1) * 128, :])
    vld = qfence("gpsimd")
    for (toff, tn) in TOKS_OWN:
        wait("tensor", s_gd, vld)
        for k in range(KT):
            rec("tensor", lambda e, k=k, toff=toff, tn=tn:
                e.matmul(pp[0][0:64, 0:tn], wbuf[0][k][:, 0:64],
                         xln[k][:, toff:toff + tn],
                         start=(k == 0), stop=(k == KT - 1)))
        vm = sig("tensor")
        wait("vector", s_mm, vm)
        rec("vector", lambda e, toff=toff, tn=tn:
            e.tensor_copy(ab_fm[:, toff - HALO:toff - HALO + tn], pp[0][0:64, 0:tn]))
        vv = sig("vector")
        wait("tensor", s_v, vv)
    barrier()

    rec("scalar", lambda e: e.activation(g_fm[:], ab_fm[0:32, :], ACT.Abs,
                                         bias=vcol[0:32, 1:2]))
    rec("scalar", lambda e: e.drain()) if SIM_DRAINS else None
    rec("scalar", lambda e: e.activation(g_fm[:], g_fm[:], ACT.Exp, scale=-1.0))
    rec("scalar", lambda e: e.drain()) if SIM_DRAINS else None
    rec("scalar", lambda e: e.activation(g_fm[:], g_fm[:], ACT.Ln,
                                         bias=c_eps[0:32, 2:3]))
    rec("scalar", lambda e: e.drain()) if SIM_DRAINS else None
    rec("scalar", lambda e: e.activation(f_scr[0:32, 0:TC], ab_fm[0:32, :],
                                         ACT.Relu, bias=vcol[0:32, 1:2]))
    rec("scalar", lambda e: e.drain()) if SIM_DRAINS else None
    rec("scalar", lambda e: e.activation(beta0[:], ab_fm[32:64, :], ACT.Sigmoid))
    vs = sig("scalar")
    wait("vector", s_s, vs)
    rec("vector", lambda e: e.tensor_add(g_fm[:], g_fm[:], f_scr[0:32, 0:TC]))
    rec("vector", lambda e: e.drain()) if VEC_DRAINS else None
    rec("vector", lambda e: e.tensor_scalar(g_fm[:], g_fm[:], expA[0:32, 0:1],
                                            -1.0, ALU.mult, ALU.mult))
    rec("vector", lambda e: e.drain()) if VEC_DRAINS else None
    for i in range(NCH):
        rec("vector", lambda e, i=i: e.tensor_tensor_scan(
            bcum[:, i * CH:(i + 1) * CH], g_fm[:, i * CH:(i + 1) * CH],
            zer128[0:32, :], 0.0, ALU.add, ALU.add))
    rec("vector", lambda e: e.drain()) if VEC_DRAINS else None
    vv = sig("vector")
    wait("scalar", s_v, vv)
    rec("scalar", lambda e: e.activation(lam[:], bcum[:], ACT.Exp))
    sig("scalar")

    for nm, t in (("g_fm", g_fm), ("beta0", beta0), ("bcum", bcum)):
        tp = tap(nm, [32, TC])
        if tp is not None:
            wait("sync", s_s, s_s.n)
            wait("sync", s_v, s_v.n)
            dma("sync", tp, t[:], sem=s_st)

    barrier()

    for i in range(NCH):
        rec("tensor", lambda e, i=i: e.transpose(ps_m[:, 0:32],
                                                 bcum[:, i * CH:(i + 1) * CH],
                                                 c_idn32[0:32, 0:32]))
        vm = sig("tensor")
        wait("vector", s_mm, vm)
        rec("vector", lambda e, i=i: e.tensor_copy(bT[i][:], ps_m[:, 0:32]))
        vv = sig("vector")
        wait("tensor", s_v, vv)
        rec("tensor", lambda e, i=i: e.transpose(ps_m[:, 0:32],
                                                 beta0[:, i * CH:(i + 1) * CH],
                                                 c_idn32[0:32, 0:32]))
        vm = sig("tensor")
        wait("vector", s_mm, vm)
        rec("vector", lambda e, i=i: e.tensor_copy(betT[i][:], ps_m[:, 0:32]))
        vv = sig("vector")
        wait("tensor", s_v, vv)
        rec("tensor", lambda e, i=i: e.matmul(ps_m[0:1, 64:96],
                                              bcum[:, (i + 1) * CH - 1:(i + 1) * CH],
                                              c_idn32[0:32, 0:32],
                                              start=True, stop=True))
        vm = sig("tensor")
        wait("vector", s_mm, vm)
        rec("vector", lambda e: e.tensor_copy(trow[:], ps_m[0:1, 64:96]))
        vv = sig("vector")
        wait("tensor", s_v, vv)
        rec("tensor", lambda e, i=i: e.matmul(ps_m[:, 32:64], c_oner[:],
                                              trow[:], start=True, stop=True))
        vm = sig("tensor")
        wait("vector", s_mm, vm)
        rec("vector", lambda e, i=i: e.tensor_tensor(gamT[i][:], ps_m[:, 32:64],
                                                     bT[i][:], ALU.subtract))
        vv = sig("vector")
        wait("scalar", s_v, vv)
        rec("scalar", lambda e, i=i: e.activation(gamT[i][:], gamT[i][:], ACT.Exp))
        rec("scalar", lambda e, i=i: e.activation(lamCr[i][:], ps_m[:, 32:64],
                                                  ACT.Exp))
        vs = sig("scalar")
        wait("tensor", s_s, vs)
        wait("vector", s_s, vs)
    barrier()

    for nm in ("bT3", "gamT3", "lamC3", "betT3"):
        tp = tap(nm, [128, 32])
        if tp is not None:
            src = {"bT3": bT, "gamT3": gamT, "lamC3": lamCr, "betT3": betT}[nm][3]
            dma("sync", tp, src[:], sem=s_st)
    barrier()

    lb0 = g_fm
    rec("vector", lambda e: e.tensor_tensor(lb0[:], lam[:], beta0[:], ALU.mult))
    sig("vector")
    barrier(("vector", "tensor", "scalar"))

    def emit_mixer(layer, gates):
        gated = (layer == 0)
        HV = L0_HV if gated else L1_H
        NG = HV // 2
        msc = contextlib.ExitStack()
        hsel = gates["hsel"]
        HSP = 32 if gated else 16
        L = f"m{layer}_"
        xsq_b = sbh(msc, L + "xsq_b", 128, TC, BF16)
        nheads_k = 1 if gated else 2
        cvi = [sbh(msc, f"{L}cvi{t}", 128, TF, BF16) for t in range(4)]
        qhat = [sbh(msc, f"{L}qhat{j}", 128, TC, BF16) for j in range(nheads_k)]
        khat = [sbh(msc, f"{L}khat{j}", 128, TC, BF16) for j in range(nheads_k)]
        vtm = ([] if gated else
               [sbh(msc, f"{L}vtm{j}", 128, TC, BF16) for j in range(2)])
        st_v = sbh(msc, L + "st_v", 128, 256, BF16)
        scPU = [sbh(msc, f"{L}scPU_{k}", 128, 512, BF16) for k in range(6)]
        scPU.append(sbh(msc, f"{L}scPU_6", 128, 256, BF16))
        scP2 = [scPU[k][:, 0:256] for k in range(7)]
        scU_of = [scPU[k][:, 256:512] for k in range(6)]
        scY2 = sbh(msc, L + "scY2", 128, 512, BF16)
        scN = [scU_of[0][:, 0:128], scU_of[0][:, 128:256]]
        scNT = [scP2[0][:, 0:128], scP2[0][:, 128:256]]
        scY = [scY2[:, 0:256], scY2[:, 256:512]]
        scPT = [sbh(msc, f"{L}scPT{h}", 128, 128, BF16) for h in range(2)]
        scKb = [sbh(msc, f"{L}scKb{h}", 128, 128, BF16) for h in range(2)]
        scQb = [sbh(msc, f"{L}scQb{h}", 128, 128, BF16) for h in range(2)]
        scKt = [sbh(msc, f"{L}scKt{h}", 128, 128, BF16) for h in range(2)]
        Sa2 = sbh(msc, L + "Sa2", 128, 512)
        Sab2 = sbh(msc, L + "Sab2", 128, 512, BF16)
        Sa = [Sa2[:, 0:256], Sa2[:, 256:512]]
        Sab = [Sab2[:, 0:256], Sab2[:, 256:512]]
        sc_m2 = sbh(msc, L + "scm2", 128, 256)
        sc_t2 = sbh(msc, L + "sct2", 128, 256)
        sc_n2 = sbh(msc, L + "scn2", 128, 256)
        sc_e1 = sbh(msc, L + "sce1", 128, 256)
        sc_e2 = sbh(msc, L + "sce2", 128, 256)
        st_o = sbh(msc, L + "st_o", 128, 256, BF16)
        st_c = sbh(msc, L + "st_c", 128, 256, BF16)
        st_p = sbh(msc, L + "st_p", 128, 128)

        w_qkv_l = w_qkv if gated else None

        def tok_all():
            return TOKS_ALL if gated else TOKS_OWN

        for g in range(NG):
            def mk_evac(tile_idx, silu=False, colbase=0):
                def evac(jg, bank, toff, tn):
                    dst = cvi[tile_idx + jg - colbase]
                    if silu:
                        rec("scalar", lambda e, dst=dst, toff=toff, tn=tn, bank=bank:
                            e.activation(dst[:, toff:toff + tn], bank[:, 0:tn],
                                         ACT.Silu))
                    else:
                        rec("vector", lambda e, dst=dst, toff=toff, tn=tn, bank=bank:
                            e.tensor_copy(dst[:, toff:toff + tn], bank[:, 0:tn]))
                return evac

            if gated:
                proj_cols(w_qkv, 128 * g, 128, tok_all(), mk_evac(0), slab_of=1)
                proj_cols(w_qkv, 2048 + 128 * g, 128, tok_all(), mk_evac(1, colbase=0),
                          slab_of=1)
                proj_cols(w_qkv, 4096 + 256 * g, 256, tok_all(), mk_evac(2, colbase=0),
                          slab_of=2)
            else:
                proj_cols(w_q1, 256 * g, 256, tok_all(), mk_evac(0, colbase=0),
                          slab_of=2)
                proj_cols(w_k1, 256 * g, 256, tok_all(), mk_evac(2, colbase=0),
                          slab_of=2)

            barrier(("tensor", "vector", "scalar", "gpsimd"))

            if gated:
                gts = [g, 16 + g, 32 + 2 * g, 33 + 2 * g]
                for t in range(4):
                    ct = gts[t]
                    rec("vector", lambda e, t=t, ct=ct: e.tensor_scalar(
                        f_scr[:, 0:TC], cvi[t][:, 1:1 + TC],
                        cw_sb[:, 4 * ct:4 * ct + 1], None, ALU.mult))
                    rec("vector", lambda e: e.drain()) if VEC_DRAINS else None
                    for i_ in range(1, KW):
                        rec("vector", lambda e, t=t, ct=ct, i_=i_:
                            e.scalar_tensor_tensor(f_scr[:, 0:TC],
                                                   cvi[t][:, 1 + i_:1 + i_ + TC],
                                                   cw_sb[:, 4 * ct + i_:4 * ct + i_ + 1],
                                                   f_scr[:, 0:TC], ALU.mult, ALU.add))
                        rec("vector", lambda e: e.drain()) if VEC_DRAINS else None
                    vv = sig("vector")
                    wait("scalar", s_v, vv)
                    rec("scalar", lambda e, t=t: e.activation(cvi[t][:, HALO:TF],
                                                              f_scr[:, 0:TC],
                                                              ACT.Silu))
                    vs = sig("scalar")
                    wait("vector", s_s, vs)
                barrier(("vector", "scalar", "tensor"))

            for j in range(nheads_k):
                qt = cvi[0 if gated else j]
                kt_ = cvi[1 if gated else 2 + j]
                for (which, tile, qscale) in (("q", qt, True), ("k", kt_, False)):
                    rec("scalar", lambda e, tile=tile: e.activation(
                        xsq_b[:, 0:TC], tile[:, HALO:TF], ACT.Square))
                    vs = sig("scalar")
                    wait("tensor", s_s, vs)
                    for (o_, n_) in _cc(TC, 512):
                        rec("tensor", lambda e, o_=o_, n_=n_: e.matmul(
                            ps_m[0:1, 0:n_], c_onecb[:], xsq_b[:, o_:o_ + n_],
                            start=True, stop=True))
                        vm = sig("tensor")
                        wait("vector", s_mm, vm)
                        rec("vector", lambda e, o_=o_, n_=n_: e.tensor_copy(
                            r_mu[0:1, o_:o_ + n_], ps_m[0:1, 0:n_]))
                        vv = sig("vector")
                        wait("tensor", s_v, vv)
                    wait("scalar", s_v, s_v.n)
                    if gated:
                        rec("scalar", lambda e: e.activation(
                            r_t0[0:1, 0:TC], r_mu[0:1, 0:TC], ACT.Ln,
                            bias=c_eps[0:1, 0:1]))
                        rec("scalar", lambda e: e.drain())
                        rec("scalar", lambda e: e.activation(
                            r_t0[0:1, 0:TC], r_t0[0:1, 0:TC], ACT.Exp,
                            scale=-0.5))
                        vs = sig("scalar")
                        wait("vector", s_s, vs)
                    else:
                        rec("scalar", lambda e: e.activation(
                            r_t0[0:1, 0:TC], r_mu[0:1, 0:TC], ACT.Ln))
                        rec("scalar", lambda e: e.drain())
                        rec("scalar", lambda e: e.activation(
                            r_t0[0:1, 0:TC], r_t0[0:1, 0:TC], ACT.Exp,
                            scale=-0.5))
                        vs = sig("scalar")
                        wait("vector", s_s, vs)
                        rec("vector", lambda e: e.tensor_scalar(
                            r_t0[0:1, 0:TC], r_t0[0:1, 0:TC], 1e12, None,
                            ALU.min))
                    rec("vector", lambda e: e.drain()) if VEC_DRAINS else None
                    if qscale:
                        rec("vector", lambda e: e.tensor_scalar(
                            r_t0[0:1, 0:TC], r_t0[0:1, 0:TC],
                            float(DK) ** -0.5, None, ALU.mult))
                        rec("vector", lambda e: e.drain()) if VEC_DRAINS else None
                    vv = sig("vector")
                    wait("tensor", s_v, vv)
                    dst = (qhat if which == "q" else khat)[j]
                    for (o_, n_) in _cc(TC, 512):
                        rec("tensor", lambda e, o_=o_, n_=n_: e.matmul(
                            pp[0][:, 0:n_], c_oner[:], r_t0[0:1, o_:o_ + n_],
                            start=True, stop=True))
                        vm = sig("tensor")
                        wait("vector", s_mm, vm)
                        rec("vector", lambda e, o_=o_, n_=n_, tile=tile, dst=dst:
                            e.tensor_tensor(dst[:, o_:o_ + n_],
                                            tile[:, HALO + o_:HALO + o_ + n_],
                                            pp[0][:, 0:n_], ALU.mult))
                        vv = sig("vector")
                        wait("tensor", s_v, vv)
            barrier(("tensor", "vector", "scalar"))

            if not gated:
                for h in range(2):
                    vc0 = 128 * (2 * g + h)
                    if h > 0:
                        wait("gpsimd", s_mm, s_mm.n)
                    vldv = load_wslab(w_v1, vc0, 128, wbuf[0], kt=KT)
                    for tt in range(NCH):
                        wait("tensor", s_gd, vldv)
                        for k in range(KT):
                            rec("tensor", lambda e, k=k, tt=tt: e.matmul(
                                pp[1][:, 0:128], xln[k][:, HALO + tt * CH:HALO + (tt + 1) * CH],
                                wbuf[0][k][:, 0:128],
                                start=(k == 0), stop=(k == KT - 1)))
                        vm = sig("tensor")
                        wait("scalar", s_mm, vm)
                        rec("scalar", lambda e, h=h, tt=tt: e.activation(
                            vtm[h][:, tt * CH:(tt + 1) * CH], pp[1][:, 0:128],
                            ACT.Silu))
                        vs = sig("scalar")
                        wait("tensor", s_s, vs)
            barrier(("tensor", "vector", "scalar"))

            if g == 0:
                for nm_, src_ in ((f"qhat_l{layer}", qhat[0]),
                                  (f"khat_l{layer}", khat[0]),
                                  (f"v0_l{layer}", cvi[2] if gated else vtm[0]),
                                  (f"v1_l{layer}", cvi[3] if gated else vtm[1])):
                    tp_ = tap(nm_, [128, src_.shape[1] if hasattr(src_, "shape")
                                    else TF], BF16)
                    if tp_ is not None:
                        wait("sync", s_v, s_v.n)
                        wait("sync", s_s, s_s.n)
                        dma("sync", tp_, src_[:], sem=s_st)
                barrier()

            for h in range(2):
                rec("vector", lambda e, h=h: e.memset(Sa2[:, 256 * h:256 * h + 128], 0.0))
                rec("vector", lambda e, h=h: e.memset(Sab2[:, 256 * h:256 * h + 128], 0.0))
                rec("vector", lambda e, h=h: e.tensor_copy(Sa2[:, 256 * h + 128:256 * (h + 1)],
                                                           c_idn32[:]))
                rec("vector", lambda e, h=h: e.tensor_copy(Sab2[:, 256 * h + 128:256 * (h + 1)],
                                                           c_idn[:]))
            rec("vector", lambda e: e.drain()) if VEC_DRAINS else None
            sig("vector")
            barrier(("tensor", "vector", "scalar"))

            for i in range(NCH):
                hv0 = 2 * g
                tc0 = i * CH
                for j in range(nheads_k):
                    rec("tensor", lambda e, j=j, tc0=tc0: e.matmul(
                        pp[0][:, 256 * j:256 * j + 128], khat[j][:, tc0:tc0 + CH],
                        khat[j][:, tc0:tc0 + CH], start=True, stop=True))
                    rec("tensor", lambda e, j=j, tc0=tc0: e.matmul(
                        pp[0][:, 256 * j + 128:256 * j + 256], khat[j][:, tc0:tc0 + CH],
                        qhat[j][:, tc0:tc0 + CH], start=True, stop=True))
                    rec("tensor", lambda e, j=j, tc0=tc0: e.transpose(
                        ps_tb[:, 128 * j:128 * (j + 1)], khat[j][:, tc0:tc0 + CH],
                        c_idn[:]))
                if gated:
                    for h in range(2):
                        rec("tensor", lambda e, h=h, tc0=tc0: e.transpose(
                            ps_tb[:, 256 + 128 * h:256 + 128 * (h + 1)],
                            cvi[2 + h][:, HALO + tc0:HALO + tc0 + CH], c_idn[:]))
                vmA = sig("tensor")
                wait("vector", s_mm, vmA)
                if gated:
                    rec("vector", lambda e: e.tensor_copy(st_v[:], ps_tb[:, 256:512]))
                    rec("vector", lambda e: e.drain()) if VEC_DRAINS else None
                vvV = sig("vector")
                wait("tensor", s_v, vvV)

                for h in range(2):
                    hh = hv0 + h
                    j = 0 if gated else h
                    psA = pp[0][:, 256 * j:256 * j + 128]
                    psKQ = pp[0][:, 256 * j + 128:256 * j + 256]
                    if gated:
                        rec("tensor", lambda e, hh=hh, tc0=tc0, h=h: e.matmul(
                            pp[1][:, 128 * h:128 * h + 128],
                            hsel[0:HSP, 128 * hh:128 * (hh + 1)],
                            gates["bcum"][:, tc0:tc0 + CH], start=True, stop=True))
                        rec("tensor", lambda e, hh=hh, tc0=tc0, h=h: e.matmul(
                            pp[1][:, 256 + 128 * h:256 + 128 * h + 128],
                            hsel[0:HSP, 128 * hh:128 * (hh + 1)],
                            gates["beta"][:, tc0:tc0 + CH], start=True, stop=True))
                        rec("tensor", lambda e, hh=hh, tc0=tc0, h=h: e.matmul(
                            pp[2][:, 128 * h:128 * h + 128],
                            hsel[0:HSP, 128 * hh:128 * (hh + 1)],
                            gates["lb"][:, tc0:tc0 + CH], start=True, stop=True))
                        rec("tensor", lambda e, hh=hh, tc0=tc0, h=h: e.matmul(
                            pp[2][:, 256 + 128 * h:256 + 128 * h + 128],
                            hsel[0:HSP, 128 * hh:128 * (hh + 1)],
                            gates["lam"][:, tc0:tc0 + CH], start=True, stop=True))
                    else:
                        rec("tensor", lambda e, hh=hh, tc0=tc0, h=h: e.matmul(
                            pp[1][:, 256 + 128 * h:256 + 128 * h + 128],
                            hsel[0:HSP, 128 * hh:128 * (hh + 1)],
                            gates["beta"][:, tc0:tc0 + CH], start=True, stop=True))
                vmR = sig("tensor")
                wait("vector", s_mm, vmR)
                wait("scalar", s_mm, vmR)

                D2 = lambda: (rec("vector", lambda e: e.drain())
                              if VEC_DRAINS else None)
                psA_ = [pp[0][:, 256 * (0 if gated else h_):
                              256 * (0 if gated else h_) + 128] for h_ in range(2)]
                psKQ_ = [pp[0][:, 256 * (0 if gated else h_) + 128:
                               256 * (0 if gated else h_) + 256] for h_ in range(2)]
                if gated:
                    for h in range(2):
                        rec("vector", lambda e, h=h,
                            bTc=gates["bT"][i][:, (hv0 + h):(hv0 + h) + 1]:
                            e.scalar_tensor_tensor(
                                sc_m2[:, 128 * h:128 * (h + 1)],
                                pp[1][:, 128 * h:128 * h + 128], bTc,
                                zer128[:], ALU.subtract, ALU.subtract))
                    D2()
                    rec("vector", lambda e: e.tensor_scalar(
                        sc_t2[:], sc_m2[:], 0.0, None, ALU.max))
                    rec("vector", lambda e: e.tensor_scalar(
                        sc_n2[:], sc_m2[:], 0.0, None, ALU.min))
                    D2()
                    vv = sig("vector")
                    wait("scalar", s_v, vv)
                    rec("scalar", lambda e: e.activation(sc_e1[:], sc_t2[:],
                                                         ACT.Exp, scale=-1.0))
                    rec("scalar", lambda e: e.drain()) if SIM_DRAINS else None
                    rec("scalar", lambda e: e.activation(sc_e2[:], sc_n2[:],
                                                         ACT.Exp))
                    vs = sig("scalar")
                    wait("vector", s_s, vs)
                    for h in range(2):
                        rec("vector", lambda e, h=h: e.tensor_tensor(
                            sc_t2[:, 128 * h:128 * (h + 1)],
                            sc_e1[:, 128 * h:128 * (h + 1)], psA_[h], ALU.mult))
                    for h in range(2):
                        rec("vector", lambda e, h=h: e.tensor_tensor(
                            sc_n2[:, 128 * h:128 * (h + 1)],
                            sc_e2[:, 128 * h:128 * (h + 1)], psA_[h], ALU.mult))
                    D2()
                    for h in range(2):
                        rec("vector", lambda e, h=h,
                            beTc=gates["betT"][i][:, (hv0 + h):(hv0 + h) + 1]:
                            e.scalar_tensor_tensor(
                                scN[h], sc_t2[:, 128 * h:128 * (h + 1)], beTc,
                                c_striln[:], ALU.mult, ALU.mult))
                    for h in range(2):
                        rec("vector", lambda e, h=h: e.tensor_tensor(
                            sc_n2[:, 128 * h:128 * (h + 1)],
                            sc_n2[:, 128 * h:128 * (h + 1)],
                            pp[1][:, 256 + 128 * h:256 + 128 * h + 128], ALU.mult))
                    D2()
                    for h in range(2):
                        rec("vector", lambda e, h=h: e.tensor_tensor(
                            scNT[h], sc_n2[:, 128 * h:128 * (h + 1)],
                            c_striun[:], ALU.mult))
                    for h in range(2):
                        rec("vector", lambda e, h=h: e.tensor_tensor(
                            sc_t2[:, 128 * h:128 * (h + 1)],
                            sc_e2[:, 128 * h:128 * (h + 1)], psKQ_[h], ALU.mult))
                    D2()
                    for h in range(2):
                        rec("vector", lambda e, h=h: e.tensor_tensor(
                            scPT[h][:], sc_t2[:, 128 * h:128 * (h + 1)],
                            c_triu1[:], ALU.mult))
                else:
                    for h in range(2):
                        rec("vector", lambda e, h=h,
                            beTc=gates["betT"][i][:, (hv0 + h):(hv0 + h) + 1]:
                            e.scalar_tensor_tensor(scN[h], psA_[h], beTc,
                                                   c_striln[:], ALU.mult, ALU.mult))
                    for h in range(2):
                        rec("vector", lambda e, h=h: e.tensor_tensor(
                            sc_n2[:, 128 * h:128 * (h + 1)], psA_[h],
                            c_striun[:], ALU.mult))
                    D2()
                    for h in range(2):
                        rec("vector", lambda e, h=h: e.tensor_tensor(
                            scNT[h], sc_n2[:, 128 * h:128 * (h + 1)],
                            pp[1][:, 256 + 128 * h:256 + 128 * h + 128], ALU.mult))
                    for h in range(2):
                        rec("vector", lambda e, h=h: e.tensor_tensor(
                            scPT[h][:], psKQ_[h], c_triu1[:], ALU.mult))
                for h in range(2):
                    src_rep = (pp[2][:, 128 * h:128 * h + 128] if gated
                               else pp[1][:, 256 + 128 * h:256 + 128 * h + 128])
                    kj = khat[0 if gated else h]
                    rec("vector", lambda e, h=h, kj=kj, tc0=tc0, src_rep=src_rep:
                        e.tensor_tensor(scKb[h][:], kj[:, tc0:tc0 + CH], src_rep,
                                        ALU.mult))
                D2()
                for h in range(2):
                    jk = 0 if gated else h
                    if gated:
                        rec("vector", lambda e, h=h, jk=jk, hh=hv0 + h, i=i:
                            e.tensor_scalar(scKt[h][:],
                                            ps_tb[:, 128 * jk:128 * (jk + 1)],
                                            gates["gamT"][i][:, hh:hh + 1], None,
                                            ALU.mult))
                    else:
                        rec("vector", lambda e, h=h, jk=jk:
                            e.tensor_copy(scKt[h][:],
                                          ps_tb[:, 128 * jk:128 * (jk + 1)]))
                if not gated:
                    D2()
                if gated:
                    for h in range(2):
                        rec("vector", lambda e, h=h, qj=qhat[0], tc0=tc0:
                            e.tensor_tensor(scQb[h][:], qj[:, tc0:tc0 + CH],
                                            pp[2][:, 256 + 128 * h:256 + 128 * h + 128],
                                            ALU.mult))
                    D2()
                vv = sig("vector")
                wait("tensor", s_v, vv)

                for h in range(2):
                    rec("tensor", lambda e, h=h: e.matmul(
                        pp[4][:, 256 * h:256 * h + 256], scKb[h][:], Sab[h],
                        start=True, stop=True))
                vmRHS = sig("tensor")
                wait("vector", s_mm, vmRHS)
                for h in range(2):
                    vsrc = (st_v[:, 128 * h:128 * (h + 1)] if gated
                            else vtm[h][:, tc0:tc0 + CH])
                    rec("vector", lambda e, h=h, vsrc=vsrc,
                        bsl=gates["betT"][i][:, (2 * g + h):(2 * g + h) + 1]:
                        e.scalar_tensor_tensor(
                            scY2[:, 256 * h:256 * h + 128], vsrc, bsl,
                            pp[4][:, 256 * h:256 * h + 128], ALU.mult,
                            ALU.subtract))
                    rec("vector", lambda e, h=h: e.tensor_scalar(
                        scY2[:, 256 * h + 128:256 * (h + 1)], pp[4][:, 256 * h + 128:256 * h + 256],
                        -1.0, None, ALU.mult))
                rec("vector", lambda e: e.drain()) if VEC_DRAINS else None
                vv = sig("vector")
                wait("tensor", s_v, vv)

                if g == 0 and i == 0:
                    for nm_, src_ in ((f"scR_l{layer}", scY[0]),
                                      (f"stv_l{layer}", st_v)):
                        tp_ = tap(nm_, [128, 256], BF16)
                        if tp_ is not None:
                            wait("sync", s_v, s_v.n)
                            dma("sync", tp_, src_[:], sem=s_st)
                    if tapd_any[0]:
                        wait("sync", s_st, s_st.n)
                        barrier(("tensor", "vector", "scalar", "sync"))

                for k_ in range(6):
                    for h in range(2):
                        rec("tensor", lambda e, h=h, k_=k_: e.matmul(
                            pp[2][:, 128 * h:128 * h + 128],
                            scU_of[k_][:, 128 * h:128 * h + 128],
                            scP2[k_][:, 128 * h:128 * h + 128],
                            start=True, stop=True))
                        if k_ < 5:
                            rec("tensor", lambda e, h=h, k_=k_: e.matmul(
                                pp[2][:, 256 + 128 * h:256 + 128 * h + 128],
                                scP2[k_][:, 128 * h:128 * h + 128],
                                scU_of[k_][:, 128 * h:128 * h + 128],
                                start=True, stop=True))
                    vm = sig("tensor")
                    wait("vector", s_mm, vm)
                    rec("vector", lambda e, k_=k_: e.tensor_copy(
                        scPU[k_ + 1][:, 0:(512 if k_ < 5 else 256)],
                        pp[2][:, 0:(512 if k_ < 5 else 256)]))
                    rec("vector", lambda e: e.drain()) if VEC_DRAINS else None
                    vv = sig("vector")
                    wait("tensor", s_v, vv)
                for k_ in range(7):
                    for h in range(2):
                        rec("tensor", lambda e, h=h, k_=k_: e.matmul(
                            pp[3][:, 256 * h:256 * h + 256],
                            scP2[k_][:, 128 * h:128 * h + 128],
                            scY[h], start=True, stop=True))
                    vm = sig("tensor")
                    wait("vector", s_mm, vm)
                    rec("vector", lambda e: e.tensor_tensor(
                        scY2[:], scY2[:], pp[3][:, 0:512], ALU.add))
                    rec("vector", lambda e: e.drain()) if VEC_DRAINS else None
                    vv = sig("vector")
                    wait("tensor", s_v, vv)

                if g == 0 and i == 0:
                    for nm_, src_ in ((f"scN_l{layer}", scN[0]),
                                      (f"scNT_l{layer}", scNT[0]),
                                      (f"scPT_l{layer}", scPT[0]),
                                      (f"scY_l{layer}", scY[0]),
                                      (f"scKb_l{layer}", scKb[0]),
                                      (f"scKt_l{layer}", scKt[0]),
                                      (f"scQb_l{layer}", scQb[0])):
                        tp_ = tap(nm_, [128, 256 if src_ is scY[0] else 128], BF16)
                        if tp_ is not None:
                            wait("sync", s_v, s_v.n)
                            dma("sync", tp_, src_[:], sem=s_st)
                    if tapd_any[0]:
                        wait("sync", s_st, s_st.n)
                        barrier(("tensor", "vector", "scalar", "sync"))

                for h in range(2):
                    qb = (scQb[h][:] if gated
                          else qhat[h][:, tc0:tc0 + CH])
                    rec("tensor", lambda e, h=h, qb=qb: e.matmul(
                        ps_m[:, 128 * h:128 * h + 128],
                        Sab2[:, 256 * h:256 * h + 128], qb,
                        start=True, stop=False))
                    rec("tensor", lambda e, h=h: e.matmul(
                        ps_m[:, 128 * h:128 * h + 128],
                        scY2[:, 256 * h:256 * h + 128], scPT[h][:],
                        start=False, stop=True))
                    rec("tensor", lambda e, h=h, qb=qb: e.matmul(
                        ps_m[:, 256 + 128 * h:256 + 128 * h + 128],
                        Sab2[:, 256 * h + 128:256 * (h + 1)],
                        qb, start=True, stop=False))
                    rec("tensor", lambda e, h=h: e.matmul(
                        ps_m[:, 256 + 128 * h:256 + 128 * h + 128],
                        scY2[:, 256 * h + 128:256 * (h + 1)],
                        scPT[h][:], start=False, stop=True))
                    rec("tensor", lambda e, h=h: e.matmul(
                        pp[5][:, 256 * h:256 * h + 256], scKt[h][:], scY[h],
                        start=True, stop=True))
                vmO = sig("tensor")
                wait("vector", s_mm, vmO)
                rec("vector", lambda e: e.tensor_copy(st_o[:], ps_m[:, 0:256]))
                rec("vector", lambda e: e.tensor_copy(st_c[:], ps_m[:, 256:512]))
                if gated:
                    for h in range(2):
                        rec("vector", lambda e, h=h, i=i, hh=2 * g + h:
                            e.scalar_tensor_tensor(Sa[h], Sa[h],
                                                   gates["lamCr"][i][:, hh:hh + 1],
                                                   pp[5][:, 256 * h:256 * h + 256],
                                                   ALU.mult, ALU.add))
                else:
                    rec("vector", lambda e: e.tensor_tensor(
                        Sa2[:], Sa2[:], pp[5][:, 0:512], ALU.add))
                rec("vector", lambda e: e.drain()) if VEC_DRAINS else None
                rec("vector", lambda e: e.tensor_copy(Sab2[:], Sa2[:]))
                rec("vector", lambda e: e.drain()) if VEC_DRAINS else None
                vv = sig("vector")
                wait("sync", s_v, vv)
                for h in range(2):
                    hh = 2 * g + h
                    dma("sync", o0_dram[layer].ap()[hh, :, tc0:tc0 + CH],
                        st_o[:, 128 * h:128 * h + 128], sem=s_st)
                    dma("sync", cc_dram[layer].ap()[hh, i, :, :],
                        st_c[:, 128 * h:128 * h + 128], sem=s_st)
                wait("tensor", s_st, s_st.n)
                wait("vector", s_st, s_st.n)

            for h in range(2):
                hh = 2 * g + h
                rec("tensor", lambda e, h=h: e.transpose(
                    ps_m[:, 0:128], Sa2[:, 256 * h + 128:256 * (h + 1)], c_idn32[:]))
                vm = sig("tensor")
                wait("vector", s_mm, vm)
                rec("vector", lambda e: e.tensor_copy(st_p[:], ps_m[:, 0:128]))
                vv = sig("vector")
                wait("sync", s_v, vv)
                wait("tensor", s_v, vv)
                dma("sync", agg_in[layer].ap()[:, 384 * hh:384 * hh + 256],
                    Sa[h], sem=s_st)
                dma("sync", agg_in[layer].ap()[:, 384 * hh + 256:384 * (hh + 1)],
                    st_p[:], sem=s_st)
                wait("vector", s_st, s_st.n)
            barrier(("tensor", "vector", "scalar", "gpsimd", "sync"))

        msc.close()
        return None


    def emit_scan_out(layer):
        gated = (layer == 0)
        HV = L0_HV if gated else L1_H
        eps_sl = c_eps[0:1, 0:1] if gated else c_eps[0:1, 1:2]
        gnw = gnw0 if gated else gnw1
        wout = w_out0 if gated else w_o1
        ssc = contextlib.ExitStack()
        L = f"s{layer}_"
        Tbuf = sbh(ssc, L + "Tbuf", 128, HV * 128, BF16)
        Jsl = sbh(ssc, L + "Jsl", 128, 512, BF16)
        Psl = sbh(ssc, L + "Psl", 128, 512, BF16)
        bl1 = sbh(ssc, L + "bl1", 128, 512)
        bl2 = sbh(ssc, L + "bl2", 128, 512)

        wait("gpsimd", s_st, s_st.n)
        vcc = s_cc.bump(1)
        rec("gpsimd", lambda e: e.collective_compute(
            "AllGather", ALU.bypass, replica_groups=[list(range(NCORES))],
            ins=[agg_in[layer].ap().opt()], outs=[agg_out[layer].ap().opt()])
            .then_inc(s_cc.h, 1), can_inc=False)
        wait("gpsimd", s_cc, vcc)
        sig("gpsimd")
        barrier()

        rec("vector", lambda e: e.memset(Tbuf[:], 0.0))
        sig("vector")
        barrier(("vector", "tensor", "gpsimd"))
        ag3 = agg_out[layer].ap().rearrange("r a (h c) -> r a h c", c=384)
        for p_ in range(NCORES - 1):
            for g4 in range(HV // 4):
                vj = dma("gpsimd", Jsl[:].rearrange("a (h c) -> a h c", c=128),
                         ag3[p_, :, 4 * g4:4 * g4 + 4, 0:128])
                vph = dma("gpsimd", Psl[:].rearrange("a (h c) -> a h c", c=128),
                          ag3[p_, :, 4 * g4:4 * g4 + 4, 256:384])
                wait("tensor", s_gd, vph)
                for r in range(4):
                    hh = 4 * g4 + r
                    rec("tensor", lambda e, hh=hh, r=r: e.matmul(
                        pp[0][:, 128 * r:128 * (r + 1)],
                        Psl[:, 128 * r:128 * (r + 1)],
                        Tbuf[:, 128 * hh:128 * (hh + 1)], start=True, stop=True))
                vm = sig("tensor")
                wait("vector", s_mm, vm)
                c0_, c1_ = 512 * g4, 512 * (g4 + 1)
                rec("vector", lambda e, c0_=c0_, c1_=c1_: e.tensor_tensor(
                    bl1[:], pp[0][:, 0:512], Jsl[:], ALU.add))
                rec("vector", lambda e: e.drain()) if VEC_DRAINS else None
                rec("vector", lambda e, c0_=c0_, c1_=c1_: e.tensor_tensor(
                    bl2[:], bl1[:], Tbuf[:, c0_:c1_], ALU.subtract))
                rec("vector", lambda e: e.drain()) if VEC_DRAINS else None
                rec("vector", lambda e, p_=p_, c0_=c0_, c1_=c1_:
                    e.scalar_tensor_tensor(Tbuf[:, c0_:c1_], bl2[:],
                                           c_pmask[:, p_:p_ + 1],
                                           Tbuf[:, c0_:c1_], ALU.mult, ALU.add))
                rec("vector", lambda e: e.drain()) if VEC_DRAINS else None
                vv = sig("vector")
                wait("tensor", s_v, vv)
                wait("gpsimd", s_v, vv)
            barrier(("tensor", "vector", "gpsimd"))

        ob = sbh(ssc, L + "ob", 128, TC, BF16)
        ccb = sbh(ssc, L + "ccb", 128, TC, BF16)
        o_wrk = sbh(ssc, L + "o_wrk", 128, TC)
        onrm = [sbh(ssc, L + f"onrm{j}", 128, TC, BF16) for j in range(8)]
        zsil = sbh(ssc, L + "zsil", 128, 512, BF16)
        cc3 = cc_dram[layer].ap()

        for b8 in range(HV // 8):
            for j in range(8):
                hh = 8 * b8 + j
                v1 = dma("sync", ob[:], o0_dram[layer].ap()[hh])
                v2 = dma("sync", ccb[:].rearrange("a (i c) -> a i c", c=128),
                         cc3[hh].rearrange("i a c -> a i c"))
                wait("tensor", s_ld, v2)
                for i in range(NCH):
                    rec("tensor", lambda e, i=i, hh=hh: e.matmul(
                        pp[i // 4][:, 128 * (i % 4):128 * (i % 4 + 1)],
                        Tbuf[:, 128 * hh:128 * (hh + 1)],
                        ccb[:, 128 * i:128 * (i + 1)], start=True, stop=True))
                vm = sig("tensor")
                wait("vector", s_mm, vm)
                wait("vector", s_ld, v2)
                rec("vector", lambda e: e.tensor_tensor(
                    o_wrk[:, 0:512], ob[:, 0:512], pp[0][:, 0:512], ALU.add))
                rec("vector", lambda e: e.tensor_tensor(
                    o_wrk[:, 512:1024], ob[:, 512:1024], pp[1][:, 0:512], ALU.add))
                rec("vector", lambda e: e.drain()) if VEC_DRAINS else None
                vv = sig("vector")
                wait("scalar", s_v, vv)
                wait("sync", s_v, vv)
                rec("scalar", lambda e: e.activation(f_scr[:, 0:TC], o_wrk[:],
                                                     ACT.Square))
                vs = sig("scalar")
                wait("tensor", s_s, vs)
                for ci, (o_, n_) in enumerate(_cc(TC, 512)):
                    rec("tensor", lambda e, o_=o_, n_=n_, ci=ci: e.matmul(
                        pp[2][0:1, 256 * ci:256 * ci + n_ if False else 0:n_]
                        if False else pp[2 + ci][0:1, 0:n_],
                        c_onec[:], f_scr[:, o_:o_ + n_], start=True, stop=True))
                vm = sig("tensor")
                wait("vector", s_mm, vm)
                rec("vector", lambda e: e.tensor_copy(r_mu[0:1, 0:512], pp[2][0:1, 0:512]))
                rec("vector", lambda e: e.tensor_copy(r_mu[0:1, 512:1024],
                                                      pp[3][0:1, 0:512]))
                rec("vector", lambda e: e.drain()) if VEC_DRAINS else None
                rec("vector", lambda e: e.tensor_scalar(r_mu[0:1, 0:TC], r_mu[0:1, 0:TC],
                                                        1.0 / DV, None, ALU.mult))
                vv = sig("vector")
                wait("scalar", s_v, vv)
                rec("scalar", lambda e: e.activation(r_t0[0:1, 0:TC],
                                                     r_mu[0:1, 0:TC], ACT.Sqrt,
                                                     bias=eps_sl))
                vs = sig("scalar")
                wait("vector", s_s, vs)
                rec("vector", lambda e: e.reciprocal(r_t0[0:1, 0:TC],
                                                     r_t0[0:1, 0:TC]))
                rec("vector", lambda e: e.drain()) if VEC_DRAINS else None
                vv = sig("vector")
                wait("tensor", s_v, vv)
                for ci, (o_, n_) in enumerate(_cc(TC, 512)):
                    rec("tensor", lambda e, o_=o_, n_=n_, ci=ci: e.matmul(
                        pp[2 + ci][:, 0:n_], c_oner[:],
                        r_t0[0:1, o_:o_ + n_], start=True, stop=True))
                vm = sig("tensor")
                wait("vector", s_mm, vm)
                for ci, (o_, n_) in enumerate(_cc(TC, 512)):
                    rec("vector", lambda e, j=j, o_=o_, n_=n_, ci=ci:
                        e.scalar_tensor_tensor(onrm[j][:, o_:o_ + n_],
                                               o_wrk[:, o_:o_ + n_], gnw[:],
                                               pp[2 + ci][:, 0:n_],
                                               ALU.mult, ALU.mult))
                rec("vector", lambda e: e.drain()) if VEC_DRAINS else None
                vv = sig("vector")
                wait("tensor", s_v, vv)
                wait("scalar", s_v, vv)
            barrier(("tensor", "vector", "scalar", "gpsimd"))

            if gated:
                def evac_z(jg, bank, toff, tn):
                    wait("scalar", s_v, s_v.n)
                    rec("scalar", lambda e, bank=bank, tn=tn: e.activation(
                        zsil[:, 0:tn], bank[:, 0:tn], ACT.Silu))
                    vs_ = sig("scalar")
                    wait("vector", s_s, vs_)
                    rec("vector", lambda e, jg=jg, toff=toff, tn=tn: e.tensor_tensor(
                        onrm[jg][:, toff - HALO:toff - HALO + tn],
                        onrm[jg][:, toff - HALO:toff - HALO + tn],
                        zsil[:, 0:tn], ALU.mult))
                proj_cols(w_z, 1024 * b8, 1024, TOKS_OWN, evac_z, slab_of=4)

            wrows = wout[1024 * b8:1024 * (b8 + 1), :] if gated else \
                wout[1024 * b8:1024 * (b8 + 1), :]
            def evac_out(jg, bank, toff, tn):
                rec("vector", lambda e, jg=jg, bank=bank, toff=toff, tn=tn:
                    e.tensor_tensor(x_sb[jg][:, toff:toff + tn],
                                    x_sb[jg][:, toff:toff + tn],
                                    bank[:, 0:tn], ALU.add))
            proj_cols(wrows, 0, D, TOKS_OWN, evac_out, kt=8, src=onrm,
                      slab_of=4, src_off=-HALO)
            barrier(("tensor", "vector", "scalar", "gpsimd"))
        ssc.close()

    tern_cc = [0]

    def emit_ternary():
        tsc = contextlib.ExitStack()
        tw = sbh(tsc, "t_w", 128, 1024)
        tq = sbh(tsc, "t_q", 128, 1024, BF16)
        tcmp = sbh(tsc, "t_c", 128, 1024)
        tacc = sbh(tsc, "t_acc", 128, 8)
        tred = sbh(tsc, "t_red", 128, 1)
        t32a = sbh(tsc, "t32a", 32, 32)
        t32b = sbh(tsc, "t32b", 32, 32)
        throw = sbh(tsc, "throw", 1, 8)
        scrow = sbh(tsc, "scrow", 1, 8)
        thcol = {w: sbh(tsc, f"th_{w}", 128, 2)
                 for w in ("ffn0", "up", "gate", "down")}
        WDEFS = [("ffn0", wT_ffn0, D, NSH0), ("up", wT_up, D, NSHE),
                 ("gate", wT_gate, D, NSHE), ("down", wT_down, DE, NSHD)]
        rec("vector", lambda e: e.memset(tacc[:], 0.0))
        rec("vector", lambda e: e.memset(t32a[:], 0.0))
        rec("vector", lambda e: e.memset(t32b[:], 0.0))
        rec("vector", lambda e: e.memset(throw[:], 0.0))
        rec("vector", lambda e: e.memset(scrow[:, 4:8], 0.0))
        rec("vector", lambda e: e.memset(scrow[:, 0:1], 1.0 / (2048.0 * 2048.0)))
        rec("vector", lambda e: e.memset(scrow[:, 1:2], 1.0 / (DE * D)))
        rec("vector", lambda e: e.memset(scrow[:, 2:3], 1.0 / (DE * D)))
        rec("vector", lambda e: e.memset(scrow[:, 3:4], 1.0 / (D * DE)))
        sig("vector")
        barrier(("vector", "sync", "gpsimd", "scalar", "tensor"))
        for wi, (nm, wd, nr, ncol) in enumerate(WDEFS):
            for kr in range(nr // 128):
                for (o_, n_) in _cc(ncol, 1024):
                    v1 = dma("sync", tw[:, 0:n_],
                             wd[kr * 128:(kr + 1) * 128, o_:o_ + n_])
                    wait("vector", s_ld, v1)
                    rec("vector", lambda e, n_=n_: e.tensor_reduce(
                        tred[:], tw[:, 0:n_], AX, ALU.add,
                        apply_absolute_value=True))
                    rec("vector", lambda e: e.drain()) if VEC_DRAINS else None
                    rec("vector", lambda e, wi=wi: e.tensor_tensor(
                        tacc[:, wi:wi + 1], tacc[:, wi:wi + 1], tred[:], ALU.add))
                    rec("vector", lambda e: e.drain()) if VEC_DRAINS else None
                    vv = sig("vector")
                    wait("sync", s_v, vv)
        vv = sig("vector")
        wait("tensor", s_v, vv)
        rec("tensor", lambda e: e.matmul(ps_m[0:1, 0:8], c_onec[:], tacc[:, 0:8],
                                         start=True, stop=True))
        vm = sig("tensor")
        wait("vector", s_mm, vm)
        rec("vector", lambda e: e.tensor_copy(throw[:], ps_m[0:1, 0:8]))
        rec("vector", lambda e: e.drain()) if VEC_DRAINS else None
        vv = sig("vector")
        wait("sync", s_v, vv)
        dma("sync", tern_in.ap(), throw[:])
        wait("gpsimd", s_ld, s_ld.n)
        vcc = s_cc.bump(1)
        rec("gpsimd", lambda e: e.collective_compute(
            "AllReduce", ALU.add, replica_groups=[list(range(NCORES))],
            ins=[tern_in.ap().opt()], outs=[tern_out.ap().opt()])
            .then_inc(s_cc.h, 1), can_inc=False)
        wait("gpsimd", s_cc, vcc)
        dma("gpsimd", throw[:], tern_out.ap())
        vg = gfence()
        wait("vector", s_g, vg)
        rec("vector", lambda e: e.tensor_tensor(throw[:, 0:4], throw[:, 0:4],
                                                scrow[:, 0:4], ALU.mult))
        rec("vector", lambda e: e.drain()) if VEC_DRAINS else None
        rec("vector", lambda e: e.tensor_scalar(throw[:, 0:1], throw[:, 0:1],
                                                1e-5, None, ALU.max))
        rec("vector", lambda e: e.drain()) if VEC_DRAINS else None
        vv = sig("vector")
        wait("tensor", s_v, vv)
        for wi, (nm, wd, nr, ncol) in enumerate(WDEFS):
            rec("tensor", lambda e, wi=wi: e.matmul(
                ps_m[:, 2 * wi:2 * wi + 1], c_oner[:], throw[0:1, wi:wi + 1],
                start=True, stop=True))
        vm = sig("tensor")
        wait("vector", s_mm, vm)
        for wi, (nm, wd, nr, ncol) in enumerate(WDEFS):
            eps_add = 0.0 if wi == 0 else 1e-8
            rec("vector", lambda e, wi=wi, nm=nm, eps_add=eps_add:
                e.tensor_scalar(thcol[nm][:, 0:1], ps_m[:, 2 * wi:2 * wi + 1],
                                eps_add, 0.5, ALU.add, ALU.mult))
            rec("vector", lambda e: e.drain()) if VEC_DRAINS else None
            rec("vector", lambda e, nm=nm: e.tensor_scalar(
                thcol[nm][:, 1:2], thcol[nm][:, 0:1], -1.0, None, ALU.mult))
            rec("vector", lambda e: e.drain()) if VEC_DRAINS else None
        vv = sig("vector")
        wait("sync", s_v, vv)
        barrier()
        for wi, (nm, wd, nr, ncol) in enumerate(WDEFS):
            for kr in range(nr // 128):
                for (o_, n_) in _cc(ncol, 1024):
                    v1 = dma("sync", tw[:, 0:n_],
                             wd[kr * 128:(kr + 1) * 128, o_:o_ + n_])
                    wait("vector", s_ld, v1)
                    rec("vector", lambda e, nm=nm, n_=n_: e.tensor_scalar(
                        tcmp[:, 0:n_], tw[:, 0:n_], thcol[nm][:, 1:2], None,
                        ALU.is_lt))
                    rec("vector", lambda e: e.drain()) if VEC_DRAINS else None
                    rec("vector", lambda e, nm=nm, n_=n_: e.scalar_tensor_tensor(
                        tq[:, 0:n_], tw[:, 0:n_], thcol[nm][:, 0:1],
                        tcmp[:, 0:n_], ALU.is_gt, ALU.subtract))
                    rec("vector", lambda e: e.drain()) if VEC_DRAINS else None
                    vv = sig("vector")
                    wait("sync", s_v, vv)
                    dma("sync", tq_in[nm].ap()[kr * 128:(kr + 1) * 128, o_:o_ + n_],
                        tq[:, 0:n_], sem=s_st)
                    wait("vector", s_st, s_st.n)
        wait("gpsimd", s_st, s_st.n)
        for nm in ("ffn0", "up", "gate", "down"):
            vcc = s_cc.bump(1)
            rec("gpsimd", lambda e, nm=nm: e.collective_compute(
                "AllGather", ALU.bypass, replica_groups=[list(range(NCORES))],
                ins=[tq_in[nm].ap().opt()], outs=[tq_out[nm].ap().opt()])
                .then_inc(s_cc.h, 1), can_inc=False)
        tern_cc[0] = s_cc.n
        sig("gpsimd")
        barrier()
        tsc.close()

    def emit_ffn0():
        fsc = contextlib.ExitStack()
        gcol = sbh(fsc, "f0_g", 128, KT)
        xnrow = sbh(fsc, "f0_row", 1, TC)
        amrow = sbh(fsc, "f0_am", 128, 2)
        arow8 = sbh(fsc, "f0_a8", 1, 8)
        t32a = sbh(fsc, "f0_32a", 32, 32)
        t32b = sbh(fsc, "f0_32b", 32, 32)
        sc2 = sbh(fsc, "f0_sc", 128, 2)
        xq = sbh(fsc, "f0_xq", 128, TC, BF16)
        rs1 = sbh(fsc, "f0_rs1", 128, TC)
        rs2 = sbh(fsc, "f0_rs2", 128, TC)
        wtile = sbh(fsc, "f0_wt", 128, 512, BF16)
        for i in range(KT):
            v1 = dma("sync", gcol[:, i:i + 1], g_ffn0[i * 128:(i + 1) * 128, :])
        emit_ln(HALO, TC)
        emit_xln(HALO, TC)
        barrier()
        rec("vector", lambda e: e.tensor_scalar(xnrow[:], r_t0[0:1, HALO:TF],
                                                1e-5, None, ALU.add))
        rec("vector", lambda e: e.drain()) if VEC_DRAINS else None
        rec("vector", lambda e: e.reciprocal(xnrow[:], xnrow[:]))
        rec("vector", lambda e: e.drain()) if VEC_DRAINS else None
        rec("vector", lambda e: e.tensor_tensor(xnrow[:], xnrow[:],
                                                r_t0[0:1, HALO:TF], ALU.mult))
        rec("vector", lambda e: e.drain()) if VEC_DRAINS else None
        vv = sig("vector")
        wait("scalar", s_v, vv)
        rec("scalar", lambda e: e.activation(xnrow[:], xnrow[:], ACT.Sqrt,
                                             bias=c_eps[0:1, 0:1]))
        vs = sig("scalar")
        wait("vector", s_s, vs)
        rec("vector", lambda e: e.reciprocal(xnrow[:], xnrow[:]))
        rec("vector", lambda e: e.drain()) if VEC_DRAINS else None
        vv = sig("vector")
        wait("tensor", s_v, vv)
        rec("vector", lambda e: e.memset(amrow[:, 0:1], 0.0))
        sig("vector")
        for ci, (o_, n_) in enumerate(_cc(TC, 512, HALO)):
            rec("tensor", lambda e, o_=o_, n_=n_, ci=ci: e.matmul(
                pp[4 + ci][:, 0:n_], c_oner[:], xnrow[0:1, o_ - HALO:o_ - HALO + n_],
                start=True, stop=True))
        vm = sig("tensor")
        wait("vector", s_mm, vm)
        for i in range(KT):
            for ci, (o_, n_) in enumerate(_cc(TC, 512, HALO)):
                rec("vector", lambda e, i=i, o_=o_, n_=n_, ci=ci: e.tensor_tensor(
                    f_scr[:, o_:o_ + n_], xln[i][:, o_:o_ + n_],
                    pp[4 + ci][:, 0:n_], ALU.mult))
            rec("vector", lambda e: e.drain()) if VEC_DRAINS else None
            rec("vector", lambda e, i=i: e.tensor_scalar(
                f_scr[:, HALO:TF], f_scr[:, HALO:TF], gcol[:, i:i + 1], None,
                ALU.mult))
            rec("vector", lambda e: e.drain()) if VEC_DRAINS else None
            rec("vector", lambda e: e.tensor_reduce(
                amrow[:, 1:2], f_scr[:, HALO:TF], AX, ALU.max,
                apply_absolute_value=True))
            rec("vector", lambda e: e.drain()) if VEC_DRAINS else None
            rec("vector", lambda e: e.tensor_tensor(
                amrow[:, 0:1], amrow[:, 0:1], amrow[:, 1:2], ALU.max))
            rec("vector", lambda e: e.drain()) if VEC_DRAINS else None
        vv = sig("vector")
        wait("tensor", s_v, vv)
        rec("tensor", lambda e: e.transpose(ps_m[0:1, 0:128], amrow[:, 0:1],
                                            c_idn32[:]))
        vm = sig("tensor")
        wait("vector", s_mm, vm)
        rec("vector", lambda e: e.memset(arow8[:], 0.0))
        rec("vector", lambda e: e.drain()) if VEC_DRAINS else None
        rec("vector", lambda e: e.tensor_reduce(arow8[:, 0:1], ps_m[0:1, 0:128],
                                                AX, ALU.max))
        rec("vector", lambda e: e.drain()) if VEC_DRAINS else None
        vv = sig("vector")
        wait("sync", s_v, vv)
        dma("sync", amax_in.ap(), arow8[:])
        wait("gpsimd", s_ld, s_ld.n)
        vcc = s_cc.bump(1)
        rec("gpsimd", lambda e: e.collective_compute(
            "AllReduce", ALU.max, replica_groups=[list(range(NCORES))],
            ins=[amax_in.ap().opt()], outs=[amax_out.ap().opt()])
            .then_inc(s_cc.h, 1), can_inc=False)
        wait("gpsimd", s_cc, vcc)
        dma("gpsimd", arow8[:], amax_out.ap())
        vg = gfence()
        wait("vector", s_g, vg)
        rec("vector", lambda e: e.tensor_scalar(arow8[:, 0:1], arow8[:, 0:1],
                                                1e-5, None, ALU.max))
        rec("vector", lambda e: e.drain()) if VEC_DRAINS else None
        rec("vector", lambda e: e.reciprocal(arow8[:, 1:2], arow8[:, 0:1]))
        rec("vector", lambda e: e.drain()) if VEC_DRAINS else None
        rec("vector", lambda e: e.tensor_scalar(arow8[:, 1:2], arow8[:, 1:2],
                                                127.0, None, ALU.mult))
        rec("vector", lambda e: e.drain()) if VEC_DRAINS else None
        vv = sig("vector")
        wait("tensor", s_v, vv)
        rec("tensor", lambda e: e.matmul(ps_m[:, 0:2], c_oner[:], arow8[0:1, 0:2],
                                         start=True, stop=True))
        vm = sig("tensor")
        wait("vector", s_mm, vm)
        rec("vector", lambda e: e.tensor_copy(sc2[:], ps_m[:, 0:2]))
        rec("vector", lambda e: e.drain()) if VEC_DRAINS else None
        vv = sig("vector")
        wait("tensor", s_v, vv)

        for ci2, (o2, n2) in enumerate(_cc(TC, 512, HALO)):
            pass
        for i in range(KT):
            for ci, (o_, n_) in enumerate(_cc(TC, 512, HALO)):
                rec("vector", lambda e, i=i, o_=o_, n_=n_, ci=ci: e.tensor_tensor(
                    f_scr[:, o_:o_ + n_], xln[i][:, o_:o_ + n_],
                    pp[4 + ci][:, 0:n_], ALU.mult))
            rec("vector", lambda e: e.drain()) if VEC_DRAINS else None
            rec("vector", lambda e, i=i: e.tensor_scalar(
                f_scr[:, HALO:TF], f_scr[:, HALO:TF], gcol[:, i:i + 1], None,
                ALU.mult))
            rec("vector", lambda e: e.drain()) if VEC_DRAINS else None
            rec("vector", lambda e: e.tensor_scalar(
                f_scr[:, HALO:TF], f_scr[:, HALO:TF], sc2[:, 1:2], None,
                ALU.mult))
            rec("vector", lambda e: e.drain()) if VEC_DRAINS else None
            rec("vector", lambda e: e.tensor_scalar(
                f_scr[:, HALO:TF], f_scr[:, HALO:TF], 127.0, -128.0,
                ALU.min, ALU.max))
            rec("vector", lambda e: e.drain()) if VEC_DRAINS else None
            rec("vector", lambda e: e.tensor_scalar(
                rs1[:, 0:TC], f_scr[:, HALO:TF], 384.5, None, ALU.add))
            rec("vector", lambda e: e.drain()) if VEC_DRAINS else None
            rec("vector", lambda e: e.tensor_scalar(
                rs1[:, 0:TC].bitcast(mybir.dt.int32),
                rs1[:, 0:TC].bitcast(mybir.dt.int32), -32768, None,
                ALU.bitwise_and))
            rec("vector", lambda e: e.drain()) if VEC_DRAINS else None
            rec("vector", lambda e: e.tensor_scalar(
                xq[:, 0:TC], rs1[:, 0:TC], -384.0, None, ALU.add))
            rec("vector", lambda e: e.drain()) if VEC_DRAINS else None
            vv = sig("vector")
            wait("scalar", s_v, vv)
            rec("scalar", lambda e, i=i: e.copy(xln[i][:, HALO:TF], xq[:, 0:TC]))
            vs = sig("scalar")
            wait("vector", s_s, vs)
        barrier()
        dma("gpsimd", arow8[:, 2:3], tern_out.ap()[0:1, 0:1])
        vg = gfence()
        wait("vector", s_g, vg)
        rec("vector", lambda e: e.tensor_scalar(arow8[:, 2:3], arow8[:, 2:3],
                                                1.0 / (2048.0 * 2048.0), None,
                                                ALU.mult))
        rec("vector", lambda e: e.drain()) if VEC_DRAINS else None
        rec("vector", lambda e: e.tensor_scalar(arow8[:, 2:3], arow8[:, 2:3],
                                                1e-5, None, ALU.max))
        rec("vector", lambda e: e.drain()) if VEC_DRAINS else None
        vv = sig("vector")
        wait("tensor", s_v, vv)
        rec("tensor", lambda e: e.matmul(ps_m[:, 4:5], c_oner[:], arow8[0:1, 2:3],
                                         start=True, stop=True))
        vm = sig("tensor")
        wait("vector", s_mm, vm)
        rec("vector", lambda e: e.tensor_scalar(sc2[:, 1:2], sc2[:, 0:1],
                                                1.0 / 127.0, None, ALU.mult))
        rec("vector", lambda e: e.drain()) if VEC_DRAINS else None
        rec("vector", lambda e: e.tensor_tensor(sc2[:, 1:2], sc2[:, 1:2],
                                                ps_m[:, 4:5], ALU.mult))
        rec("vector", lambda e: e.drain()) if VEC_DRAINS else None
        vv = sig("vector")
        wait("tensor", s_v, vv)

        tqf = tq_out["ffn0"].ap()
        mm_end = {}
        ev_sigs = []
        vldqs = {}
        wait("gpsimd", s_cc, tern_cc[0])
        for j in range(KT + 1):
            if j < KT:
                r0 = (128 * j) // NSH0
                c0_ = (128 * j) % NSH0
                if j >= 2:
                    wait("gpsimd", s_mm, mm_end[j - 2])
                vq = None
                dma("gpsimd",
                    wb_t[j % 2][:].rearrange("p (k c) -> p k c", c=256)
                    [:, :, 0:128],
                    tqf[r0].rearrange("(k p) c -> p k c", p=128)
                    [:, :, c0_:c0_ + 128])
                vldqs[j] = qfence("gpsimd")
            if j == 0:
                continue
            jj = j - 1
            for (toff, tn) in TOKS_OWN:
                gi = len(ev_sigs)
                bank = pp[gi % 4]
                if gi >= 4:
                    wait("tensor", s_v, ev_sigs[gi - 4])
                wait("tensor", s_gd, vldqs[jj])
                for k in range(KT):
                    rec("tensor", lambda e, k=k, jj=jj, toff=toff, tn=tn, bank=bank:
                        e.matmul(bank[:, 0:tn], wbuf[jj % 2][k][:, 0:128],
                                 xln[k][:, toff:toff + tn],
                                 start=(k == 0), stop=(k == KT - 1)))
                vm = sig("tensor")
                wait("vector", s_mm, vm)
                rec("vector", lambda e, jj=jj, toff=toff, tn=tn, bank=bank:
                    e.scalar_tensor_tensor(x_sb[jj][:, toff:toff + tn],
                                           bank[:, 0:tn], sc2[:, 1:2],
                                           x_sb[jj][:, toff:toff + tn],
                                           ALU.mult, ALU.add))
                ev_sigs.append(sig("vector"))
            mm_end[jj] = s_mm.n
        barrier()
        fsc.close()

    def emit_ffn1():
        fsc = contextlib.ExitStack()
        hb = [sbh(fsc, f"f1_h{j}", 128, 512, BF16) for j in range(64)]
        sup = sbh(fsc, "f1_su", 128, 4)
        srow = sbh(fsc, "f1_srow", 1, 8)
        rec("vector", lambda e: e.memset(sup[:], 0.0))
        sig("vector")
        barrier(("vector", "gpsimd", "sync"))
        dma("gpsimd", srow[:], tern_out.ap())
        vg = gfence()
        wait("vector", s_g, vg)
        rec("vector", lambda e: e.tensor_scalar(srow[:, 1:4], srow[:, 1:4],
                                                1.0 / (DE * D), 1e-8,
                                                ALU.mult, ALU.add))
        rec("vector", lambda e: e.drain()) if VEC_DRAINS else None
        vv = sig("vector")
        wait("tensor", s_v, vv)
        rec("tensor", lambda e: e.matmul(ps_m[:, 0:3], c_oner[:],
                                         srow[0:1, 1:4], start=True, stop=True))
        vm = sig("tensor")
        wait("vector", s_mm, vm)
        rec("vector", lambda e: e.tensor_copy(sup[:, 0:3], ps_m[:, 0:3]))
        rec("vector", lambda e: e.drain()) if VEC_DRAINS else None
        vv = sig("vector")
        wait("scalar", s_v, vv)
        wait("tensor", s_v, vv)

        emit_ln(HALO, TC)
        emit_xln(HALO, TC)
        barrier()
        wait("gpsimd", s_cc, tern_cc[0])
        tqu = tq_out["up"].ap()
        tqg = tq_out["gate"].ap()
        tqd = tq_out["down"].ap()
        for (toff, tn) in [(HALO, 512), (HALO + 512, 512)]:
            ug_mm_end = {}
            ug_sigs = []
            vlds = {}
            for j in range(65):
                if j < 64:
                    r0 = (128 * j) // NSHE
                    c0_ = (128 * j) % NSHE
                    if j >= 2:
                        wait("gpsimd", s_mm, ug_mm_end[j - 2])
                    dma("gpsimd",
                        wb_t[j % 2][:].rearrange("p (k c) -> p k c", c=256)
                        [:, :, 0:128],
                        tqu[r0].rearrange("(k p) c -> p k c", p=128)
                        [:, :, c0_:c0_ + 128])
                    dma("gpsimd",
                        wb_t[j % 2][:].rearrange("p (k c) -> p k c", c=256)
                        [:, :, 128:256],
                        tqg[r0].rearrange("(k p) c -> p k c", p=128)
                        [:, :, c0_:c0_ + 128])
                    vlds[j] = qfence("gpsimd")
                if j == 0:
                    continue
                jj = j - 1
                gi = len(ug_sigs)
                if gi >= 2:
                    wait("tensor", s_v, ug_sigs[gi - 2][0])
                    wait("tensor", s_s, ug_sigs[gi - 2][1])
                wait("tensor", s_gd, vlds[jj])
                for k in range(KT):
                    rec("tensor", lambda e, k=k, jj=jj, toff=toff, tn=tn:
                        e.matmul(pp[(jj % 2) * 2][:, 0:tn],
                                 wbuf[jj % 2][k][:, 0:128],
                                 xln[k][:, toff:toff + tn],
                                 start=(k == 0), stop=(k == KT - 1)))
                for k in range(KT):
                    rec("tensor", lambda e, k=k, jj=jj, toff=toff, tn=tn:
                        e.matmul(pp[(jj % 2) * 2 + 1][:, 0:tn],
                                 wbuf[jj % 2][k][:, 128:256],
                                 xln[k][:, toff:toff + tn],
                                 start=(k == 0), stop=(k == KT - 1)))
                vm = sig("tensor")
                wait("scalar", s_mm, vm)
                wait("scalar", s_v, s_v.n)
                rec("scalar", lambda e, jj=jj, tn=tn: e.activation(
                    f_scr[:, 0:tn], pp[(jj % 2) * 2][:, 0:tn], ACT.Silu,
                    scale=sup[:, 0:1]))
                vs = sig("scalar")
                wait("vector", s_s, vs)
                rec("vector", lambda e, jj=jj, tn=tn: e.scalar_tensor_tensor(
                    hb[jj][:, 0:tn], pp[(jj % 2) * 2 + 1][:, 0:tn], sup[:, 1:2],
                    f_scr[:, 0:tn], ALU.mult, ALU.mult))
                ug_sigs.append((sig("vector"), sig("scalar")))
                ug_mm_end[jj] = s_mm.n
            barrier(("tensor", "vector", "scalar", "gpsimd"))
            dn_sigs = []
            for j in range(KT):
                r0 = (128 * j) // NSHD
                c0_ = (128 * j) % NSHD
                if j >= 1:
                    wait("gpsimd", s_mm, s_mm.n)
                for kb in range(4):
                    dma("gpsimd",
                        wb_t[kb // 2][:].rearrange("p (k c) -> p k c", c=256)
                        [:, :, 128 * (kb % 2):128 * (kb % 2) + 128],
                        tqd[r0, 2048 * kb:2048 * (kb + 1), :]
                        .rearrange("(k p) c -> p k c", p=128)
                        [:, :, c0_:c0_ + 128])
                vldd = qfence("gpsimd")
                gi = len(dn_sigs)
                bank = pp[4 + (gi % 2)]
                if gi >= 2:
                    wait("tensor", s_v, dn_sigs[gi - 2])
                wait("tensor", s_gd, vldd)
                for k in range(64):
                    rec("tensor", lambda e, k=k, tn=tn, bank=bank:
                        e.matmul(bank[:, 0:tn],
                                 wbuf[k // 32][k % 16][:, 128 * ((k // 16) % 2):
                                                       128 * ((k // 16) % 2) + 128],
                                 hb[k][:, 0:tn],
                                 start=(k == 0), stop=(k == 63)))
                vm = sig("tensor")
                wait("vector", s_mm, vm)
                rec("vector", lambda e, j=j, toff=toff, tn=tn, bank=bank:
                    e.scalar_tensor_tensor(x_sb[j][:, toff:toff + tn],
                                           bank[:, 0:tn], sup[:, 2:3],
                                           x_sb[j][:, toff:toff + tn],
                                           ALU.mult, ALU.add))
                dn_sigs.append(sig("vector"))
            barrier(("tensor", "vector", "scalar", "gpsimd"))
        fsc.close()


    def tap_x(nm):
        tp = tap(nm, [D, TC])
        if tp is not None:
            barrier()
            for i in range(KT):
                dma("sync", tp[i * 128:(i + 1) * 128, :], x_sb[i][:, HALO:TF],
                    sem=s_st)
            wait("sync", s_st, s_st.n)
            barrier()

    emit_ternary()
    gates0 = {"bcum": bcum, "lam": lam, "beta": beta0, "lb": lb0, "bT": bT,
              "betT": betT, "gamT": gamT, "lamCr": lamCr, "hsel": hsel0}
    emit_mixer(0, gates0)
    l0sc.close()
    emit_scan_out(0)
    barrier()
    tap_x("x1")
    emit_ffn0()
    barrier()
    tap_x("x2")

    l1sc = contextlib.ExitStack()
    beta1 = sbh(l1sc, "beta1", 16, TC)
    betT1 = [sbh(l1sc, f"betT1_{i}", 128, 16) for i in range(NCH)]
    emit_ln(HALO, TC)
    emit_xln(HALO, TC)
    barrier()
    for k in range(KT):
        dma("gpsimd", wbuf[0][k][:, 0:16], w_b1[k * 128:(k + 1) * 128, :])
    vldb = qfence("gpsimd")
    for (toff, tn) in TOKS_OWN:
        wait("tensor", s_gd, vldb)
        for k in range(KT):
            rec("tensor", lambda e, k=k, toff=toff, tn=tn:
                e.matmul(pp[0][0:16, 0:tn], wbuf[0][k][:, 0:16],
                         xln[k][:, toff:toff + tn],
                         start=(k == 0), stop=(k == KT - 1)))
        vm = sig("tensor")
        wait("scalar", s_mm, vm)
        rec("scalar", lambda e, toff=toff, tn=tn: e.activation(
            beta1[:, toff - HALO:toff - HALO + tn], pp[0][0:16, 0:tn],
            ACT.Sigmoid))
        vs = sig("scalar")
        wait("tensor", s_s, vs)
    barrier()
    for i in range(NCH):
        rec("tensor", lambda e, i=i: e.transpose(ps_m[:, 0:16],
                                                 beta1[:, i * CH:(i + 1) * CH],
                                                 c_idn32[0:16, 0:16]))
        vm = sig("tensor")
        wait("vector", s_mm, vm)
        rec("vector", lambda e, i=i: e.tensor_copy(betT1[i][:], ps_m[:, 0:16]))
        vv = sig("vector")
        wait("tensor", s_v, vv)
    barrier()
    gates1 = {"beta": beta1, "betT": betT1, "hsel": hsel0}
    emit_mixer(1, gates1)
    l1sc.close()
    emit_scan_out(1)
    barrier()
    msel.close()
    tap_x("x3")
    emit_ffn1()
    barrier()

    for nm, dt_, shp, srcap in (
        ("o0_l0", BF16, [L0_HV, DV, TC], o0_dram[0].ap()),
        ("cc_l0", BF16, [L0_HV, NCH, DK, CH], cc_dram[0].ap()),
        ("agg0_in", F32, [128, L0_HV * 384], agg_in[0].ap()),
        ("o0_l1", BF16, [L1_H, DV, TC], o0_dram[1].ap()),
    ):
        tp = tap(nm, shp, dt_)
        if tp is not None:
            dma("sync", tp, srcap, sem=s_st)
    wait("sync", s_st, s_st.n)

    for i in range(KT):
        dma("sync", y_out[i * 128:(i + 1) * 128, :], x_sb[i][:, HALO:TF], sem=s_st)
    wait("sync", s_st, s_st.n)

    block = E(nc.Block())

    @block.sync
    def _(e):
        for fn in streams["sync"]:
            fn(e)

    @block.scalar
    def _(e):
        for fn in streams["scalar"]:
            fn(e)

    @block.vector
    def _(e):
        for fn in streams["vector"]:
            fn(e)

    @block.tensor
    def _(e):
        for fn in streams["tensor"]:
            fn(e)

    @block.gpsimd
    def _(e):
        for fn in streams["gpsimd"]:
            fn(e)

    stack.close()
    return nc, taps


def _hsel(nh):
    m = np.zeros((nh, nh * 128), np.float32)
    for h in range(nh):
        m[h, h * 128:(h + 1) * 128] = 1.0
    return m


def make_inputs(inputs, core):
    x = np.asarray(inputs["L0_attn_x"])[0]
    xt = np.ascontiguousarray(x.T)
    c0 = core * TC
    if core == 0:
        halo = np.zeros((D, HALO), np.float32)
    else:
        halo = xt[:, c0 - HALO:c0]
    xs = np.concatenate([halo, xt[:, c0:c0 + TC]], axis=1)

    vc = np.zeros((128, 8), np.float32)
    vc[:32, 0] = np.asarray(inputs["L0_attn_A_log"])
    vc[:32, 1] = np.asarray(inputs["L0_attn_dt_bias"])
    vc[:, 2] = np.asarray(inputs["L0_attn_norm_weight"])
    vc[:, 3] = np.asarray(inputs["L1_attn_o_norm_weight"])
    vc[0, 4] = 1.0 / (2048 * 2048)
    vc[1, 4] = 1.0 / (DE * D)
    vc[2, 4] = 1.0 / (DE * D)
    vc[3, 4] = 1.0 / (D * DE)

    pm = np.zeros((128, NCORES), np.float32)
    pm[:, :core] = 1.0

    return {
        "x": np.ascontiguousarray(xs, np.float32),
        "w_qkv": np.asarray(inputs["L0_attn_W_qkv"], np.float32),
        "w_z": np.asarray(inputs["L0_attn_W_z"], np.float32),
        "w_ab": np.ascontiguousarray(np.concatenate(
            [np.asarray(inputs["L0_attn_W_a"]), np.asarray(inputs["L0_attn_W_b"])],
            axis=1), np.float32),
        "w_out0": np.asarray(inputs["L0_attn_W_out"], np.float32),
        "conv_w": np.asarray(inputs["L0_attn_conv_w"], np.float32),
        "vec_col": vc,
        "g_ffn0": np.ascontiguousarray(
            np.asarray(inputs["L0_ffn_norm_weight"], np.float32).reshape(D, 1)),
        "wT_ffn0": np.ascontiguousarray(
            np.asarray(inputs["L0_ffn_weight"], np.float32).T
            [:, core * NSH0:(core + 1) * NSH0]),
        "w_q1": np.asarray(inputs["L1_attn_Wq"], np.float32),
        "w_k1": np.asarray(inputs["L1_attn_Wk"], np.float32),
        "w_v1": np.asarray(inputs["L1_attn_Wv"], np.float32),
        "w_b1": np.asarray(inputs["L1_attn_Wb"], np.float32),
        "w_o1": np.asarray(inputs["L1_attn_Wo"], np.float32),
        "wT_up": np.ascontiguousarray(
            np.asarray(inputs["L1_ffn_W_up"], np.float32).T
            [:, core * NSHE:(core + 1) * NSHE]),
        "wT_gate": np.ascontiguousarray(
            np.asarray(inputs["L1_ffn_W_gate"], np.float32).T
            [:, core * NSHE:(core + 1) * NSHE]),
        "wT_down": np.ascontiguousarray(
            np.asarray(inputs["L1_ffn_W_down"], np.float32).T
            [:, core * NSHD:(core + 1) * NSHD]),
        "pmask": pm,
        "hsel0": _hsel(32),
    }


def kernel(**inputs):
    nc, taps = build_program()
    in_maps = [make_inputs(inputs, c) for c in range(NCORES)]
    res = run_bass_kernel_spmd(nc, in_maps, core_ids=list(range(NCORES)))
    outs = [np.asarray(r["y"]) for r in res.results]
    full = np.concatenate(outs, axis=1)
    return np.ascontiguousarray(full.T)[None]
